# version 03
# baseline (speedup 1.0000x reference)
import contextlib

import numpy as np
import ml_dtypes
import concourse.bass as bass
import concourse.mybir as mybir
from concourse.bass_utils import run_bass_kernel_spmd

F32 = mybir.dt.float32
BF16 = mybir.dt.bfloat16
I32 = mybir.dt.int32
AF = mybir.ActivationFunctionType
ALU = mybir.AluOpType
AX = mybir.AxisListType

NCORES = 8
TC = 4096
D = 1024
NH, DH = 16, 64
CH = 128
NCHUNK = TC // CH
BF = ml_dtypes.bfloat16

DMA_OPS = {"dma_start", "indirect_dma_start", "collective_compute", "dma_start_transpose"}


class Eq:
    def __init__(self, fw, name, self_serialize=True):
        self.fw = fw
        self.name = name
        self.self_serialize = self_serialize
        self.sem_c = fw.st.enter_context(fw.nc.semaphore(f"qc_{name}"))
        self.sem_d = fw.st.enter_context(fw.nc.semaphore(f"qd_{name}"))
        self.nc_ = 0
        self.nd_ = 0
        self.recs = []
        self._pending = {}

    def token(self):
        return self.nc_

    def dtoken(self):
        return self.nd_

    def _add_wait(self, sem, val):
        if val <= 0:
            return
        if self._pending.get(sem, 0) < val:
            self._pending[sem] = val

    def wait(self, other, dma=True):
        self._add_wait(other.sem_c, other.nc_)
        if dma:
            self._add_wait(other.sem_d, other.nd_)

    def wait_tok(self, other, tok):
        self._add_wait(other.sem_c, tok)

    def wait_dtok(self, other, dtok):
        self._add_wait(other.sem_d, dtok)

    def op(self, opname, *args, inc=None, **kwargs):
        is_dma = opname in DMA_OPS
        waits = []
        if self.self_serialize and self.nc_ > 0:
            waits.append((self.sem_c, self.nc_))
        for sem, val in self._pending.items():
            waits.append((sem, val))
        self._pending.clear()
        if inc is None:
            inc = True
        if is_dma:
            inc_sem, inc_val = self.sem_d, 16
            self.nd_ += 16
        elif inc:
            inc_sem, inc_val = self.sem_c, 1
            self.nc_ += 1
        else:
            inc_sem, inc_val = None, 0
        self.recs.append((opname, args, kwargs, waits, inc_sem, inc_val))

    def barrier_all(self):
        self.recs.append(("__wait__", (), {},
                          [(self.sem_d, self.nd_), (self.sem_c, self.nc_)], None, 0))

    def __getattr__(self, item):
        if item.startswith("_"):
            raise AttributeError(item)

        def wrapped(*args, inc=None, **kwargs):
            return self.op(item, *args, inc=inc, **kwargs)

        return wrapped


class Fw:
    def __init__(self, nc):
        self.nc = nc
        self.st = contextlib.ExitStack()
        self.V = Eq(self, "vector")
        self.S = Eq(self, "scalar")
        self.T = Eq(self, "tensor", self_serialize=False)
        self.G = Eq(self, "gpsimd")
        self.SY = Eq(self, "sync", self_serialize=False)
        self._cache = {}

    def sb(self, name, shape, dt=F32):
        return self.st.enter_context(self.nc.sbuf_tensor(name, shape, dt))

    def psbank(self, name):
        return self.st.enter_context(self.nc.psum_tensor(name, [128, 512], F32))

    def init_consts(self):
        c = self._cache
        c["ident"] = self.sb("c_ident", [128, 128], F32)
        c["pcol"] = self.sb("c_pcol", [128, 1], F32)
        c["ones_col"] = self.sb("c_onescol", [128, 1], BF16)
        c["ones_row"] = self.sb("c_onesrow", [1, 128], BF16)
        c["ones1"] = self.sb("c_ones1", [1, 1], BF16)
        G, V, T = self.G, self.V, self.T
        G.iota(c["ident"][:, :], [[1, 128]], channel_multiplier=0,
               allow_small_or_imprecise_dtypes=True)
        G.iota(c["pcol"][:, :], [[0, 1]], channel_multiplier=1,
               allow_small_or_imprecise_dtypes=True)
        V.wait(G)
        V.tensor_scalar(c["ident"][:, :], c["ident"][:, :], c["pcol"][:, :],
                        None, ALU.is_equal)
        V.memset(c["ones_col"][:, :], 1.0)
        V.memset(c["ones_row"][:, :], 1.0)
        V.memset(c["ones1"][:, :], 1.0)
        T.wait(V)

    @property
    def ident(self):
        return self._cache["ident"][:, :]

    @property
    def ones_col(self):
        return self._cache["ones_col"][:, :]

    @property
    def ones_row(self):
        return self._cache["ones_row"][:, :]

    @property
    def ones1(self):
        return self._cache["ones1"][:, :]

    def build(self):
        def mk(q):
            def emit(eng):
                for opname, args, kwargs, waits, inc_sem, inc_val in q.recs:
                    for sem, val in waits:
                        eng.wait_ge(sem, val)
                    if opname == "__wait__":
                        continue
                    inst = getattr(eng, opname)(*args, **kwargs)
                    if inc_sem is not None:
                        inst.then_inc(inc_sem, inc_val)
            return emit

        with self.nc.Block() as block:
            block.sync(mk(self.SY))
            block.gpsimd(mk(self.G))
            block.vector(mk(self.V))
            block.scalar(mk(self.S))
            block.tensor(mk(self.T))
        self.st.close()
        return self.nc


def ln_stats(fw, xt, st, scratch, n=D, eps=1e-5):
    V, S = fw.V, fw.S
    V.reduce_sum(st[:, 4:5], xt, axis=AX.X)
    V.tensor_scalar(st[:, 0:1], st[:, 4:5], 1.0 / n, None, ALU.mult)
    S.wait(V)
    S.activation(scratch, xt, AF.Square, accum_out=st[:, 5:6])
    V.wait(S)
    V.tensor_scalar(st[:, 6:7], st[:, 5:6], 1.0 / n, None, ALU.mult)
    V.tensor_tensor(st[:, 7:8], st[:, 0:1], st[:, 0:1], ALU.mult)
    V.tensor_sub(st[:, 2:3], st[:, 6:7], st[:, 7:8])
    V.tensor_scalar(st[:, 7:8], st[:, 2:3], eps, None, ALU.add)
    V.reciprocal(st[:, 6:7], st[:, 7:8])
    S.wait(V)
    S.activation(st[:, 1:2], st[:, 6:7], AF.Sqrt)
    fw.V.wait(S)


def build_l0_qkv():
    nc = bass.Bass("TRN2", target_bir_lowering=False)
    fw = Fw(nc)
    V, S, T, G, SY = fw.V, fw.S, fw.T, fw.G, fw.SY

    x_d = nc.dram_tensor("x", [TC, D], F32, kind="ExternalInput").ap()
    wq_d = nc.dram_tensor("wqT", [D, D], F32, kind="ExternalInput")
    wk_d = nc.dram_tensor("wkT", [D, D], F32, kind="ExternalInput")
    wv_d = nc.dram_tensor("wvT", [D, D], F32, kind="ExternalInput")
    wb_d = nc.dram_tensor("wbT", [D, 16], F32, kind="ExternalInput")
    rin_d = nc.dram_tensor("rin", [1, D], F32, kind="ExternalInput")
    rk_d = nc.dram_tensor("rk", [1, D], F32, kind="ExternalInput")
    rv_d = nc.dram_tensor("rv", [1, D], F32, kind="ExternalInput")
    bb_d = nc.dram_tensor("beta_b", [1, 16], F32, kind="ExternalInput")
    bbl_d = nc.dram_tensor("bbl", [1, 16], F32, kind="ExternalInput")

    q_o = nc.dram_tensor("q_tok", [TC, D], BF16, kind="ExternalOutput").ap()
    k_o = nc.dram_tensor("kt_tok", [TC, D], BF16, kind="ExternalOutput").ap()
    v_o = nc.dram_tensor("vt_tok", [TC, D], BF16, kind="ExternalOutput").ap()
    b_o = nc.dram_tensor("beta_tok", [TC, 16], F32, kind="ExternalOutput").ap()

    wq = fw.sb("wq", [128, 8 * D], BF16)
    wk = fw.sb("wk", [128, 8 * D], BF16)
    wv = fw.sb("wv", [128, 8 * D], BF16)
    wb = fw.sb("wb", [128, 8 * 16], BF16)
    rin_c = fw.sb("rin_c", [128, 8], F32)
    rk_c = fw.sb("rk_c", [128, 8], F32)
    rv_c = fw.sb("rv_c", [128, 8], F32)
    bb_row = fw.sb("bb_row", [1, 16], BF16)
    bbl_row = fw.sb("bbl_row", [1, 16], BF16)
    bb_t = fw.sb("bb_t", [128, 16], F32)
    bbl_t = fw.sb("bbl_t", [128, 16], F32)
    srow = fw.sb("srow", [1, 128], F32)
    srow_b = fw.sb("srow_b", [1, 128], BF16)

    NT = 2
    xt = [fw.sb(f"xt{i}", [128, D], F32) for i in range(NT)]
    xs = [fw.sb(f"xs{i}", [128, D], F32) for i in range(NT)]
    xnT = [fw.sb(f"xnT{i}", [128, D], BF16) for i in range(NT)]
    st_ = [fw.sb(f"st{i}", [128, 16], F32) for i in range(NT)]
    qout = [fw.sb(f"qout{i}", [128, D], BF16) for i in range(NT)]
    kout = [fw.sb(f"kout{i}", [128, D], BF16) for i in range(NT)]
    vout = [fw.sb(f"vout{i}", [128, D], BF16) for i in range(NT)]
    bout = [fw.sb(f"bout{i}", [128, 16], F32) for i in range(NT)]
    sq = [fw.sb(f"sq{i}", [128, D], BF16) for i in range(NT)]

    ps_tr = [fw.psbank(f"ps_tr{i}") for i in range(2)]
    ps_q = fw.psbank("ps_q")
    ps_k = fw.psbank("ps_k")
    ps_v = fw.psbank("ps_v")
    ps_b = fw.psbank("ps_b")

    fw.init_consts()

    def wchunk_ap(h, m):
        return bass.AP(h, 0, [[m, 128], [128 * m, 8], [1, m]])

    def vec_ap(h):
        return bass.AP(h, 0, [[1, 128], [128, 8]])

    G.dma_start(out=wq[:, :], in_=wchunk_ap(wq_d, D))
    G.dma_start(out=wk[:, :], in_=wchunk_ap(wk_d, D))
    G.dma_start(out=wv[:, :], in_=wchunk_ap(wv_d, D))
    G.dma_start(out=wb[:, :], in_=wchunk_ap(wb_d, 16))
    SY.dma_start(out=rin_c[:, :], in_=vec_ap(rin_d), allow_slow_non_contiguous=True)
    SY.dma_start(out=rk_c[:, :], in_=vec_ap(rk_d), allow_slow_non_contiguous=True)
    SY.dma_start(out=rv_c[:, :], in_=vec_ap(rv_d), allow_slow_non_contiguous=True)
    G.dma_start(out=bb_row[:, :], in_=bb_d[:, :])
    G.dma_start(out=bbl_row[:, :], in_=bbl_d[:, :])
    V.wait(G)
    V.wait(SY)
    for kd in range(8):
        V.tensor_scalar(wk[:, kd * D:(kd + 1) * D], wk[:, kd * D:(kd + 1) * D],
                        rk_c[:, kd:kd + 1], None, ALU.mult)
        V.tensor_scalar(wv[:, kd * D:(kd + 1) * D], wv[:, kd * D:(kd + 1) * D],
                        rv_c[:, kd:kd + 1], None, ALU.mult)
    T.wait(V)
    T.matmul(ps_b[:, 16:32], fw.ones_row, bb_row[:, :], start=True, stop=True, inc=True)
    V.wait(T)
    V.tensor_copy(bb_t[:, :], ps_b[:, 16:32])
    T.wait(V)
    T.matmul(ps_b[:, 16:32], fw.ones_row, bbl_row[:, :], start=True, stop=True, inc=True)
    S.wait(T)
    S.activation(bbl_t[:, :], ps_b[:, 16:32], AF.Sigmoid)
    V.wait(S)

    ld0 = SY.dtoken()
    SY.dma_start(out=xt[0][:, :], in_=x_d[0:128, :])

    tok_xt_free = [0] * NT
    tok_xnT_free = [0] * NT
    dtok_store = [0] * NT
    vtok_evac = {"q": 0, "k": 0, "v": 0, "b": 0, "tr0": 0, "tr1": 0}

    for c in range(NCHUNK):
        i = c % NT
        st = st_[i]
        V.wait_dtok(S, dtok_store[i])
        V.wait_dtok(SY, ld0 + 16 * (c + 1))
        if c + 1 < NCHUNK:
            j = (c + 1) % NT
            SY.wait_tok(V, tok_xt_free[j])
            SY.dma_start(out=xt[j][:, :], in_=x_d[(c + 1) * 128:(c + 2) * 128, :])

        ln_stats(fw, xt[i][:, :], st, xs[i][:, :])
        V.tensor_tensor(st[:, 8:9], st[:, 2:3], st[:, 1:2], ALU.mult)
        V.tensor_tensor(st[:, 8:9], st[:, 8:9], st[:, 1:2], ALU.mult)
        V.tensor_scalar(st[:, 9:10], st[:, 8:9], 1e-6, None, ALU.add)
        V.reciprocal(st[:, 10:11], st[:, 9:10])
        S.wait(V)
        S.activation(st[:, 11:12], st[:, 10:11], AF.Sqrt)
        V.wait(S)
        V.tensor_scalar(xs[i][:, :], xt[i][:, :], st[:, 0:1], st[:, 1:2],
                        ALU.subtract, ALU.mult)
        V.tensor_scalar(xs[i][:, :], xs[i][:, :], st[:, 11:12], None, ALU.mult)
        tok_xt_free[i] = V.token()

        T.wait(V)
        T.wait_tok(V, tok_xnT_free[i] * 0)
        for kd in range(8):
            pb = ps_tr[kd % 2]
            T.wait_tok(V, vtok_evac[f"tr{kd % 2}"])
            T.transpose(pb[:, 0:128], xs[i][:, kd * 128:(kd + 1) * 128],
                        fw.ident, inc=True)
            V.wait(T)
            V.tensor_scalar(xnT[i][:, kd * 128:(kd + 1) * 128], pb[:, 0:128],
                            rin_c[:, kd:kd + 1], None, ALU.mult)
            vtok_evac[f"tr{kd % 2}"] = V.token()

        S.wait(V)
        S.activation(sq[i][:, :], xnT[i][:, :], AF.Square)
        T.wait(S)
        T.wait_tok(V, vtok_evac["b"])
        for kd in range(8):
            T.matmul(ps_b[0:1, 128:256], fw.ones_col,
                     sq[i][:, kd * 128:(kd + 1) * 128],
                     start=(kd == 0), stop=(kd == 7), inc=(kd == 7))
        V.wait(T)
        V.tensor_scalar(srow[:, :], ps_b[0:1, 128:256], 1.0 / D, 1e-6,
                        ALU.mult, ALU.add)
        V.reciprocal(srow[:, :], srow[:, :])
        S.wait(V)
        S.activation(srow[:, :], srow[:, :], AF.Sqrt)
        V.wait(S)
        V.tensor_copy(srow_b[:, :], srow[:, :])
        T.wait(V)
        T.matmul(ps_b[:, 32:33], srow_b[:, :], fw.ones1, start=True, stop=True, inc=True)
        V.wait(T)
        V.tensor_copy(st[:, 12:13], ps_b[:, 32:33])
        vtok_evac["b"] = V.token()

        for half in range(2):
            T.wait_tok(V, vtok_evac["q"])
            for kd in range(8):
                T.matmul(ps_q[:, :], xnT[i][:, kd * 128:(kd + 1) * 128],
                         wq[:, kd * D + half * 512: kd * D + half * 512 + 512],
                         start=(kd == 0), stop=(kd == 7), inc=(kd == 7))
            V.wait(T)
            V.tensor_copy(qout[i][:, half * 512:half * 512 + 512], ps_q[:, :])
            vtok_evac["q"] = V.token()
            T.wait_tok(V, vtok_evac["k"])
            for kd in range(8):
                T.matmul(ps_k[:, :], xnT[i][:, kd * 128:(kd + 1) * 128],
                         wk[:, kd * D + half * 512: kd * D + half * 512 + 512],
                         start=(kd == 0), stop=(kd == 7), inc=(kd == 7))
            V.wait(T)
            V.tensor_scalar(kout[i][:, half * 512:half * 512 + 512], ps_k[:, :],
                            st[:, 12:13], None, ALU.mult)
            vtok_evac["k"] = V.token()
            T.wait_tok(V, vtok_evac["v"])
            for kd in range(8):
                T.matmul(ps_v[:, :], xnT[i][:, kd * 128:(kd + 1) * 128],
                         wv[:, kd * D + half * 512: kd * D + half * 512 + 512],
                         start=(kd == 0), stop=(kd == 7), inc=(kd == 7))
            V.wait(T)
            V.tensor_scalar(vout[i][:, half * 512:half * 512 + 512], ps_v[:, :],
                            st[:, 12:13], None, ALU.mult)
            vtok_evac["v"] = V.token()

        T.wait_tok(V, vtok_evac["b"])
        for kd in range(8):
            T.matmul(ps_b[:, 0:16], xnT[i][:, kd * 128:(kd + 1) * 128],
                     wb[:, kd * 16:(kd + 1) * 16],
                     start=(kd == 0), stop=(kd == 7), inc=(kd == 7))
        tok_xnT_free[i] = T.token()
        V.wait(T)
        V.tensor_tensor(bout[i][:, :], ps_b[:, 0:16], bb_t[:, :], ALU.add)
        S.wait(V)
        S.activation(bout[i][:, :], bout[i][:, :], AF.Sigmoid)
        V.wait(S)
        V.tensor_tensor(bout[i][:, :], bout[i][:, :], bbl_t[:, :], ALU.mult)
        V.tensor_scalar(bout[i][:, :], bout[i][:, :], 0.9, 0.9995, ALU.max, ALU.min)
        vtok_evac["b"] = V.token()

        S.wait(V)
        S.dma_start(out=q_o[c * 128:(c + 1) * 128, :], in_=qout[i][:, :])
        S.dma_start(out=k_o[c * 128:(c + 1) * 128, :], in_=kout[i][:, :])
        S.dma_start(out=v_o[c * 128:(c + 1) * 128, :], in_=vout[i][:, :])
        S.dma_start(out=b_o[c * 128:(c + 1) * 128, :], in_=bout[i][:, :])
        dtok_store[i] = S.dtoken()

    S.barrier_all()
    fw.build()
    return nc




def build_l0_mixer():
    nc = bass.Bass("TRN2", target_bir_lowering=False)
    fw = Fw(nc)
    V, S, T, G, SY = fw.V, fw.S, fw.T, fw.G, fw.SY
    SEQ = 2 * TC
    NCH = SEQ // CH

    qf_d = nc.dram_tensor("qf", [8, 64, SEQ], BF16, kind="ExternalInput")
    kf_d = nc.dram_tensor("kf", [8, 64, SEQ], BF16, kind="ExternalInput")
    kt_d = nc.dram_tensor("kt", [8, SEQ, 64], BF16, kind="ExternalInput")
    vt_d = nc.dram_tensor("vt", [8, SEQ, 64], BF16, kind="ExternalInput")
    be_d = nc.dram_tensor("betaf", [8, SEQ], F32, kind="ExternalInput")
    tm_d = nc.dram_tensor("trimask", [128, 128], BF16, kind="ExternalInput")
    m8_d = nc.dram_tensor("mask8", [8, 1024], BF16, kind="ExternalInput")
    o_d = nc.dram_tensor("o", [8, SEQ, 64], BF16, kind="ExternalOutput")

    bet = fw.sb("bet", [8, SEQ], F32)
    onesb = fw.sb("onesb", [8, SEQ], F32)
    den = fw.sb("den", [8, SEQ], F32)
    rden = fw.sb("rden", [8, SEQ], F32)
    gam = fw.sb("gam", [8, CH], F32)
    gam_bf = fw.sb("gam_bf", [8, CH], BF16)
    gtok = fw.sb("gtok", [128, 24], F32)
    tm = fw.sb("tm", [128, 128], BF16)
    m8 = fw.sb("m8", [8, 1024], BF16)
    ident8 = fw.sb("ident8", [8, 8], F32)
    k2 = fw.sb("k2", [128, 64], BF16)
    dm = fw.sb("dm", [128, 128], BF16)
    stl = fw.sb("stl", [128, 128], BF16)
    msb = fw.sb("msb", [128, 256], F32)
    mbf = fw.sb("mbf", [128, 256], BF16)
    mscr = fw.sb("mscr", [128, 64], F32)
    t1s = fw.sb("t1s", [128, 64], F32)
    NT = 2
    qf_s = [fw.sb(f"qf_s{i}", [128, 512], BF16) for i in range(NT)]
    kf_s = [fw.sb(f"kf_s{i}", [128, 512], BF16) for i in range(NT)]
    kt_s = [fw.sb(f"kt_s{i}", [128, 512], BF16) for i in range(NT)]
    vt_s = [fw.sb(f"vt_s{i}", [128, 512], BF16) for i in range(NT)]
    o_s = [fw.sb(f"o_s{i}", [128, 512], BF16) for i in range(NT)]

    ps_t1 = fw.psbank("ps_t1")
    ps_s = fw.psbank("ps_s")
    ps_o1 = fw.psbank("ps_o1")
    ps_o2 = fw.psbank("ps_o2")
    ps_u = fw.psbank("ps_u")
    ps_tr = fw.psbank("ps_tr")

    fw.init_consts()

    SY.dma_start(out=bet[:, :], in_=be_d[:, :])
    SY.dma_start(out=tm[:, :], in_=tm_d[:, :])
    SY.dma_start(out=m8[:, :], in_=m8_d[:, :])
    ld_pre = SY.dtoken()

    V.wait_dtok(SY, ld_pre)
    T.wait(SY)
    V.memset(onesb[:, :], 1.0)
    V.tensor_copy(ident8[:, :], fw.ident[0:8, 0:8])
    V.tensor_tensor_scan(den[:, :], bet[:, :], onesb[:, :], 0.0, ALU.mult, ALU.add)
    V.tensor_scalar(rden[:, :], den[:, :], 1e-6, None, ALU.add)
    V.reciprocal(rden[:, :], rden[:, :])

    def chunk_ap(d, c):
        return bass.AP(d, c * CH * 64, [[64, 128], [SEQ * 64, 8], [1, 64]])

    def fm_ap(d, c):
        return bass.AP(d, c * CH, [[SEQ, 128], [2 * 64 * SEQ, 4], [1, 128]])

    SY.dma_start(out=qf_s[0][:, :], in_=fm_ap(qf_d, 0))
    SY.dma_start(out=kf_s[0][:, :], in_=fm_ap(kf_d, 0))
    SY.dma_start(out=kt_s[0][:, :], in_=chunk_ap(kt_d, 0))
    SY.dma_start(out=vt_s[0][:, :], in_=chunk_ap(vt_d, 0))
    ld0 = SY.dtoken()

    V.memset(msb[:, :], 0.0)
    V.memset(mbf[:, :], 0.0)

    tok_buf_free = [0] * NT
    ttok_buf_free = [0] * NT
    dtok_ostore = [0] * NT
    vtok_bank = {k: 0 for k in ("t1", "s", "o1", "o2", "u", "tr")}
    ttok_o = 0
    per_chunk_d = 16 * 4

    for c in range(NCH):
        i = c % NT
        V.wait_dtok(SY, ld0 + per_chunk_d * c)
        if c + 1 < NCH:
            j = (c + 1) % NT
            SY.wait_tok(V, tok_buf_free[j])
            SY.wait_tok(T, ttok_buf_free[j])
            SY.dma_start(out=qf_s[j][:, :], in_=fm_ap(qf_d, c + 1))
            SY.dma_start(out=kf_s[j][:, :], in_=fm_ap(kf_d, c + 1))
            SY.dma_start(out=kt_s[j][:, :], in_=chunk_ap(kt_d, c + 1))
            SY.dma_start(out=vt_s[j][:, :], in_=chunk_ap(vt_d, c + 1))

        V.tensor_tensor_scan(gam[:, :], bet[:, c * CH:(c + 1) * CH],
                             onesb[:, 0:CH], 1.0, ALU.mult, ALU.mult)
        V.tensor_copy(gam_bf[:, :], gam[:, :])
        T.wait(V)
        T.wait_tok(V, vtok_bank["tr"])
        T.transpose(ps_tr[:, 0:8], gam[:, :], ident8[:, :], inc=True)
        T.transpose(ps_tr[:, 8:16], rden[:, c * CH:(c + 1) * CH], ident8[:, :], inc=True)
        V.wait(T)
        V.tensor_copy(gtok[:, 0:8], ps_tr[:, 0:8])
        V.tensor_copy(gtok[:, 16:24], ps_tr[:, 8:16])
        V.reciprocal(gtok[:, 8:16], gtok[:, 0:8])
        vtok_bank["tr"] = V.token()

        V.wait_dtok(S, dtok_ostore[i])

        for u in range(8):
            base = 64 * (u % 2)
            j4 = u // 2
            qsl = qf_s[i][base:base + 64, j4 * 128:(j4 + 1) * 128]
            ksl = kf_s[i][base:base + 64, j4 * 128:(j4 + 1) * 128]
            ktsl = kt_s[i][:, u * 64:(u + 1) * 64]
            vtsl = vt_s[i][:, u * 64:(u + 1) * 64]
            gcol = gtok[:, u:u + 1]
            gicol = gtok[:, 8 + u:9 + u]
            rdcol = gtok[:, 16 + u:17 + u]
            tp = (0, 64) if u % 2 else None
            mrow = slice(base, base + 64)

            T.wait(V)
            T.wait_tok(V, vtok_bank["t1"])
            T.matmul(ps_t1[:, 0:128], m8[:, u * 128:(u + 1) * 128],
                     gam_bf[:, :], start=True, stop=True, inc=True)
            V.wait(T)
            V.scalar_tensor_tensor(dm[:, :], ps_t1[:, 0:128], gicol, tm[:, :],
                                   ALU.mult, ALU.mult)
            vtok_bank["t1"] = V.token()

            T.wait_tok(V, vtok_bank["s"])
            T.matmul(ps_s[:, 0:128], ksl, qsl, start=True, stop=True, inc=True)
            V.wait(T)
            V.tensor_tensor(stl[:, :], ps_s[:, 0:128], dm[:, :], ALU.mult)
            vtok_bank["s"] = V.token()

            T.wait(V)
            T.wait_tok(V, vtok_bank["o1"])
            T.matmul(ps_o1[:, 0:64], stl[:, :], vtsl, start=True, stop=True, inc=True)
            T.wait_tok(V, vtok_bank["o2"])
            T.matmul(ps_o2[:, 0:64], qsl, mbf[mrow, j4 * 64:(j4 + 1) * 64],
                     start=True, stop=True, inc=True)
            V.tensor_scalar(k2[:, :], ktsl, gicol, None, ALU.mult)
            T.wait(V)
            T.wait_tok(V, vtok_bank["u"])
            T.matmul(ps_u[0:64, 0:64] if u % 2 == 0 else ps_u[64:128, 0:64],
                     k2[:, :], vtsl, start=True, stop=True,
                     tile_position=(0, base), inc=True)
            T.matmul(ps_u[0:64, 64:65] if u % 2 == 0 else ps_u[64:128, 64:65],
                     m8[:, u * 128:u * 128 + 64], gam_bf[:, CH - 1:CH],
                     start=True, stop=True, tile_position=(0, base), inc=True)
            V.wait(T)
            V.tensor_scalar(t1s[0:128, 0:64], ps_o2[:, 0:64], gcol, rdcol,
                            ALU.mult, ALU.mult)
            V.scalar_tensor_tensor(o_s[i][:, u * 64:(u + 1) * 64], ps_o1[:, 0:64],
                                   rdcol, t1s[0:128, 0:64], ALU.mult, ALU.add)
            vtok_bank["o1"] = V.token()
            vtok_bank["o2"] = V.token()
            usl = ps_u[mrow, 0:64]
            gtsl = ps_u[mrow, 64:65]
            V.tensor_tensor(mscr[mrow, :],
                            msb[mrow, j4 * 64:(j4 + 1) * 64], usl, ALU.add)
            V.tensor_scalar(msb[mrow, j4 * 64:(j4 + 1) * 64],
                            mscr[mrow, :], gtsl, None, ALU.mult)
            vtok_bank["u"] = V.token()
            V.tensor_copy(mbf[mrow, j4 * 64:(j4 + 1) * 64],
                          msb[mrow, j4 * 64:(j4 + 1) * 64])

        tok_buf_free[i] = V.token()
        ttok_buf_free[i] = T.token()
        S.wait(V)
        S.dma_start(out=chunk_ap(o_d, c), in_=o_s[i][:, :])
        dtok_ostore[i] = S.dtoken()

    S.barrier_all()
    fw.build()
    return nc




def host_regroup_heads(res1):
    q_full = np.empty((4, 2 * TC, D), dtype=BF)
    k_full = np.empty((4, 2 * TC, D), dtype=BF)
    v_full = np.empty((4, 2 * TC, D), dtype=BF)
    be_full = np.empty((4, 2 * TC, 16), dtype=np.float32)
    for c in range(NCORES):
        b, h = c // 2, c % 2
        sl = slice(h * TC, (h + 1) * TC)
        r = res1.results[c]
        q_full[b, sl] = r["q_tok"]
        k_full[b, sl] = r["kt_tok"]
        v_full[b, sl] = r["vt_tok"]
        be_full[b, sl] = r["beta_tok"]
    maps = []
    for m in range(NCORES):
        qf = np.empty((8, 64, 2 * TC), dtype=BF)
        kf = np.empty((8, 64, 2 * TC), dtype=BF)
        kt = np.empty((8, 2 * TC, 64), dtype=BF)
        vt = np.empty((8, 2 * TC, 64), dtype=BF)
        bf_ = np.empty((8, 2 * TC), dtype=np.float32)
        for u in range(8):
            g = m * 8 + u
            b, hh = g // 16, g % 16
            qf[u] = q_full[b, :, hh * 64:(hh + 1) * 64].T
            kf[u] = k_full[b, :, hh * 64:(hh + 1) * 64].T
            kt[u] = k_full[b, :, hh * 64:(hh + 1) * 64]
            vt[u] = v_full[b, :, hh * 64:(hh + 1) * 64]
            bf_[u] = be_full[b, :, hh]
        maps.append(dict(qf=qf, kf=kf, kt=kt, vt=vt, betaf=bf_))
    trimask = np.triu(np.ones((128, 128), np.float32)).astype(BF)
    mask8 = np.zeros((8, 1024), np.float32)
    for u in range(8):
        mask8[u, u * 128:(u + 1) * 128] = 1.0
    mask8 = mask8.astype(BF)
    for mmap in maps:
        mmap["trimask"] = trimask
        mmap["mask8"] = mask8
    return maps


def run_l0_mixer(res1):
    nc2 = build_l0_mixer()
    maps = host_regroup_heads(res1)
    return run_bass_kernel_spmd(nc2, maps, core_ids=list(range(NCORES)))


def host_regroup_o(res2):
    o_full = np.empty((4, 2 * TC, D), dtype=BF)
    for m in range(NCORES):
        o = res2.results[m]["o"]
        for u in range(8):
            g = m * 8 + u
            b, hh = g // 16, g % 16
            o_full[b, :, hh * 64:(hh + 1) * 64] = o[u]
    return [np.ascontiguousarray(o_full[c // 2, (c % 2) * TC:(c % 2 + 1) * TC, :])
            for c in range(NCORES)]




def build_l0_post():
    nc = bass.Bass("TRN2", target_bir_lowering=False)
    fw = Fw(nc)
    V, S, T, G, SY = fw.V, fw.S, fw.T, fw.G, fw.SY

    x_d = nc.dram_tensor("x", [TC, D], F32, kind="ExternalInput")
    of_d = nc.dram_tensor("o_fm", [D, TC], BF16, kind="ExternalInput")
    wo_d = nc.dram_tensor("woT", [D, D], F32, kind="ExternalInput")
    bo_d = nc.dram_tensor("bo", [1, D], F32, kind="ExternalInput")
    lnw_d = nc.dram_tensor("lnw", [1, D], F32, kind="ExternalInput")
    lnb_d = nc.dram_tensor("lnb", [1, D], F32, kind="ExternalInput")
    x1_d = nc.dram_tensor("x1", [TC, D], F32, kind="ExternalOutput")

    wo = fw.sb("wo", [128, 8 * D], BF16)
    ofm = fw.sb("ofm", [128, 8 * TC], BF16)
    brow = fw.sb("brow", [1, 3 * D], F32)
    bt = fw.sb("bt", [128, 3 * D], F32)
    ones_row_f = fw.sb("ones_row_f", [1, 128], F32)
    NT = 2
    xt = [fw.sb(f"xt{i}", [128, D], F32) for i in range(NT)]
    yt = [fw.sb(f"yt{i}", [128, D], F32) for i in range(NT)]
    x1t = [fw.sb(f"x1t{i}", [128, D], F32) for i in range(NT)]
    st_ = [fw.sb(f"st{i}", [128, 16], F32) for i in range(NT)]
    ps_o = fw.psbank("ps_o")
    ps_b = fw.psbank("ps_b")

    fw.init_consts()

    G.dma_start(out=wo[:, :], in_=bass.AP(wo_d, 0, [[D, 128], [128 * D, 8], [1, D]]))
    G.dma_start(out=ofm[:, :], in_=bass.AP(of_d, 0, [[TC, 128], [128 * TC, 8], [1, TC]]))
    SY.dma_start(out=brow[0:1, 0:D], in_=bo_d[:, :])
    SY.dma_start(out=brow[0:1, D:2 * D], in_=lnw_d[:, :])
    SY.dma_start(out=brow[0:1, 2 * D:3 * D], in_=lnb_d[:, :])
    V.wait(SY)
    V.wait(G)
    T.wait(G)
    V.memset(ones_row_f[:, :], 1.0)
    T.wait(V)
    vtb = 0
    for seg in range(6):
        T.wait_tok(V, vtb)
        T.matmul(ps_b[:, 0:512], ones_row_f[:, :], brow[0:1, seg * 512:(seg + 1) * 512],
                 start=True, stop=True, inc=True)
        V.wait(T)
        V.tensor_copy(bt[:, seg * 512:(seg + 1) * 512], ps_b[:, 0:512])
        vtb = V.token()

    ld0 = SY.dtoken()
    SY.dma_start(out=xt[0][:, :], in_=x_d[0:128, :])
    tok_xt_free = [0] * NT
    dtok_store = [0] * NT
    vtok_o = 0

    for c in range(NCHUNK):
        i = c % NT
        st = st_[i]
        V.wait_dtok(SY, ld0 + 16 * (c + 1))
        V.wait_dtok(S, dtok_store[i])
        if c + 1 < NCHUNK:
            j = (c + 1) % NT
            SY.wait_tok(V, tok_xt_free[j])
            SY.dma_start(out=xt[j][:, :], in_=x_d[(c + 1) * 128:(c + 2) * 128, :])

        for half in range(2):
            T.wait(V)
            T.wait_tok(V, vtok_o)
            for ke in range(8):
                T.matmul(ps_o[:, :], ofm[:, ke * TC + c * 128: ke * TC + (c + 1) * 128],
                         wo[:, ke * D + half * 512: ke * D + half * 512 + 512],
                         start=(ke == 0), stop=(ke == 7), inc=(ke == 7))
            V.wait(T)
            V.tensor_copy(yt[i][:, half * 512:half * 512 + 512], ps_o[:, :])
            vtok_o = V.token()
        ln_stats(fw, xt[i][:, :], st, x1t[i][:, :])
        V.tensor_scalar(x1t[i][:, :], xt[i][:, :], st[:, 0:1], st[:, 1:2],
                        ALU.subtract, ALU.mult)
        V.tensor_tensor(yt[i][:, :], yt[i][:, :], x1t[i][:, :], ALU.add)
        V.tensor_tensor(yt[i][:, :], yt[i][:, :], bt[:, 0:D], ALU.add)
        ln_stats(fw, yt[i][:, :], st, x1t[i][:, :])
        V.tensor_scalar(yt[i][:, :], yt[i][:, :], st[:, 0:1], st[:, 1:2],
                        ALU.subtract, ALU.mult)
        V.tensor_tensor(yt[i][:, :], yt[i][:, :], bt[:, D:2 * D], ALU.mult)
        V.tensor_tensor(yt[i][:, :], yt[i][:, :], bt[:, 2 * D:3 * D], ALU.add)
        V.tensor_tensor(x1t[i][:, :], xt[i][:, :], yt[i][:, :], ALU.add)
        tok_xt_free[i] = V.token()
        S.wait(V)
        S.dma_start(out=x1_d[c * 128:(c + 1) * 128, :], in_=x1t[i][:, :])
        dtok_store[i] = S.dtoken()

    S.barrier_all()
    fw.build()
    return nc


def run_l0_post(inp, shards, o_tok_list):
    nc = build_l0_post()
    woT = np.ascontiguousarray(np.asarray(inp["L0_attn_Wo"]).T)
    maps = []
    for c in range(NCORES):
        maps.append(dict(
            x=shards[c],
            o_fm=np.ascontiguousarray(o_tok_list[c].T),
            woT=woT,
            bo=np.ascontiguousarray(np.asarray(inp["L0_attn_bo"]).reshape(1, D)),
            lnw=np.ascontiguousarray(np.asarray(inp["L0_attn_ln_w"]).reshape(1, D)),
            lnb=np.ascontiguousarray(np.asarray(inp["L0_attn_ln_b"]).reshape(1, D)),
        ))
    return run_bass_kernel_spmd(nc, maps, core_ids=list(range(NCORES)))




def build_l0_ffn():
    nc = bass.Bass("TRN2", target_bir_lowering=False)
    fw = Fw(nc)
    V, S, T, G, SY = fw.V, fw.S, fw.T, fw.G, fw.SY

    x1_d = nc.dram_tensor("x1", [TC, D], F32, kind="ExternalInput")
    wr_d = nc.dram_tensor("wr", [D, 8], F32, kind="ExternalInput")
    wiT_d = nc.dram_tensor("wiT", [8, D, D], F32, kind="ExternalInput")
    wod_d = nc.dram_tensor("wo", [8, D, D], F32, kind="ExternalInput")
    x2_d = nc.dram_tensor("x2", [TC, D], F32, kind="ExternalOutput")
    we_d = nc.dram_tensor("we_scr", [8, D, D], BF16)

    xnT = fw.sb("xnT", [128, 8 * TC], BF16)
    wia = fw.sb("wia", [128, 8 * D], BF16)
    wob = fw.sb("wob", [128, 8 * D], BF16)
    wes = [fw.sb(f"wes{i}", [128, 8 * D], BF16) for i in range(2)]
    wr = fw.sb("wr_sb", [128, 64], F32)
    gates = fw.sb("gates", [128, 8 * NCHUNK], F32)
    xt = [fw.sb(f"xt{i}", [128, D], F32) for i in range(2)]
    xsc = fw.sb("xsc", [128, D], F32)
    xf32 = fw.sb("xf32", [128, D], F32)
    st0 = fw.sb("st0", [128, 16], F32)
    prow = fw.sb("prow", [128, 24], F32)
    outp = [fw.sb(f"outp{i}", [128, 4 * D], F32) for i in range(2)]

    ps_tr = [fw.psbank(f"ps_tr{i}") for i in range(2)]
    ps_r = fw.psbank("ps_r")
    ps_1 = [fw.psbank(f"ps_1{i}") for i in range(2)]
    ps_2 = [fw.psbank(f"ps_2{i}") for i in range(2)]

    fw.init_consts()

    G.dma_start(out=x2_d[:, :], in_=x1_d[:, :])
    SY.dma_start(out=wr[:, :], in_=bass.AP(wr_d, 0, [[8, 128], [128 * 8, 8], [1, 8]]))
    ld0 = SY.dtoken()
    SY.dma_start(out=xt[0][:, :], in_=x1_d[0:128, :])
    T.wait(SY)
    V.wait(SY)

    wpat = lambda h, e: bass.AP(h, e * D * D, [[D, 128], [128 * D, 8], [1, D]])
    vtok_we = [0, 0]
    gdt_we = 0
    dtw = [0, 0]
    for e in range(8):
        i = e % 2
        G.wait_tok(T, dtw[i])
        G.dma_start(out=wia[:, :] if i == 0 else wes[0][:, :], in_=wpat(wiT_d, e))
        G.dma_start(out=wob[:, :] if i == 0 else wes[1][:, :], in_=wpat(wod_d, e))
        wld = G.dtoken()
        T.wait_dtok(G, wld)
        lw = wia if i == 0 else wes[0]
        rw = wob if i == 0 else wes[1]
        for dc in range(8):
            for half in range(2):
                pb = ps_1[(dc * 2 + half) % 2]
                T.wait_tok(V, vtok_we[(dc * 2 + half) % 2])
                for kf in range(8):
                    T.matmul(pb[:, :], lw[:, kf * D + dc * 128: kf * D + dc * 128 + 128],
                             rw[:, kf * D + half * 512: kf * D + half * 512 + 512],
                             start=(kf == 0), stop=(kf == 7), inc=(kf == 7))
                V.wait(T)
                V.wait_dtok(G, gdt_we)
                V.tensor_copy(outp[0].bitcast(BF16)[:, half * 512:half * 512 + 512], pb[:, :])
                vtok_we[(dc * 2 + half) % 2] = V.token()
            G.wait(V)
            G.dma_start(out=bass.AP(we_d, e * D * D + dc * 128 * D, [[D, 128], [1, D]]),
                        in_=outp[0].bitcast(BF16)[:, 0:D])
            gdt_we = G.dtoken()
        dtw[i] = T.token()

    vtok_tr = [0, 0]
    vtok_r = 0
    ttok_chunk = 0
    tok_xt_free = [0, 0]
    for c in range(NCHUNK):
        i = c % 2
        V.wait_dtok(SY, ld0 + 16 * (c + 1))
        if c + 1 < NCHUNK:
            j = (c + 1) % 2
            SY.wait_tok(V, tok_xt_free[j])
            SY.dma_start(out=xt[j][:, :], in_=x1_d[(c + 1) * 128:(c + 2) * 128, :])
        S.wait_tok(T, ttok_chunk)
        V.wait_tok(T, ttok_chunk)
        ln_stats(fw, xt[i][:, :], st0, xsc[:, :])
        V.tensor_scalar(xsc[:, :], xt[i][:, :], st0[:, 0:1], st0[:, 1:2],
                        ALU.subtract, ALU.mult)
        tok_xt_free[i] = V.token()
        T.wait(V)
        for kd in range(8):
            pb = ps_tr[kd % 2]
            T.wait_tok(V, vtok_tr[kd % 2])
            T.transpose(pb[:, 0:128], xsc[:, kd * 128:(kd + 1) * 128], fw.ident, inc=True)
            V.wait(T)
            V.tensor_copy(xnT[:, kd * TC + c * 128: kd * TC + (c + 1) * 128], pb[:, 0:128])
            V.tensor_copy(xf32[:, kd * 128:(kd + 1) * 128], pb[:, 0:128])
            vtok_tr[kd % 2] = V.token()
        ttok_chunk = T.token()
        T.wait(V)
        T.wait_tok(V, vtok_r)
        for kd in range(8):
            T.matmul(ps_r[:, 0:8], xf32[:, kd * 128:(kd + 1) * 128],
                     wr[:, kd * 8:(kd + 1) * 8],
                     start=(kd == 0), stop=(kd == 7), inc=(kd == 7))
        V.wait(T)
        V.reduce_max(prow[:, 16:17], ps_r[:, 0:8], axis=AX.X)
        V.tensor_scalar(prow[:, 17:18], prow[:, 16:17], -1.0, None, ALU.mult)
        S.wait(V)
        S.activation(prow[:, 0:8], ps_r[:, 0:8], AF.Exp, bias=prow[:, 17:18])
        V.wait(S)
        V.reduce_sum(prow[:, 18:19], prow[:, 0:8], axis=AX.X)
        vtok_r = V.token()
        V.reciprocal(prow[:, 19:20], prow[:, 18:19])
        V.tensor_scalar(prow[:, 0:8], prow[:, 0:8], prow[:, 19:20], None, ALU.mult)
        V.max(out=prow[:, 8:16], in_=prow[:, 0:8])
        V.tensor_scalar(prow[:, 20:21], prow[:, 9:10], 0.0, None, ALU.bypass)
        V.tensor_scalar(gates[:, c * 8:(c + 1) * 8], prow[:, 0:8], prow[:, 20:21],
                        None, ALU.is_ge)
        V.tensor_tensor(gates[:, c * 8:(c + 1) * 8], gates[:, c * 8:(c + 1) * 8],
                        prow[:, 0:8], ALU.mult)

    T.wait(V)
    wepat = lambda e: bass.AP(we_d, e * D * D, [[D, 128], [128 * D, 8], [1, D]])
    gdtok_out = [0, 0]
    vtok_s2 = [0, 0]
    ttok_we = [0, 0]
    wedt = [0, 0]
    G.wait(V)
    G.dma_start(out=wes[0][:, :], in_=wepat(0))
    wedt[0] = G.dtoken()
    for e in range(8):
        i = e % 2
        if e + 1 < 8:
            G.wait_tok(T, ttok_we[(e + 1) % 2])
            G.dma_start(out=wes[(e + 1) % 2][:, :], in_=wepat(e + 1))
            wedt[(e + 1) % 2] = G.dtoken()
        T.wait_dtok(G, wedt[i])
        for tile in range(8):
            ob = outp[tile % 2]
            V.wait_dtok(G, gdtok_out[tile % 2])
            for tsub in range(4):
                for half in range(2):
                    k2i = (tsub * 2 + half) % 2
                    pb = ps_2[k2i]
                    T.wait_tok(V, vtok_s2[k2i])
                    for kd in range(8):
                        T.matmul(pb[:, :],
                                 xnT[:, kd * TC + tile * 512 + tsub * 128: kd * TC + tile * 512 + tsub * 128 + 128],
                                 wes[i][:, kd * D + half * 512: kd * D + half * 512 + 512],
                                 start=(kd == 0), stop=(kd == 7), inc=(kd == 7))
                    V.wait(T)
                    V.tensor_scalar(ob[:, tsub * D + half * 512: tsub * D + half * 512 + 512],
                                    pb[:, :],
                                    gates[:, (tile * 4 + tsub) * 8 + e:(tile * 4 + tsub) * 8 + e + 1],
                                    None, ALU.mult)
                    vtok_s2[k2i] = V.token()
            ttok_we[i] = T.token()
            G.wait(V)
            G.dma_start(
                out=bass.AP(x2_d, tile * 512 * D, [[D, 128], [128 * D, 4], [512, 2], [1, 512]]),
                in_=bass.AP(ob, 0, [[4 * D, 128], [D, 4], [512, 2], [1, 512]]),
                accum_op=ALU.add)
            gdtok_out[tile % 2] = G.dtoken()

    G.barrier_all()
    S.barrier_all()
    fw.build()
    return nc


def run_l0_ffn(inp, res3):
    nc = build_l0_ffn()
    maps = []
    for c in range(NCORES):
        maps.append(dict(
            x1=res3.results[c]["x1"],
            wr=np.ascontiguousarray(np.asarray(inp["L0_ffn_Wr"])),
            wiT=np.ascontiguousarray(np.asarray(inp["L0_ffn_Wi"]).transpose(0, 2, 1)),
            wo=np.ascontiguousarray(np.asarray(inp["L0_ffn_Wo"])),
        ))
    return run_bass_kernel_spmd(nc, maps, core_ids=list(range(NCORES)))




def build_l1_abc():
    nc = bass.Bass("TRN2", target_bir_lowering=False)
    fw = Fw(nc)
    V, S, T, G, SY = fw.V, fw.S, fw.T, fw.G, fw.SY

    x2_d = nc.dram_tensor("x2", [TC, D], F32, kind="ExternalInput")
    wa_d = nc.dram_tensor("wa", [D, D], F32, kind="ExternalInput")
    wb_d = nc.dram_tensor("wb", [D, D], F32, kind="ExternalInput")
    wc_d = nc.dram_tensor("wc", [D, D], F32, kind="ExternalInput")
    g1_d = nc.dram_tensor("g1", [1, D], F32, kind="ExternalInput")
    b1_d = nc.dram_tensor("b1", [1, D], F32, kind="ExternalInput")
    ba_d = nc.dram_tensor("ba", [1, D], F32, kind="ExternalInput")
    bb_d = nc.dram_tensor("bb", [1, D], F32, kind="ExternalInput")
    bc_d = nc.dram_tensor("bc", [1, D], F32, kind="ExternalInput")
    xin_d = nc.dram_tensor("xin", [TC, D], F32, kind="ExternalOutput")
    a_d = nc.dram_tensor("a_fm", [D, TC], F32, kind="ExternalOutput")
    u_d = nc.dram_tensor("u_fm", [D, TC], F32, kind="ExternalOutput")
    c_d = nc.dram_tensor("c_fm", [D, TC], F32, kind="ExternalOutput")

    wa = fw.sb("wa_sb", [128, 8 * D], BF16)
    wb = fw.sb("wb_sb", [128, 8 * D], BF16)
    wc = fw.sb("wc_sb", [128, 8 * D], BF16)
    h1T = fw.sb("h1T", [128, 8 * TC], BF16)
    vcols = fw.sb("vcols", [128, 5 * 8], F32)
    NT = 2
    xt = [fw.sb(f"xt{i}", [128, D], F32) for i in range(NT)]
    xinb = [fw.sb(f"xinb{i}", [128, D], F32) for i in range(NT)]
    zsc = fw.sb("zsc", [128, D], F32)
    st_ = [fw.sb(f"st{i}", [128, 16], F32) for i in range(NT)]
    a_sb = [fw.sb(f"a_sb{i}", [128, 512], F32) for i in range(NT)]
    b_sb = [fw.sb(f"b_sb{i}", [128, 512], F32) for i in range(NT)]
    u_sb = [fw.sb(f"u_sb{i}", [128, 512], F32) for i in range(NT)]
    c_sb = [fw.sb(f"c_sb{i}", [128, 512], F32) for i in range(NT)]
    ps_tr = [fw.psbank(f"ps_tr{i}") for i in range(2)]
    ps_a = fw.psbank("ps_a")
    ps_bb = fw.psbank("ps_bb")
    ps_c = fw.psbank("ps_c")

    fw.init_consts()

    G.dma_start(out=wa[:, :], in_=bass.AP(wa_d, 0, [[D, 128], [128 * D, 8], [1, D]]))
    G.dma_start(out=wb[:, :], in_=bass.AP(wb_d, 0, [[D, 128], [128 * D, 8], [1, D]]))
    G.dma_start(out=wc[:, :], in_=bass.AP(wc_d, 0, [[D, 128], [128 * D, 8], [1, D]]))
    for k, h in enumerate((g1_d, b1_d, ba_d, bb_d, bc_d)):
        SY.dma_start(out=vcols[:, k * 8:(k + 1) * 8],
                     in_=bass.AP(h, 0, [[1, 128], [128, 8]]),
                     allow_slow_non_contiguous=True)
    ld0 = SY.dtoken()
    SY.dma_start(out=xt[0][:, :], in_=x2_d[0:128, :])
    V.wait(SY)
    V.wait(G)
    T.wait(SY)
    T.wait(G)
    S.wait(SY)

    vtok_tr = [0, 0]
    ttok_chunk = 0
    tok_xt_free = [0, 0]
    dtok_xin = [0, 0]
    for c in range(NCHUNK):
        i = c % 2
        st = st_[i]
        V.wait_dtok(SY, ld0 + 16 * (c + 1))
        if c + 1 < NCHUNK:
            j = (c + 1) % 2
            SY.wait_tok(V, tok_xt_free[j])
            SY.dma_start(out=xt[j][:, :], in_=x2_d[(c + 1) * 128:(c + 2) * 128, :])
        S.wait_tok(T, ttok_chunk)
        V.wait_tok(T, ttok_chunk)
        V.wait_dtok(S, dtok_xin[i])
        ln_stats(fw, xt[i][:, :], st, zsc[:, :])
        V.tensor_scalar(xinb[i][:, :], xt[i][:, :], st[:, 0:1], st[:, 1:2],
                        ALU.subtract, ALU.mult)
        tok_xt_free[i] = V.token()
        S.wait(V)
        S.dma_start(out=xin_d[c * 128:(c + 1) * 128, :], in_=xinb[i][:, :])
        dtok_xin[i] = S.dtoken()
        ln_stats(fw, xinb[i][:, :], st, zsc[:, :])
        V.tensor_scalar(zsc[:, :], xinb[i][:, :], st[:, 0:1], st[:, 1:2],
                        ALU.subtract, ALU.mult)
        T.wait(V)
        for kd in range(8):
            pb = ps_tr[kd % 2]
            T.wait_tok(V, vtok_tr[kd % 2])
            T.transpose(pb[:, 0:128], zsc[:, kd * 128:(kd + 1) * 128], fw.ident, inc=True)
            V.wait(T)
            V.tensor_scalar(h1T[:, kd * TC + c * 128: kd * TC + (c + 1) * 128],
                            pb[:, 0:128], vcols[:, kd:kd + 1], vcols[:, 8 + kd:9 + kd],
                            ALU.mult, ALU.add)
            vtok_tr[kd % 2] = V.token()
        ttok_chunk = T.token()

    T.wait(V)
    vtok_e = {"a": 0, "b": 0, "c": 0}
    stok_e = {"a": 0, "b": 0, "c": 0}
    dtok_sb = [0, 0]
    for tile in range(8):
        for ec in range(8):
            i = (tile * 8 + ec) % 2
            V.wait_dtok(G, dtok_sb[i])
            S.wait_dtok(G, dtok_sb[i])
            for nm, wsb, psb in (("a", wa, ps_a), ("b", wb, ps_bb), ("c", wc, ps_c)):
                T.wait_tok(S, stok_e[nm])
                T.wait_tok(V, vtok_e[nm])
                for kd in range(8):
                    T.matmul(psb[:, :], wsb[:, kd * D + ec * 128: kd * D + ec * 128 + 128],
                             h1T[:, kd * TC + tile * 512: kd * TC + (tile + 1) * 512],
                             start=(kd == 0), stop=(kd == 7), inc=(kd == 7))
            S.wait(T)
            S.activation(a_sb[i][:, :], ps_a[:, :], AF.Sigmoid, bias=vcols[:, 16 + ec:17 + ec])
            stok_e["a"] = S.token()
            S.activation(b_sb[i][:, :], ps_bb[:, :], AF.Tanh, bias=vcols[:, 24 + ec:25 + ec])
            stok_e["b"] = S.token()
            S.activation(c_sb[i][:, :], ps_c[:, :], AF.Sigmoid, bias=vcols[:, 32 + ec:33 + ec])
            stok_e["c"] = S.token()
            V.wait(S)
            V.tensor_tensor(u_sb[i][:, :], a_sb[i][:, :], b_sb[i][:, :], ALU.mult)
            V.tensor_sub(u_sb[i][:, :], b_sb[i][:, :], u_sb[i][:, :])
            vtok_e["a"] = V.token()
            vtok_e["b"] = V.token()
            vtok_e["c"] = V.token()
            G.wait(V)
            G.wait(S)
            G.dma_start(out=bass.AP(a_d, ec * 128 * TC + tile * 512, [[TC, 128], [1, 512]]),
                        in_=a_sb[i][:, :])
            G.dma_start(out=bass.AP(u_d, ec * 128 * TC + tile * 512, [[TC, 128], [1, 512]]),
                        in_=u_sb[i][:, :])
            G.dma_start(out=bass.AP(c_d, ec * 128 * TC + tile * 512, [[TC, 128], [1, 512]]),
                        in_=c_sb[i][:, :])
            dtok_sb[i] = G.dtoken()

    G.barrier_all()
    S.barrier_all()
    fw.build()
    return nc


def run_l1_abc(inp, x2_list):
    nc = build_l1_abc()
    maps = []
    for c in range(NCORES):
        maps.append(dict(
            x2=x2_list[c],
            wa=np.ascontiguousarray(np.asarray(inp["L1_attn_Wa"])),
            wb=np.ascontiguousarray(np.asarray(inp["L1_attn_Wb"])),
            wc=np.ascontiguousarray(np.asarray(inp["L1_attn_Wc"])),
            g1=np.ascontiguousarray(np.asarray(inp["L1_attn_g1"]).reshape(1, D)),
            b1=np.ascontiguousarray(np.asarray(inp["L1_attn_b1"]).reshape(1, D)),
            ba=np.ascontiguousarray(np.asarray(inp["L1_attn_ba"]).reshape(1, D)),
            bb=np.ascontiguousarray(np.asarray(inp["L1_attn_bb"]).reshape(1, D)),
            bc=np.ascontiguousarray(np.asarray(inp["L1_attn_bc"]).reshape(1, D)),
        ))
    return run_bass_kernel_spmd(nc, maps, core_ids=list(range(NCORES)))


def build_l1_scan():
    nc = bass.Bass("TRN2", target_bir_lowering=False)
    fw = Fw(nc)
    V, S, SY = fw.V, fw.S, fw.SY
    SEQ = 2 * TC

    a_d = nc.dram_tensor("a_sh", [512, SEQ], F32, kind="ExternalInput")
    u_d = nc.dram_tensor("u_sh", [512, SEQ], F32, kind="ExternalInput")
    c_d = nc.dram_tensor("c_sh", [512, SEQ], F32, kind="ExternalInput")
    y_d = nc.dram_tensor("y_sh", [512, SEQ], F32, kind="ExternalOutput")

    at = fw.sb("at", [128, SEQ], F32)
    ut = fw.sb("ut", [128, SEQ], F32)
    ct = fw.sb("ct", [128, SEQ], F32)
    yt = fw.sb("yt", [128, SEQ], F32)

    dtok_y = 0
    vtok_g = 0
    for g in range(4):
        SY.wait_tok(V, vtok_g)
        SY.dma_start(out=at[:, :], in_=a_d[g * 128:(g + 1) * 128, :])
        SY.dma_start(out=ut[:, :], in_=u_d[g * 128:(g + 1) * 128, :])
        SY.dma_start(out=ct[:, :], in_=c_d[g * 128:(g + 1) * 128, :])
        V.wait_dtok(SY, SY.dtoken())
        V.wait_dtok(S, dtok_y)
        V.tensor_tensor_scan(yt[:, :], at[:, :], ut[:, :], 0.0, ALU.mult, ALU.add)
        V.tensor_tensor(yt[:, :], yt[:, :], ct[:, :], ALU.mult)
        vtok_g = V.token()
        S.wait(V)
        S.dma_start(out=y_d[g * 128:(g + 1) * 128, :], in_=yt[:, :])
        dtok_y = S.dtoken()
    S.barrier_all()
    fw.build()
    return nc


def host_regroup_scan(res5):
    full = {}
    for nm in ("a_fm", "u_fm", "c_fm"):
        arr = np.empty((4, D, 2 * TC), np.float32)
        for c in range(NCORES):
            b, h = c // 2, c % 2
            arr[b, :, h * TC:(h + 1) * TC] = res5.results[c][nm]
        full[nm] = arr
    maps = []
    for m in range(NCORES):
        b, half4 = m // 2, m % 2
        sl = slice(half4 * 512, (half4 + 1) * 512)
        maps.append(dict(a_sh=np.ascontiguousarray(full["a_fm"][b, sl]),
                         u_sh=np.ascontiguousarray(full["u_fm"][b, sl]),
                         c_sh=np.ascontiguousarray(full["c_fm"][b, sl])))
    return maps


def run_l1_scan(res5):
    nc = build_l1_scan()
    return run_bass_kernel_spmd(nc, host_regroup_scan(res5), core_ids=list(range(NCORES)))


def host_regroup_y(res6):
    yfull = np.empty((4, D, 2 * TC), np.float32)
    for m in range(NCORES):
        b, half4 = m // 2, m % 2
        yfull[b, half4 * 512:(half4 + 1) * 512, :] = res6.results[m]["y_sh"]
    return [np.ascontiguousarray(yfull[c // 2, :, (c % 2) * TC:(c % 2 + 1) * TC].T)
            for c in range(NCORES)]




def build_l1_post():
    nc = bass.Bass("TRN2", target_bir_lowering=False)
    fw = Fw(nc)
    V, S, T, G, SY = fw.V, fw.S, fw.T, fw.G, fw.SY
    F = 4 * D

    x2_d = nc.dram_tensor("x2", [TC, D], F32, kind="ExternalInput")
    xin_d = nc.dram_tensor("xin", [TC, D], F32, kind="ExternalInput")
    ys_d = nc.dram_tensor("ys", [TC, D], F32, kind="ExternalInput")
    w1_d = nc.dram_tensor("w1", [D, F], F32, kind="ExternalInput")
    w2_d = nc.dram_tensor("w2", [F, D], F32, kind="ExternalInput")
    g2_d = nc.dram_tensor("g2", [1, D], F32, kind="ExternalInput")
    b2_d = nc.dram_tensor("b2", [1, D], F32, kind="ExternalInput")
    bm1_d = nc.dram_tensor("bm1", [1, F], F32, kind="ExternalInput")
    bm2_d = nc.dram_tensor("bm2", [1, D], F32, kind="ExternalInput")
    wg_d = nc.dram_tensor("wg", [D, 8], F32, kind="ExternalInput")
    x3_d = nc.dram_tensor("x3", [TC, D], F32, kind="ExternalOutput")
    xn3_d = nc.dram_tensor("xn3", [TC, D], BF16, kind="ExternalOutput")
    pr_d = nc.dram_tensor("probs", [TC, 8], F32, kind="ExternalOutput")
    h2T_d = nc.dram_tensor("h2T_scr", [128, 8 * TC], BF16)

    w1h = fw.sb("w1h", [128, 8 * 2048], BF16)
    w2h = fw.sb("w2h", [128, 16 * D], BF16)
    wg = fw.sb("wg_sb", [128, 64], F32)
    h2T_t = fw.sb("h2T_t", [128, 8 * 512], BF16)
    mid = fw.sb("mid", [128, 16 * 512], BF16)
    acc = fw.sb("acc", [128, 4 * D], F32)
    vcols = fw.sb("vcols", [128, 16], F32)
    bm1c = fw.sb("bm1c", [128, 32], F32)
    brow = fw.sb("brow", [1, D], F32)
    bm2t = fw.sb("bm2t", [128, D], F32)
    NT = 2
    xt = [fw.sb(f"xt{i}", [128, D], F32) for i in range(NT)]
    x2t = [fw.sb(f"x2t{i}", [128, D], F32) for i in range(NT)]
    yt = [fw.sb(f"yt{i}", [128, D], F32) for i in range(NT)]
    zsc = fw.sb("zsc", [128, D], F32)
    zb = fw.sb("zb", [128, D], BF16)
    st_ = [fw.sb(f"st{i}", [128, 16], F32) for i in range(NT)]
    prow = fw.sb("prow", [128, 24], F32)
    x3t = [fw.sb(f"x3t{i}", [128, D], F32) for i in range(NT)]
    ps_tr = [fw.psbank(f"ps_tr{i}") for i in range(2)]
    ps_1 = [fw.psbank(f"ps_1{i}") for i in range(2)]
    ps_2 = [fw.psbank(f"ps_2{i}") for i in range(2)]
    ps_r = fw.psbank("ps_r")

    fw.init_consts()

    SY.dma_start(out=wg[:, :], in_=bass.AP(wg_d, 0, [[8, 128], [128 * 8, 8], [1, 8]]))
    SY.dma_start(out=vcols[:, 0:8], in_=bass.AP(g2_d, 0, [[1, 128], [128, 8]]),
                 allow_slow_non_contiguous=True)
    SY.dma_start(out=vcols[:, 8:16], in_=bass.AP(b2_d, 0, [[1, 128], [128, 8]]),
                 allow_slow_non_contiguous=True)
    SY.dma_start(out=bm1c[:, :], in_=bass.AP(bm1_d, 0, [[1, 128], [128, 32]]),
                 allow_slow_non_contiguous=True)
    SY.dma_start(out=brow[:, :], in_=bm2_d[:, :])
    V.wait(SY)
    T.wait(SY)
    S.wait(SY)
    onesf = fw.sb("onesf", [1, 128], F32)
    V.memset(onesf[:, :], 1.0)
    T.wait(V)
    vtb = 0
    for seg in range(2):
        T.wait_tok(V, vtb)
        T.matmul(ps_r[:, 0:512], onesf[:, :], brow[0:1, seg * 512:(seg + 1) * 512],
                 start=True, stop=True, inc=True)
        V.wait(T)
        V.tensor_copy(bm2t[:, seg * 512:(seg + 1) * 512], ps_r[:, 0:512])
        vtb = V.token()

    ld0 = SY.dtoken()
    SY.dma_start(out=xt[0][:, :], in_=xin_d[0:128, :])
    SY.dma_start(out=yt[0][:, :], in_=ys_d[0:128, :])
    SY.dma_start(out=x2t[0][:, :], in_=x2_d[0:128, :])
    vtok_tr = [0, 0]
    ttok_chunk = 0
    tok_free = [0, 0]
    dtok_h2 = [0, 0]
    dtok_zb = [0]
    for c in range(NCHUNK):
        i = c % 2
        st = st_[i]
        V.wait_dtok(SY, ld0 + 48 * (c + 1))
        if c + 1 < NCHUNK:
            j = (c + 1) % 2
            SY.wait_tok(V, tok_free[j])
            SY.wait_dtok(G, dtok_h2[j])
            SY.dma_start(out=xt[j][:, :], in_=xin_d[(c + 1) * 128:(c + 2) * 128, :])
            SY.dma_start(out=yt[j][:, :], in_=ys_d[(c + 1) * 128:(c + 2) * 128, :])
            SY.dma_start(out=x2t[j][:, :], in_=x2_d[(c + 1) * 128:(c + 2) * 128, :])
        S.wait_tok(T, ttok_chunk)
        V.wait_tok(T, ttok_chunk)
        V.wait_dtok(G, dtok_h2[i])
        V.wait_dtok(G, dtok_zb[0])
        V.tensor_tensor(xt[i][:, :], xt[i][:, :], yt[i][:, :], ALU.add)
        ln_stats(fw, xt[i][:, :], st, zsc[:, :])
        V.tensor_scalar(zsc[:, :], xt[i][:, :], st[:, 0:1], st[:, 1:2],
                        ALU.subtract, ALU.mult)
        V.tensor_tensor(x2t[i][:, :], x2t[i][:, :], xt[i][:, :], ALU.add)
        V.tensor_tensor(x2t[i][:, :], x2t[i][:, :], bm2t[:, :], ALU.add)
        tok_free[i] = V.token()
        T.wait(V)
        for kd in range(8):
            pb = ps_tr[kd % 2]
            T.wait_tok(V, vtok_tr[kd % 2])
            T.transpose(pb[:, 0:128], zsc[:, kd * 128:(kd + 1) * 128], fw.ident, inc=True)
            V.wait(T)
            V.tensor_scalar(zb[:, kd * 128:(kd + 1) * 128],
                            pb[:, 0:128], vcols[:, kd:kd + 1], vcols[:, 8 + kd:9 + kd],
                            ALU.mult, ALU.add)
            vtok_tr[kd % 2] = V.token()
        ttok_chunk = T.token()
        G.wait(V)
        G.dma_start(out=bass.AP(h2T_d, c * 128, [[8 * TC, 128], [TC, 8], [1, 128]]),
                    in_=bass.AP(zb, 0, [[8 * 128, 128], [128, 8], [1, 128]]))
        dtok_zb[0] = G.dtoken()
        G.dma_start(out=x3_d[c * 128:(c + 1) * 128, :], in_=x2t[i][:, :])
        dtok_h2[i] = G.dtoken()

    T.wait(V)
    gdt_phase1 = G.nd_
    vtok_s1 = [0, 0]
    vtok_s2 = [0, 0]
    ttok_w = 0
    gdtok_acc = [0, 0]
    dtok_h2t = [0, 0]
    ttok_h2t = [0, 0]
    h2T_t2 = [h2T_t, fw.sb("h2T_t2", [128, 8 * 512], BF16)]
    acc2 = [acc, fw.sb("acc2", [128, 4 * D], F32)]
    for fh in range(2):
        G.wait_tok(T, ttok_w)
        G.dma_start(out=w1h[:, :],
                    in_=bass.AP(w1_d, fh * 2048, [[F, 128], [128 * F, 8], [1, 2048]]))
        G.dma_start(out=w2h[:, :],
                    in_=bass.AP(w2_d, fh * 2048 * D, [[D, 128], [128 * D, 16], [1, D]]))
        wdt = G.dtoken()
        T.wait_dtok(G, wdt)
        for tile in range(8):
            ib = tile % 2
            ht = h2T_t2[ib]
            ob = acc2[ib]
            SY.wait_tok(T, ttok_h2t[ib])
            SY.wait_dtok(G, gdt_phase1)
            SY.dma_start(out=ht[:, :],
                         in_=bass.AP(h2T_d, tile * 512, [[8 * TC, 128], [TC, 8], [1, 512]]))
            hdt = SY.dtoken()
            T.wait_dtok(SY, hdt)
            for fc in range(16):
                pb = ps_1[fc % 2]
                T.wait_tok(S, vtok_s1[fc % 2])
                for kd in range(8):
                    T.matmul(pb[:, :], w1h[:, kd * 2048 + fc * 128: kd * 2048 + fc * 128 + 128],
                             ht[:, kd * 512:(kd + 1) * 512],
                             start=(kd == 0), stop=(kd == 7), inc=(kd == 7))
                S.wait(T)
                S.activation(mid[:, fc * 512:(fc + 1) * 512], pb[:, :], AF.Gelu,
                             bias=bm1c[:, fh * 16 + fc:fh * 16 + fc + 1])
                vtok_s1[fc % 2] = S.token()
            ttok_h2t[ib] = T.token()
            T.wait(S)
            V.wait_dtok(G, gdtok_acc[ib])
            for tsub in range(4):
                for half in range(2):
                    k2i = (tsub * 2 + half) % 2
                    pb = ps_2[k2i]
                    T.wait_tok(V, vtok_s2[k2i])
                    for fc in range(16):
                        T.matmul(pb[:, :], mid[:, fc * 512 + tsub * 128: fc * 512 + tsub * 128 + 128],
                                 w2h[:, fc * D + half * 512: fc * D + half * 512 + 512],
                                 start=(fc == 0), stop=(fc == 15), inc=(fc == 15))
                    V.wait(T)
                    V.tensor_copy(ob[:, tsub * D + half * 512: tsub * D + half * 512 + 512],
                                  pb[:, :])
                    vtok_s2[k2i] = V.token()
            ttok_w = T.token()
            G.wait(V)
            G.dma_start(
                out=bass.AP(x3_d, tile * 512 * D, [[D, 128], [128 * D, 4], [512, 2], [1, 512]]),
                in_=bass.AP(ob, 0, [[4 * D, 128], [D, 4], [512, 2], [1, 512]]),
                accum_op=ALU.add)
            gdtok_acc[ib] = G.dtoken()

    G.barrier_all()
    ld3 = SY.dtoken()
    SY.wait_dtok(G, G.nd_)
    SY.dma_start(out=xt[0][:, :], in_=x3_d[0:128, :])
    vtok_r = vtb
    vtok_tr = [0, 0]
    ttok_chunk2 = 0
    tok_free3 = [0, 0]
    dtok_st = [0, 0]
    for c in range(NCHUNK):
        i = c % 2
        st = st_[i]
        V.wait_dtok(SY, ld3 + 16 * (c + 1))
        if c + 1 < NCHUNK:
            j = (c + 1) % 2
            SY.wait_tok(V, tok_free3[j])
            SY.dma_start(out=xt[j][:, :], in_=x3_d[(c + 1) * 128:(c + 2) * 128, :])
        S.wait_tok(T, ttok_chunk2)
        V.wait_tok(T, ttok_chunk2)
        V.wait_dtok(S, dtok_st[i])
        ln_stats(fw, xt[i][:, :], st, zsc[:, :])
        V.tensor_scalar(zsc[:, :], xt[i][:, :], st[:, 0:1], st[:, 1:2],
                        ALU.subtract, ALU.mult)
        V.tensor_copy(zb[:, 0:D], zsc[:, :])
        tok_free3[i] = V.token()
        T.wait(V)
        for kd in range(8):
            pb = ps_tr[kd % 2]
            T.wait_tok(V, vtok_tr[kd % 2])
            T.transpose(pb[:, 0:128], zsc[:, kd * 128:(kd + 1) * 128], fw.ident, inc=True)
            V.wait(T)
            V.tensor_copy(x3t[i][:, kd * 128:(kd + 1) * 128], pb[:, 0:128])
            vtok_tr[kd % 2] = V.token()
        T.wait(V)
        T.wait_tok(V, vtok_r)
        for kd in range(8):
            T.matmul(ps_r[:, 0:8], x3t[i][:, kd * 128:(kd + 1) * 128],
                     wg[:, kd * 8:(kd + 1) * 8],
                     start=(kd == 0), stop=(kd == 7), inc=(kd == 7))
        ttok_chunk2 = T.token()
        V.wait(T)
        V.reduce_max(prow[:, 16:17], ps_r[:, 0:8], axis=AX.X)
        V.tensor_scalar(prow[:, 17:18], prow[:, 16:17], -1.0, None, ALU.mult)
        S.wait(V)
        S.activation(prow[:, 0:8], ps_r[:, 0:8], AF.Exp, bias=prow[:, 17:18])
        V.wait(S)
        V.reduce_sum(prow[:, 18:19], prow[:, 0:8], axis=AX.X)
        vtok_r = V.token()
        V.reciprocal(prow[:, 19:20], prow[:, 18:19])
        V.tensor_scalar(prow[:, 0:8], prow[:, 0:8], prow[:, 19:20], None, ALU.mult)
        S.wait(V)
        S.dma_start(out=xn3_d[c * 128:(c + 1) * 128, :], in_=zb[:, 0:D])
        S.dma_start(out=pr_d[c * 128:(c + 1) * 128, :], in_=prow[:, 0:8])
        dtok_st[i] = S.dtoken()

    S.barrier_all()
    fw.build()
    return nc


def run_l1_post(inp, x2_list, res5, ys_list):
    nc = build_l1_post()
    maps = []
    for c in range(NCORES):
        maps.append(dict(
            x2=x2_list[c],
            xin=res5.results[c]["xin"],
            ys=ys_list[c],
            w1=np.ascontiguousarray(np.asarray(inp["L1_attn_W1"])),
            w2=np.ascontiguousarray(np.asarray(inp["L1_attn_W2"])),
            g2=np.ascontiguousarray(np.asarray(inp["L1_attn_g2"]).reshape(1, D)),
            b2=np.ascontiguousarray(np.asarray(inp["L1_attn_b2"]).reshape(1, D)),
            bm1=np.ascontiguousarray(np.asarray(inp["L1_attn_bm1"]).reshape(1, 4 * D)),
            bm2=np.ascontiguousarray(np.asarray(inp["L1_attn_bm2"]).reshape(1, D)),
            wg=np.ascontiguousarray(np.asarray(inp["L1_ffn_Wg"])),
        ))
    return run_bass_kernel_spmd(nc, maps, core_ids=list(range(NCORES)))




def build_l1_route():
    nc = bass.Bass("TRN2", target_bir_lowering=False)
    fw = Fw(nc)
    V, S, T, G, SY = fw.V, fw.S, fw.T, fw.G, fw.SY
    CAP = 4096.0

    p_d = nc.dram_tensor("pT", [128, 256], F32, kind="ExternalInput")
    o_d = nc.dram_tensor("route", [1, 4], F32, kind="ExternalOutput")

    pt = fw.sb("pt", [128, 256], F32)
    gidx = fw.sb("gidx", [128, 256], F32)
    sc = fw.sb("sc", [128, 256], F32)
    sc2 = fw.sb("sc2", [128, 256], F32)
    rows = fw.sb("rows", [128, 4], F32)
    onesc = fw.sb("onesc", [128, 1], F32)
    onesr = fw.sb("onesr", [1, 128], F32)
    sm = fw.sb("sm", [1, 16], F32)
    ps_a = fw.psbank("ps_a")
    ps_b = fw.psbank("ps_b")

    fw.init_consts()
    G.iota(gidx[:, :], [[1, 256]], channel_multiplier=256,
           allow_small_or_imprecise_dtypes=True)
    SY.dma_start(out=pt[:, :], in_=p_d[:, :])
    V.wait(SY)
    V.wait(G)
    V.memset(onesc[:, :], 1.0)
    V.memset(onesr[:, :], 1.0)
    V.memset(sm[0:1, 0:1], 0.0)
    V.memset(sm[0:1, 1:2], 1.000001)
    T.wait(V)

    vtok_ps = [0, 0]

    def bcast_col(src11):
        nonlocal_ = 0
        T.wait(V)
        T.wait_tok(V, vtok_ps[0])
        T.matmul(ps_a[:, 0:1], onesr[:, :], src11, start=True, stop=True, inc=True)
        V.wait(T)

    def total_count(src_tile, out11):
        V.reduce_sum(rows[:, 0:1], src_tile, axis=AX.X)
        T.wait(V)
        T.wait_tok(V, vtok_ps[1])
        T.matmul(ps_b[0:1, 0:1], rows[:, 0:1], onesc[:, :], start=True, stop=True, inc=True)
        V.wait(T)
        V.tensor_copy(out11, ps_b[0:1, 0:1])
        vtok_ps[1] = V.token()

    def upd(lo_sl, hi_sl, th_sl, m_sl):
        V.tensor_tensor(sm[0:1, 5:6], th_sl, lo_sl, ALU.subtract)
        V.tensor_tensor(sm[0:1, 5:6], sm[0:1, 5:6], m_sl, ALU.mult)
        V.tensor_tensor(lo_sl, lo_sl, sm[0:1, 5:6], ALU.add)
        V.tensor_tensor(sm[0:1, 5:6], th_sl, hi_sl, ALU.subtract)
        V.tensor_tensor(sm[0:1, 11:12], sm[0:1, 5:6], m_sl, ALU.mult)
        V.tensor_tensor(sm[0:1, 5:6], sm[0:1, 5:6], sm[0:1, 11:12], ALU.subtract)
        V.tensor_tensor(hi_sl, hi_sl, sm[0:1, 5:6], ALU.add)

    for it in range(40):
        V.tensor_tensor(sm[0:1, 2:3], sm[0:1, 0:1], sm[0:1, 1:2], ALU.add)
        V.tensor_scalar(sm[0:1, 2:3], sm[0:1, 2:3], 0.5, None, ALU.mult)
        bcast_col(sm[0:1, 2:3])
        V.tensor_scalar(sc[:, :], pt[:, :], ps_a[:, 0:1], None, ALU.is_ge)
        vtok_ps[0] = V.token()
        total_count(sc[:, :], sm[0:1, 3:4])
        V.tensor_scalar(sm[0:1, 4:5], sm[0:1, 3:4], CAP, None, ALU.is_ge)
        upd(sm[0:1, 0:1], sm[0:1, 1:2], sm[0:1, 2:3], sm[0:1, 4:5])

    bcast_col(sm[0:1, 0:1])
    V.tensor_scalar(sc[:, :], pt[:, :], ps_a[:, 0:1], None, ALU.is_ge)
    vtok_ps[0] = V.token()
    V.tensor_tensor(sc2[:, :], pt[:, :], sc[:, :], ALU.mult)
    V.tensor_scalar(sc[:, :], sc[:, :], -1.0, 1.0, ALU.mult, ALU.add)
    V.tensor_scalar(sc[:, :], sc[:, :], 2.0, None, ALU.mult)
    V.tensor_tensor(sc2[:, :], sc2[:, :], sc[:, :], ALU.add)
    V.tensor_reduce(rows[:, 1:2], sc2[:, :], axis=AX.X, op=ALU.min)
    T.wait(V)
    T.transpose(ps_b[0:1, 0:128], rows[:, 1:2], fw.ident, inc=True)
    V.wait(T)
    V.tensor_reduce(sm[0:1, 6:7], ps_b[0:1, 0:128], axis=AX.X, op=ALU.min)
    vtok_ps[1] = V.token()
    bcast_col(sm[0:1, 6:7])
    V.tensor_scalar(sc[:, :], pt[:, :], ps_a[:, 0:1], None, ALU.is_gt)
    vtok_ps[0] = V.token()
    total_count(sc[:, :], sm[0:1, 10:11])
    V.tensor_scalar(sm[0:1, 7:8], sm[0:1, 10:11], -1.0, CAP, ALU.mult, ALU.add)
    bcast_col(sm[0:1, 6:7])
    V.tensor_scalar(sc[:, :], pt[:, :], ps_a[:, 0:1], None, ALU.is_equal)
    vtok_ps[0] = V.token()
    V.memset(sm[0:1, 8:9], -1.0)
    V.memset(sm[0:1, 9:10], 32768.0)
    for it in range(18):
        V.tensor_tensor(sm[0:1, 2:3], sm[0:1, 8:9], sm[0:1, 9:10], ALU.add)
        V.tensor_scalar(sm[0:1, 2:3], sm[0:1, 2:3], 0.5, None, ALU.mult)
        bcast_col(sm[0:1, 2:3])
        V.tensor_scalar(sc2[:, :], gidx[:, :], ps_a[:, 0:1], None, ALU.is_le)
        vtok_ps[0] = V.token()
        V.tensor_tensor(sc2[:, :], sc2[:, :], sc[:, :], ALU.mult)
        total_count(sc2[:, :], sm[0:1, 3:4])
        V.tensor_scalar(sm[0:1, 4:5], sm[0:1, 3:4], sm[0:1, 7:8], None, ALU.is_ge)
        V.tensor_tensor(sm[0:1, 5:6], sm[0:1, 2:3], sm[0:1, 9:10], ALU.subtract)
        V.tensor_tensor(sm[0:1, 5:6], sm[0:1, 5:6], sm[0:1, 4:5], ALU.mult)
        V.tensor_tensor(sm[0:1, 9:10], sm[0:1, 9:10], sm[0:1, 5:6], ALU.add)
        V.tensor_tensor(sm[0:1, 5:6], sm[0:1, 2:3], sm[0:1, 8:9], ALU.subtract)
        V.tensor_tensor(sm[0:1, 11:12], sm[0:1, 5:6], sm[0:1, 4:5], ALU.mult)
        V.tensor_tensor(sm[0:1, 5:6], sm[0:1, 5:6], sm[0:1, 11:12], ALU.subtract)
        V.tensor_tensor(sm[0:1, 8:9], sm[0:1, 8:9], sm[0:1, 5:6], ALU.add)

    V.tensor_copy(rows[0:1, 0:1], sm[0:1, 6:7])
    V.tensor_copy(rows[0:1, 1:2], sm[0:1, 9:10])
    V.tensor_copy(rows[0:1, 2:3], sm[0:1, 7:8])
    V.tensor_copy(rows[0:1, 3:4], sm[0:1, 10:11])
    S.wait(V)
    S.dma_start(out=o_d[:, :], in_=rows[0:1, 0:4])
    S.barrier_all()
    fw.build()
    return nc


def run_l1_route(res7):
    probs_full = np.concatenate([res7.results[c]["probs"] for c in range(NCORES)], 0)
    nc = build_l1_route()
    maps = [dict(pT=np.ascontiguousarray(probs_full[:, m].reshape(128, 256)))
            for m in range(NCORES)]
    return run_bass_kernel_spmd(nc, maps, core_ids=list(range(NCORES)))




def build_l1_ffn():
    nc = bass.Bass("TRN2", target_bir_lowering=False)
    fw = Fw(nc)
    V, S, T, G, SY = fw.V, fw.S, fw.T, fw.G, fw.SY
    F = 2048
    CAP = 1024
    NLIST = 8 * CAP + 256

    x3_d = nc.dram_tensor("x3", [TC, D], F32, kind="ExternalInput")
    xn3_d = nc.dram_tensor("xn3", [TC, D], BF16, kind="ExternalInput")
    pr_d = nc.dram_tensor("probs", [TC, 8], F32, kind="ExternalInput")
    vi_d = nc.dram_tensor("vitab", [2, 8], F32, kind="ExternalInput")
    base_d = nc.dram_tensor("idxbase", [1, 1], F32, kind="ExternalInput")
    w1_d = nc.dram_tensor("w1", [8, D, F], F32, kind="ExternalInput")
    w2_d = nc.dram_tensor("w2", [8, F, D], F32, kind="ExternalInput")
    x4_d = nc.dram_tensor("x4", [TC, D], F32, kind="ExternalOutput")
    il_d = nc.dram_tensor("idxlist", [NLIST, 1], I32)
    gl_d = nc.dram_tensor("gatelist", [NLIST, 1], F32)
    xnp_d = nc.dram_tensor("xn3pad", [TC + 128, D], BF16)
    x4a_d = nc.dram_tensor("x4acc", [TC + 128, D], F32)

    w1 = fw.sb("w1_sb", [128, 8 * F], BF16)
    w2 = fw.sb("w2_sb", [128, 16 * D], BF16)
    mid = fw.sb("mid", [128, 16 * 512], BF16)
    xg = [fw.sb(f"xg{i}", [128, 4 * D], BF16) for i in range(2)]
    xgT = [fw.sb(f"xgT{i}", [128, 8 * 512], BF16) for i in range(2)]
    idxc = [fw.sb(f"idxc{i}", [128, 4], I32) for i in range(2)]
    gc = [fw.sb(f"gc{i}", [128, 4], F32) for i in range(2)]
    outg = [fw.sb(f"outg{i}", [128, 4 * D], F32) for i in range(2)]
    pb_ = [fw.sb(f"pb{i}", [128, 8], F32) for i in range(2)]
    prow = fw.sb("prow", [128, 48], F32)
    vit = fw.sb("vit", [128, 16], F32)
    wpow = fw.sb("wpow", [128, 8], F32)
    ecap = fw.sb("ecap", [128, 8], F32)
    pcol = fw.sb("pcol", [128, 1], F32)
    bcol = fw.sb("bcol", [128, 1], F32)
    virow = fw.sb("virow", [1, 16], F32)
    brow = fw.sb("brow", [1, 1], F32)
    onesf = fw.sb("onesf", [1, 128], F32)
    onescf = fw.sb("onescf", [128, 1], BF16)
    ltri = fw.sb("ltri", [128, 128], BF16)
    jt = fw.sb("jt", [128, 128], F32)
    totrow = fw.sb("totrow", [1, 8], F32)
    pickb = fw.sb("pickb", [128, 8], BF16)
    zloc = fw.sb("zloc", [128, 8], F32)
    znov = fw.sb("znov", [128, 8], F32)
    slotc = fw.sb("slotc", [128, 1], F32)
    sloti = fw.sb("sloti", [128, 1], I32)
    tokid = fw.sb("tokid", [128, 1], I32)
    gsel = fw.sb("gsel", [128, 1], F32)
    fillc = fw.sb("fillc", [128, 66], I32)
    identb = fw.sb("identb", [128, 128], BF16)
    ps_trb = [fw.st.enter_context(nc.psum_tensor(f"ps_trb{i}", [128, 128], BF16))
              for i in range(2)]
    ps_1 = [fw.psbank(f"ps_1{i}") for i in range(2)]
    ps_2 = [fw.psbank(f"ps_2{i}") for i in range(2)]
    ps_m = fw.psbank("ps_m")
    ps_c = fw.psbank("ps_c")

    fw.init_consts()
    STAGE = 9

    SY.dma_start(out=virow[:, :], in_=bass.AP(vi_d, 0, [[16, 1], [1, 16]]))
    SY.dma_start(out=brow[:, :], in_=base_d[:, :])
    G.iota(pcol[:, :], [[0, 1]], channel_multiplier=1,
           allow_small_or_imprecise_dtypes=True)
    G.iota(wpow[:, :], [[1, 8]], channel_multiplier=0,
           allow_small_or_imprecise_dtypes=True)
    G.iota(jt[:, :], [[1, 128]], channel_multiplier=0,
           allow_small_or_imprecise_dtypes=True)
    G.iota(fillc[:, :], [[0, 66]], base=TC, channel_multiplier=1)
    V.wait(SY)
    V.wait(G)
    V.memset(onesf[:, :], 1.0)
    V.memset(onescf[:, :], 1.0)
    V.memset(totrow[:, :], 0.0)

    V.tensor_copy(identb[:, :], fw.ident[:, :])
    V.tensor_scalar(jt[:, :], jt[:, :], pcol[:, :], None, ALU.is_ge)
    V.tensor_copy(ltri[:, :], jt[:, :])
    V.tensor_scalar(ecap[:, :], wpow[:, :], float(CAP), None, ALU.mult)
    V.tensor_scalar(wpow[:, :], wpow[:, :], -0.6931471805599453, None, ALU.mult)
    S.wait(V)
    S.activation(wpow[:, :], wpow[:, :], AF.Exp)
    T.wait(V)
    T.wait(S)
    T.matmul(ps_m[:, 0:8], onesf[:, :], virow[0:1, 0:8], start=True, stop=True, inc=True)
    T.matmul(ps_m[:, 8:16], onesf[:, :], virow[0:1, 8:16], start=True, stop=True, inc=True)
    T.matmul(ps_m[:, 16:17], onesf[:, :], brow[0:1, :], start=True, stop=True, inc=True)
    V.wait(T)
    V.tensor_copy(vit[:, 0:8], ps_m[:, 0:8])
    V.tensor_copy(vit[:, 8:16], ps_m[:, 8:16])
    V.tensor_copy(bcol[:, :], ps_m[:, 16:17])

    zpad = fw.sb("zpad", [128, D], BF16)
    V.memset(zpad[:, :], 0.0)
    G.wait(V)
    G.dma_start(out=x4a_d[0:TC, :], in_=x3_d[:, :])
    G.dma_start(out=bass.AP(il_d, 0, [[1, NLIST]]),
                in_=bass.AP(fillc, 0, [[66, 128], [1, 66]]))
    G.dma_start(out=xnp_d[0:TC, :], in_=xn3_d[:, :])
    G.dma_start(out=xnp_d[TC:TC + 128, :], in_=zpad[:, :])
    gdt_init = G.dtoken()

    ld0 = SY.dtoken()
    SY.dma_start(out=pb_[0][:, :], in_=pr_d[0:128, :])
    tok_free = [0, 0]
    gdt_sc = 0
    for c in range(NCHUNK):
        i = c % 2
        V.wait_dtok(SY, ld0 + 16 * (c + 1))
        V.wait_dtok(G, gdt_sc)
        if c + 1 < NCHUNK:
            j = (c + 1) % 2
            SY.wait_tok(V, tok_free[j])
            SY.dma_start(out=pb_[j][:, :], in_=pr_d[(c + 1) * 128:(c + 2) * 128, :])
        p = pb_[i]
        V.tensor_scalar(prow[:, 40:41], pcol[:, :], bcol[:, :], float(c * 128),
                        ALU.add, ALU.add)
        V.tensor_tensor(prow[:, 0:8], p[:, :], vit[:, 0:8], ALU.is_gt)
        V.tensor_tensor(prow[:, 8:16], p[:, :], vit[:, 0:8], ALU.is_equal)
        V.tensor_scalar(prow[:, 16:24], vit[:, 8:16], prow[:, 40:41], None, ALU.is_ge)
        V.tensor_tensor(prow[:, 8:16], prow[:, 8:16], prow[:, 16:24], ALU.mult)
        V.tensor_tensor(prow[:, 0:8], prow[:, 0:8], prow[:, 8:16], ALU.add)
        V.reduce_max(prow[:, 41:42], prow[:, 0:8], axis=AX.X)
        V.tensor_tensor(prow[:, 8:16], p[:, :], prow[:, 0:8], ALU.mult)
        V.tensor_tensor(prow[:, 8:16], prow[:, 8:16], prow[:, 0:8], ALU.add)
        V.tensor_scalar(prow[:, 8:16], prow[:, 8:16], 1.0, None, ALU.subtract)
        V.tensor_scalar(prow[:, 8:16], prow[:, 8:16], prow[:, 41:42], None, ALU.mult)
        V.tensor_scalar(prow[:, 42:43], prow[:, 41:42], -1.0, 1.0, ALU.mult, ALU.add)
        V.tensor_scalar(prow[:, 16:24], p[:, :], prow[:, 42:43], None, ALU.mult)
        V.tensor_tensor(prow[:, 8:16], prow[:, 8:16], prow[:, 16:24], ALU.add)
        V.reduce_max(prow[:, 43:44], prow[:, 8:16], axis=AX.X)
        V.tensor_scalar(prow[:, 16:24], prow[:, 8:16], prow[:, 43:44], None, ALU.is_ge)
        V.tensor_tensor(prow[:, 16:24], prow[:, 16:24], wpow[:, :], ALU.mult)
        V.reduce_max(prow[:, 44:45], prow[:, 16:24], axis=AX.X)
        V.tensor_scalar(prow[:, 16:24], prow[:, 16:24], prow[:, 44:45], None, ALU.is_ge)
        V.tensor_tensor(prow[:, 24:32], p[:, :], prow[:, 16:24], ALU.mult)
        V.reduce_sum(gsel[:, :], prow[:, 24:32], axis=AX.X)
        V.tensor_copy(pickb[:, :], prow[:, 16:24])
        tok_free[i] = V.token()
        T.wait(V)
        T.wait_tok(V, 0)
        T.matmul(ps_c[:, 0:8], ltri[:, :], pickb[:, :], start=True, stop=True, inc=True)
        T.matmul(ps_c[0:1, 8:16], onescf[:, :], pickb[:, :], start=True, stop=True, inc=True)
        V.wait(T)
        V.tensor_scalar(zloc[:, :], ps_c[:, 0:8], -1.0, None, ALU.add)
        T.wait(V)
        T.matmul(ps_m[:, 24:32], onesf[:, :], totrow[0:1, :], start=True, stop=True, inc=True)
        V.wait(T)
        V.tensor_tensor(zloc[:, :], zloc[:, :], ps_m[:, 24:32], ALU.add)
        V.tensor_tensor(totrow[0:1, :], totrow[0:1, :], ps_c[0:1, 8:16], ALU.add)
        V.tensor_scalar(znov[:, :], zloc[:, :], float(CAP), None, ALU.is_lt)
        V.tensor_tensor(zloc[:, :], zloc[:, :], ecap[:, :], ALU.add)
        V.tensor_tensor(zloc[:, :], zloc[:, :], znov[:, :], ALU.mult)
        V.tensor_scalar(znov[:, :], znov[:, :], -float(8 * CAP + 32), float(8 * CAP + 32),
                        ALU.mult, ALU.add)
        V.tensor_tensor(zloc[:, :], zloc[:, :], znov[:, :], ALU.add)
        V.tensor_tensor(zloc[:, :], zloc[:, :], prow[:, 16:24], ALU.mult)
        V.reduce_sum(slotc[:, :], zloc[:, :], axis=AX.X)
        V.tensor_copy(sloti[:, :], slotc[:, :])
        V.tensor_scalar(tokid[:, :], pcol[:, :], float(c * 128), None, ALU.add)
        if STAGE >= 2:
            G.wait(V)
            G.wait_dtok(G, gdt_init)
            G.indirect_dma_start(il_d[:, :], bass.IndirectOffsetOnAxis(sloti[:, :], 0),
                                 tokid[:, :], None)
            G.indirect_dma_start(gl_d[:, :], bass.IndirectOffsetOnAxis(sloti[:, :], 0),
                                 gsel[:, :], None)
            gdt_sc = G.dtoken()

    if STAGE < 3:
        G.barrier_all()
        S.barrier_all()
        V.barrier_all()
        fw.build()
        return nc
    SY.wait_dtok(G, gdt_sc)
    vtok_tr = [0, 0]
    stok_s1 = [0, 0]
    vtok_s2 = [0, 0]
    ttok_w = 0
    ttok_xg = [0, 0]
    gdt_out = [0, 0]
    dma_in = [0, 0]
    wpat1 = lambda e: bass.AP(w1_d, e * D * F, [[F, 128], [128 * F, 8], [1, F]])
    wpat2 = lambda e: bass.AP(w2_d, e * F * D, [[D, 128], [128 * D, 16], [1, D]])
    for e in range(8):
        G.wait_tok(T, ttok_w)
        G.dma_start(out=w1[:, :], in_=wpat1(e))
        G.dma_start(out=w2[:, :], in_=wpat2(e))
        wld = G.dtoken()
        T.wait_dtok(G, wld)
        for stile in range(2):
            ib = stile % 2 if e % 2 == 0 else (stile + 1) % 2
            base = e * CAP + stile * 512
            SY.wait_tok(T, ttok_xg[ib])
            SY.wait_dtok(G, gdt_out[ib])
            for g in range(4):
                SY.dma_start(out=idxc[ib][:, g:g + 1],
                             in_=il_d[base + g * 128: base + (g + 1) * 128, :])
                SY.dma_start(out=gc[ib][:, g:g + 1],
                             in_=gl_d[base + g * 128: base + (g + 1) * 128, :])
            idt = SY.dtoken()
            G.wait_dtok(SY, idt)
            G.wait_tok(T, ttok_xg[ib])
            for g in range(4):
                G.indirect_dma_start(xg[ib][:, g * D:(g + 1) * D], None,
                                     xnp_d[:, :],
                                     bass.IndirectOffsetOnAxis(idxc[ib][:, g:g + 1], 0))
            gdt_g = G.dtoken()
            if STAGE < 4:
                continue
            T.wait_dtok(G, gdt_g)
            for g in range(4):
                for kd in range(8):
                    pbk = ps_trb[kd % 2]
                    T.wait_tok(V, vtok_tr[kd % 2])
                    T.transpose(pbk[:, :], xg[ib][:, g * D + kd * 128: g * D + (kd + 1) * 128],
                                identb[:, :], inc=True)
                    V.wait(T)
                    V.tensor_copy(xgT[ib][:, kd * 512 + g * 128: kd * 512 + (g + 1) * 128],
                                  pbk[:, :])
                    vtok_tr[kd % 2] = V.token()
            ttok_xg[ib] = T.token()
            T.wait(V)
            for fc in range(16):
                pbk = ps_1[fc % 2]
                T.wait_tok(S, stok_s1[fc % 2])
                for kd in range(8):
                    T.matmul(pbk[:, :], w1[:, kd * F + fc * 128: kd * F + fc * 128 + 128],
                             xgT[ib][:, kd * 512:(kd + 1) * 512],
                             start=(kd == 0), stop=(kd == 7), inc=(kd == 7))
                S.wait(T)
                S.activation(mid[:, fc * 512:(fc + 1) * 512], pbk[:, :], AF.Gelu)
                stok_s1[fc % 2] = S.token()
            T.wait(S)
            V.wait_dtok(G, gdt_out[ib])
            for g in range(4):
                for half in range(2):
                    k2i = (g * 2 + half) % 2
                    pbk = ps_2[k2i]
                    T.wait_tok(V, vtok_s2[k2i])
                    for fc in range(16):
                        T.matmul(pbk[:, :], mid[:, fc * 512 + g * 128: fc * 512 + g * 128 + 128],
                                 w2[:, fc * D + half * 512: fc * D + half * 512 + 512],
                                 start=(fc == 0), stop=(fc == 15), inc=(fc == 15))
                    V.wait(T)
                    V.tensor_scalar(outg[ib][:, g * D + half * 512: g * D + half * 512 + 512],
                                    pbk[:, :], gc[ib][:, g:g + 1], None, ALU.mult)
                    vtok_s2[k2i] = V.token()
            ttok_w = T.token()
            if STAGE >= 5:
                G.wait(V)
                for g in range(4):
                    G.indirect_dma_start(x4a_d[:, :],
                                         bass.IndirectOffsetOnAxis(idxc[ib][:, g:g + 1], 0),
                                         outg[ib][:, g * D:(g + 1) * D], None,
                                         oob_is_err=True,
                                         compute_op=ALU.add)
                gdt_out[ib] = G.dtoken()

    G.dma_start(out=x4_d[:, :], in_=x4a_d[0:TC, :])
    G.barrier_all()
    S.barrier_all()
    fw.build()
    return nc


def run_l1_ffn(inp, res7, res8):
    vitab = np.empty((2, 8), np.float32)
    for m in range(NCORES):
        vitab[0, m] = res8.results[m]["route"][0, 0]
        vitab[1, m] = res8.results[m]["route"][0, 1]
    nc = build_l1_ffn()
    maps = []
    for c in range(NCORES):
        maps.append(dict(
            x3=res7.results[c]["x3"],
            xn3=res7.results[c]["xn3"],
            probs=res7.results[c]["probs"],
            vitab=vitab,
            idxbase=np.full((1, 1), float(c * TC), np.float32),
            w1=np.ascontiguousarray(np.asarray(inp["L1_ffn_W1"])),
            w2=np.ascontiguousarray(np.asarray(inp["L1_ffn_W2"])),
        ))
    return run_bass_kernel_spmd(nc, maps, core_ids=list(range(NCORES)))


def _shard_tokens(x):
    out = []
    for b in range(4):
        for h in range(2):
            out.append(np.ascontiguousarray(x[b, h * 4096:(h + 1) * 4096, :]))
    return out


def run_l0_qkv(inp, shards):
    nc1 = build_l0_qkv()
    wqT = np.ascontiguousarray(np.asarray(inp["L0_attn_Wq"]).T)
    wkT = np.ascontiguousarray(np.asarray(inp["L0_attn_Wk"]).T)
    wvT = np.ascontiguousarray(np.asarray(inp["L0_attn_Wv"]).T)
    wbT = np.ascontiguousarray(np.asarray(inp["L0_attn_beta_w"]).T)
    in_maps = [dict(
        x=shards[c], wqT=wqT, wkT=wkT, wvT=wvT, wbT=wbT,
        rin=np.ascontiguousarray(np.asarray(inp["L0_attn_rin_w"]).reshape(1, D)),
        rk=np.ascontiguousarray(np.asarray(inp["L0_attn_rk_w"]).reshape(1, D)),
        rv=np.ascontiguousarray(np.asarray(inp["L0_attn_rv_w"]).reshape(1, D)),
        beta_b=np.ascontiguousarray(np.asarray(inp["L0_attn_beta_b"]).reshape(1, 16)),
        bbl=np.ascontiguousarray(np.asarray(inp["L0_attn_base_beta_logit"]).reshape(1, 16)),
    ) for c in range(NCORES)]
    return run_bass_kernel_spmd(nc1, in_maps, core_ids=list(range(NCORES)))


def kernel(**inp):
    x = np.asarray(inp["L0_attn_x"])
    shards = _shard_tokens(x)
    res1 = run_l0_qkv(inp, shards)
    res2 = run_l0_mixer(res1)
    o_tok = host_regroup_o(res2)
    res3 = run_l0_post(inp, shards, o_tok)
    res4 = run_l0_ffn(inp, res3)
    x2_list = [res4.results[c]["x2"] for c in range(NCORES)]
    res5 = run_l1_abc(inp, x2_list)
    res6 = run_l1_scan(res5)
    ys_list = host_regroup_y(res6)
    res7 = run_l1_post(inp, x2_list, res5, ys_list)
    res8 = run_l1_route(res7)
    res9 = run_l1_ffn(inp, res7, res8)
    out = np.empty((4, 2 * TC, D), np.float32)
    for c in range(NCORES):
        out[c // 2, (c % 2) * TC:(c % 2 + 1) * TC, :] = res9.results[c]["x4"]
    return out


# version 05 (final)
# speedup vs baseline: 1.0106x; 1.0106x over previous
import contextlib

import numpy as np
import ml_dtypes
import concourse.bass as bass
import concourse.mybir as mybir
from concourse.bass_utils import run_bass_kernel_spmd

F32 = mybir.dt.float32
BF16 = mybir.dt.bfloat16
I32 = mybir.dt.int32
AF = mybir.ActivationFunctionType
ALU = mybir.AluOpType
AX = mybir.AxisListType

NCORES = 8
TC = 4096
D = 1024
NH, DH = 16, 64
CH = 128
NCHUNK = TC // CH
BF = ml_dtypes.bfloat16

DMA_OPS = {"dma_start", "indirect_dma_start", "collective_compute", "dma_start_transpose"}


class Eq:
    def __init__(self, fw, name, self_serialize=True):
        self.fw = fw
        self.name = name
        self.self_serialize = self_serialize
        self.sem_c = fw.st.enter_context(fw.nc.semaphore(f"qc_{name}"))
        self.sem_d = fw.st.enter_context(fw.nc.semaphore(f"qd_{name}"))
        self.nc_ = 0
        self.nd_ = 0
        self.recs = []
        self._pending = {}

    def token(self):
        return self.nc_

    def dtoken(self):
        return self.nd_

    def _add_wait(self, sem, val):
        if val <= 0:
            return
        if self._pending.get(sem, 0) < val:
            self._pending[sem] = val

    def wait(self, other, dma=True):
        self._add_wait(other.sem_c, other.nc_)
        if dma:
            self._add_wait(other.sem_d, other.nd_)

    def wait_tok(self, other, tok):
        self._add_wait(other.sem_c, tok)

    def wait_dtok(self, other, dtok):
        self._add_wait(other.sem_d, dtok)

    def op(self, opname, *args, inc=None, **kwargs):
        is_dma = opname in DMA_OPS
        waits = []
        if self.self_serialize and self.nc_ > 0:
            waits.append((self.sem_c, self.nc_))
        for sem, val in self._pending.items():
            waits.append((sem, val))
        self._pending.clear()
        if inc is None:
            inc = True
        if is_dma:
            inc_sem, inc_val = self.sem_d, 16
            self.nd_ += 16
        elif inc:
            inc_sem, inc_val = self.sem_c, 1
            self.nc_ += 1
        else:
            inc_sem, inc_val = None, 0
        self.recs.append((opname, args, kwargs, waits, inc_sem, inc_val))

    def barrier_all(self):
        self.recs.append(("__wait__", (), {},
                          [(self.sem_d, self.nd_), (self.sem_c, self.nc_)], None, 0))

    def __getattr__(self, item):
        if item.startswith("_"):
            raise AttributeError(item)

        def wrapped(*args, inc=None, **kwargs):
            return self.op(item, *args, inc=inc, **kwargs)

        return wrapped


class Fw:
    def __init__(self, nc):
        self.nc = nc
        self.st = contextlib.ExitStack()
        self.V = Eq(self, "vector")
        self.S = Eq(self, "scalar")
        self.T = Eq(self, "tensor", self_serialize=False)
        self.G = Eq(self, "gpsimd")
        self.SY = Eq(self, "sync", self_serialize=False)
        self._cache = {}

    def sb(self, name, shape, dt=F32):
        return self.st.enter_context(self.nc.sbuf_tensor(name, shape, dt))

    def psbank(self, name):
        return self.st.enter_context(self.nc.psum_tensor(name, [128, 512], F32))

    def init_consts(self):
        c = self._cache
        c["ident"] = self.sb("c_ident", [128, 128], F32)
        c["pcol"] = self.sb("c_pcol", [128, 1], F32)
        c["ones_col"] = self.sb("c_onescol", [128, 1], BF16)
        c["ones_row"] = self.sb("c_onesrow", [1, 128], BF16)
        c["ones1"] = self.sb("c_ones1", [1, 1], BF16)
        G, V, T = self.G, self.V, self.T
        G.iota(c["ident"][:, :], [[1, 128]], channel_multiplier=0,
               allow_small_or_imprecise_dtypes=True)
        G.iota(c["pcol"][:, :], [[0, 1]], channel_multiplier=1,
               allow_small_or_imprecise_dtypes=True)
        V.wait(G)
        V.tensor_scalar(c["ident"][:, :], c["ident"][:, :], c["pcol"][:, :],
                        None, ALU.is_equal)
        V.memset(c["ones_col"][:, :], 1.0)
        V.memset(c["ones_row"][:, :], 1.0)
        V.memset(c["ones1"][:, :], 1.0)
        T.wait(V)

    @property
    def ident(self):
        return self._cache["ident"][:, :]

    @property
    def ones_col(self):
        return self._cache["ones_col"][:, :]

    @property
    def ones_row(self):
        return self._cache["ones_row"][:, :]

    @property
    def ones1(self):
        return self._cache["ones1"][:, :]

    def build(self):
        def mk(q):
            def emit(eng):
                for opname, args, kwargs, waits, inc_sem, inc_val in q.recs:
                    for sem, val in waits:
                        eng.wait_ge(sem, val)
                    if opname == "__wait__":
                        continue
                    inst = getattr(eng, opname)(*args, **kwargs)
                    if inc_sem is not None:
                        inst.then_inc(inc_sem, inc_val)
            return emit

        with self.nc.Block() as block:
            block.sync(mk(self.SY))
            block.gpsimd(mk(self.G))
            block.vector(mk(self.V))
            block.scalar(mk(self.S))
            block.tensor(mk(self.T))
        self.st.close()
        return self.nc


def ln_stats(fw, xt, st, scratch, n=D, eps=1e-5):
    V, S = fw.V, fw.S
    V.reduce_sum(st[:, 4:5], xt, axis=AX.X)
    V.tensor_scalar(st[:, 0:1], st[:, 4:5], 1.0 / n, None, ALU.mult)
    S.wait(V)
    S.activation(scratch, xt, AF.Square, accum_out=st[:, 5:6])
    V.wait(S)
    V.tensor_scalar(st[:, 6:7], st[:, 5:6], 1.0 / n, None, ALU.mult)
    V.tensor_tensor(st[:, 7:8], st[:, 0:1], st[:, 0:1], ALU.mult)
    V.tensor_sub(st[:, 2:3], st[:, 6:7], st[:, 7:8])
    V.tensor_scalar(st[:, 7:8], st[:, 2:3], eps, None, ALU.add)
    V.reciprocal(st[:, 6:7], st[:, 7:8])
    S.wait(V)
    S.activation(st[:, 1:2], st[:, 6:7], AF.Sqrt)
    fw.V.wait(S)


def build_l0_qkv():
    nc = bass.Bass("TRN2", target_bir_lowering=False)
    fw = Fw(nc)
    V, S, T, G, SY = fw.V, fw.S, fw.T, fw.G, fw.SY

    x_d = nc.dram_tensor("x", [TC, D], F32, kind="ExternalInput").ap()
    wq_d = nc.dram_tensor("wqT", [D, D], F32, kind="ExternalInput")
    wk_d = nc.dram_tensor("wkT", [D, D], F32, kind="ExternalInput")
    wv_d = nc.dram_tensor("wvT", [D, D], F32, kind="ExternalInput")
    wb_d = nc.dram_tensor("wbT", [D, 16], F32, kind="ExternalInput")
    rin_d = nc.dram_tensor("rin", [1, D], F32, kind="ExternalInput")
    rk_d = nc.dram_tensor("rk", [1, D], F32, kind="ExternalInput")
    rv_d = nc.dram_tensor("rv", [1, D], F32, kind="ExternalInput")
    bb_d = nc.dram_tensor("beta_b", [1, 16], F32, kind="ExternalInput")
    bbl_d = nc.dram_tensor("bbl", [1, 16], F32, kind="ExternalInput")

    q_o = nc.dram_tensor("q_tok", [TC, D], BF16, kind="ExternalOutput").ap()
    k_o = nc.dram_tensor("kt_tok", [TC, D], BF16, kind="ExternalOutput").ap()
    v_o = nc.dram_tensor("vt_tok", [TC, D], BF16, kind="ExternalOutput").ap()
    b_o = nc.dram_tensor("beta_tok", [TC, 16], F32, kind="ExternalOutput").ap()

    wq = fw.sb("wq", [128, 8 * D], BF16)
    wk = fw.sb("wk", [128, 8 * D], BF16)
    wv = fw.sb("wv", [128, 8 * D], BF16)
    wb = fw.sb("wb", [128, 8 * 16], BF16)
    rin_c = fw.sb("rin_c", [128, 8], F32)
    rk_c = fw.sb("rk_c", [128, 8], F32)
    rv_c = fw.sb("rv_c", [128, 8], F32)
    bb_row = fw.sb("bb_row", [1, 16], BF16)
    bbl_row = fw.sb("bbl_row", [1, 16], BF16)
    bb_t = fw.sb("bb_t", [128, 16], F32)
    bbl_t = fw.sb("bbl_t", [128, 16], F32)
    srow = fw.sb("srow", [1, 128], F32)
    srow_b = fw.sb("srow_b", [1, 128], BF16)

    NT = 2
    xt = [fw.sb(f"xt{i}", [128, D], F32) for i in range(NT)]
    xs = [fw.sb(f"xs{i}", [128, D], F32) for i in range(NT)]
    xnT = [fw.sb(f"xnT{i}", [128, D], BF16) for i in range(NT)]
    st_ = [fw.sb(f"st{i}", [128, 16], F32) for i in range(NT)]
    qout = [fw.sb(f"qout{i}", [128, D], BF16) for i in range(NT)]
    kout = [fw.sb(f"kout{i}", [128, D], BF16) for i in range(NT)]
    vout = [fw.sb(f"vout{i}", [128, D], BF16) for i in range(NT)]
    bout = [fw.sb(f"bout{i}", [128, 16], F32) for i in range(NT)]
    sq = [fw.sb(f"sq{i}", [128, D], BF16) for i in range(NT)]

    ps_tr = [fw.psbank(f"ps_tr{i}") for i in range(2)]
    ps_q = fw.psbank("ps_q")
    ps_k = fw.psbank("ps_k")
    ps_v = fw.psbank("ps_v")
    ps_b = fw.psbank("ps_b")

    fw.init_consts()

    def wchunk_ap(h, m):
        return bass.AP(h, 0, [[m, 128], [128 * m, 8], [1, m]])

    def vec_ap(h):
        return bass.AP(h, 0, [[1, 128], [128, 8]])

    G.dma_start(out=wq[:, :], in_=wchunk_ap(wq_d, D))
    G.dma_start(out=wk[:, :], in_=wchunk_ap(wk_d, D))
    G.dma_start(out=wv[:, :], in_=wchunk_ap(wv_d, D))
    G.dma_start(out=wb[:, :], in_=wchunk_ap(wb_d, 16))
    SY.dma_start(out=rin_c[:, :], in_=vec_ap(rin_d), allow_slow_non_contiguous=True)
    SY.dma_start(out=rk_c[:, :], in_=vec_ap(rk_d), allow_slow_non_contiguous=True)
    SY.dma_start(out=rv_c[:, :], in_=vec_ap(rv_d), allow_slow_non_contiguous=True)
    G.dma_start(out=bb_row[:, :], in_=bb_d[:, :])
    G.dma_start(out=bbl_row[:, :], in_=bbl_d[:, :])
    V.wait(G)
    V.wait(SY)
    for kd in range(8):
        V.tensor_scalar(wk[:, kd * D:(kd + 1) * D], wk[:, kd * D:(kd + 1) * D],
                        rk_c[:, kd:kd + 1], None, ALU.mult)
        V.tensor_scalar(wv[:, kd * D:(kd + 1) * D], wv[:, kd * D:(kd + 1) * D],
                        rv_c[:, kd:kd + 1], None, ALU.mult)
    T.wait(V)
    T.matmul(ps_b[:, 16:32], fw.ones_row, bb_row[:, :], start=True, stop=True, inc=True)
    V.wait(T)
    V.tensor_copy(bb_t[:, :], ps_b[:, 16:32])
    T.wait(V)
    T.matmul(ps_b[:, 16:32], fw.ones_row, bbl_row[:, :], start=True, stop=True, inc=True)
    S.wait(T)
    S.activation(bbl_t[:, :], ps_b[:, 16:32], AF.Sigmoid)
    V.wait(S)

    ld0 = SY.dtoken()
    SY.dma_start(out=xt[0][:, :], in_=x_d[0:128, :])

    tok_xt_free = [0] * NT
    tok_xnT_free = [0] * NT
    dtok_store = [0] * NT
    vtok_evac = {"q": 0, "k": 0, "v": 0, "b": 0, "tr0": 0, "tr1": 0}

    for c in range(NCHUNK):
        i = c % NT
        st = st_[i]
        V.wait_dtok(S, dtok_store[i])
        V.wait_dtok(SY, ld0 + 16 * (c + 1))
        if c + 1 < NCHUNK:
            j = (c + 1) % NT
            SY.wait_tok(V, tok_xt_free[j])
            SY.dma_start(out=xt[j][:, :], in_=x_d[(c + 1) * 128:(c + 2) * 128, :])

        ln_stats(fw, xt[i][:, :], st, xs[i][:, :])
        V.tensor_tensor(st[:, 8:9], st[:, 2:3], st[:, 1:2], ALU.mult)
        V.tensor_tensor(st[:, 8:9], st[:, 8:9], st[:, 1:2], ALU.mult)
        V.tensor_scalar(st[:, 9:10], st[:, 8:9], 1e-6, None, ALU.add)
        V.reciprocal(st[:, 10:11], st[:, 9:10])
        S.wait(V)
        S.activation(st[:, 11:12], st[:, 10:11], AF.Sqrt)
        V.wait(S)
        V.tensor_scalar(xs[i][:, :], xt[i][:, :], st[:, 0:1], st[:, 1:2],
                        ALU.subtract, ALU.mult)
        V.tensor_scalar(xs[i][:, :], xs[i][:, :], st[:, 11:12], None, ALU.mult)
        tok_xt_free[i] = V.token()

        T.wait(V)
        T.wait_tok(V, tok_xnT_free[i] * 0)
        for kd in range(8):
            pb = ps_tr[kd % 2]
            T.wait_tok(V, vtok_evac[f"tr{kd % 2}"])
            T.transpose(pb[:, 0:128], xs[i][:, kd * 128:(kd + 1) * 128],
                        fw.ident, inc=True)
            V.wait(T)
            V.tensor_scalar(xnT[i][:, kd * 128:(kd + 1) * 128], pb[:, 0:128],
                            rin_c[:, kd:kd + 1], None, ALU.mult)
            vtok_evac[f"tr{kd % 2}"] = V.token()

        S.wait(V)
        S.activation(sq[i][:, :], xnT[i][:, :], AF.Square)
        T.wait(S)
        T.wait_tok(V, vtok_evac["b"])
        for kd in range(8):
            T.matmul(ps_b[0:1, 128:256], fw.ones_col,
                     sq[i][:, kd * 128:(kd + 1) * 128],
                     start=(kd == 0), stop=(kd == 7), inc=(kd == 7))
        V.wait(T)
        V.tensor_scalar(srow[:, :], ps_b[0:1, 128:256], 1.0 / D, 1e-6,
                        ALU.mult, ALU.add)
        V.reciprocal(srow[:, :], srow[:, :])
        S.wait(V)
        S.activation(srow[:, :], srow[:, :], AF.Sqrt)
        V.wait(S)
        V.tensor_copy(srow_b[:, :], srow[:, :])
        T.wait(V)
        T.matmul(ps_b[:, 32:33], srow_b[:, :], fw.ones1, start=True, stop=True, inc=True)
        V.wait(T)
        V.tensor_copy(st[:, 12:13], ps_b[:, 32:33])
        vtok_evac["b"] = V.token()

        for half in range(2):
            T.wait_tok(V, vtok_evac["q"])
            for kd in range(8):
                T.matmul(ps_q[:, :], xnT[i][:, kd * 128:(kd + 1) * 128],
                         wq[:, kd * D + half * 512: kd * D + half * 512 + 512],
                         start=(kd == 0), stop=(kd == 7), inc=(kd == 7))
            V.wait(T)
            V.tensor_copy(qout[i][:, half * 512:half * 512 + 512], ps_q[:, :])
            vtok_evac["q"] = V.token()
            T.wait_tok(V, vtok_evac["k"])
            for kd in range(8):
                T.matmul(ps_k[:, :], xnT[i][:, kd * 128:(kd + 1) * 128],
                         wk[:, kd * D + half * 512: kd * D + half * 512 + 512],
                         start=(kd == 0), stop=(kd == 7), inc=(kd == 7))
            V.wait(T)
            V.tensor_scalar(kout[i][:, half * 512:half * 512 + 512], ps_k[:, :],
                            st[:, 12:13], None, ALU.mult)
            vtok_evac["k"] = V.token()
            T.wait_tok(V, vtok_evac["v"])
            for kd in range(8):
                T.matmul(ps_v[:, :], xnT[i][:, kd * 128:(kd + 1) * 128],
                         wv[:, kd * D + half * 512: kd * D + half * 512 + 512],
                         start=(kd == 0), stop=(kd == 7), inc=(kd == 7))
            V.wait(T)
            V.tensor_scalar(vout[i][:, half * 512:half * 512 + 512], ps_v[:, :],
                            st[:, 12:13], None, ALU.mult)
            vtok_evac["v"] = V.token()

        T.wait_tok(V, vtok_evac["b"])
        for kd in range(8):
            T.matmul(ps_b[:, 0:16], xnT[i][:, kd * 128:(kd + 1) * 128],
                     wb[:, kd * 16:(kd + 1) * 16],
                     start=(kd == 0), stop=(kd == 7), inc=(kd == 7))
        tok_xnT_free[i] = T.token()
        V.wait(T)
        V.tensor_tensor(bout[i][:, :], ps_b[:, 0:16], bb_t[:, :], ALU.add)
        S.wait(V)
        S.activation(bout[i][:, :], bout[i][:, :], AF.Sigmoid)
        V.wait(S)
        V.tensor_tensor(bout[i][:, :], bout[i][:, :], bbl_t[:, :], ALU.mult)
        V.tensor_scalar(bout[i][:, :], bout[i][:, :], 0.9, 0.9995, ALU.max, ALU.min)
        vtok_evac["b"] = V.token()

        S.wait(V)
        S.dma_start(out=q_o[c * 128:(c + 1) * 128, :], in_=qout[i][:, :])
        S.dma_start(out=k_o[c * 128:(c + 1) * 128, :], in_=kout[i][:, :])
        S.dma_start(out=v_o[c * 128:(c + 1) * 128, :], in_=vout[i][:, :])
        S.dma_start(out=b_o[c * 128:(c + 1) * 128, :], in_=bout[i][:, :])
        dtok_store[i] = S.dtoken()

    S.barrier_all()
    fw.build()
    return nc




def build_l0_mixer():
    nc = bass.Bass("TRN2", target_bir_lowering=False)
    fw = Fw(nc)
    V, S, T, G, SY = fw.V, fw.S, fw.T, fw.G, fw.SY
    SEQ = 2 * TC
    NCH = SEQ // CH

    qf_d = nc.dram_tensor("qf", [8, 64, SEQ], BF16, kind="ExternalInput")
    kf_d = nc.dram_tensor("kf", [8, 64, SEQ], BF16, kind="ExternalInput")
    kt_d = nc.dram_tensor("kt", [8, SEQ, 64], BF16, kind="ExternalInput")
    vt_d = nc.dram_tensor("vt", [8, SEQ, 64], BF16, kind="ExternalInput")
    be_d = nc.dram_tensor("betaf", [8, SEQ], F32, kind="ExternalInput")
    tm_d = nc.dram_tensor("trimask", [128, 128], BF16, kind="ExternalInput")
    m8_d = nc.dram_tensor("mask8", [8, 1024], BF16, kind="ExternalInput")
    o_d = nc.dram_tensor("o", [8, SEQ, 64], BF16, kind="ExternalOutput")

    bet = fw.sb("bet", [8, SEQ], F32)
    onesb = fw.sb("onesb", [8, SEQ], F32)
    den = fw.sb("den", [8, SEQ], F32)
    rden = fw.sb("rden", [8, SEQ], F32)
    gam = fw.sb("gam", [8, CH], F32)
    gam_bf = fw.sb("gam_bf", [8, CH], BF16)
    gtok = fw.sb("gtok", [128, 24], F32)
    tm = fw.sb("tm", [128, 128], BF16)
    m8 = fw.sb("m8", [8, 1024], BF16)
    ident8 = fw.sb("ident8", [8, 8], F32)
    k2 = fw.sb("k2", [128, 64], BF16)
    dm = fw.sb("dm", [128, 128], BF16)
    stl = fw.sb("stl", [128, 128], BF16)
    msb = fw.sb("msb", [128, 256], F32)
    mbf = fw.sb("mbf", [128, 256], BF16)
    mscr = fw.sb("mscr", [128, 64], F32)
    t1s = fw.sb("t1s", [128, 64], F32)
    NT = 2
    qf_s = [fw.sb(f"qf_s{i}", [128, 512], BF16) for i in range(NT)]
    kf_s = [fw.sb(f"kf_s{i}", [128, 512], BF16) for i in range(NT)]
    kt_s = [fw.sb(f"kt_s{i}", [128, 512], BF16) for i in range(NT)]
    vt_s = [fw.sb(f"vt_s{i}", [128, 512], BF16) for i in range(NT)]
    o_s = [fw.sb(f"o_s{i}", [128, 512], BF16) for i in range(NT)]

    ps_t1 = fw.psbank("ps_t1")
    ps_s = fw.psbank("ps_s")
    ps_o1 = fw.psbank("ps_o1")
    ps_o2 = fw.psbank("ps_o2")
    ps_u = fw.psbank("ps_u")
    ps_tr = fw.psbank("ps_tr")

    fw.init_consts()

    SY.dma_start(out=bet[:, :], in_=be_d[:, :])
    SY.dma_start(out=tm[:, :], in_=tm_d[:, :])
    SY.dma_start(out=m8[:, :], in_=m8_d[:, :])
    ld_pre = SY.dtoken()

    V.wait_dtok(SY, ld_pre)
    T.wait(SY)
    V.memset(onesb[:, :], 1.0)
    V.tensor_copy(ident8[:, :], fw.ident[0:8, 0:8])
    V.tensor_tensor_scan(den[:, :], bet[:, :], onesb[:, :], 0.0, ALU.mult, ALU.add)
    V.tensor_scalar(rden[:, :], den[:, :], 1e-6, None, ALU.add)
    V.reciprocal(rden[:, :], rden[:, :])

    def chunk_ap(d, c):
        return bass.AP(d, c * CH * 64, [[64, 128], [SEQ * 64, 8], [1, 64]])

    def fm_ap(d, c):
        return bass.AP(d, c * CH, [[SEQ, 128], [2 * 64 * SEQ, 4], [1, 128]])

    SY.dma_start(out=qf_s[0][:, :], in_=fm_ap(qf_d, 0))
    SY.dma_start(out=kf_s[0][:, :], in_=fm_ap(kf_d, 0))
    SY.dma_start(out=kt_s[0][:, :], in_=chunk_ap(kt_d, 0))
    SY.dma_start(out=vt_s[0][:, :], in_=chunk_ap(vt_d, 0))
    ld0 = SY.dtoken()

    V.memset(msb[:, :], 0.0)
    V.memset(mbf[:, :], 0.0)

    tok_buf_free = [0] * NT
    ttok_buf_free = [0] * NT
    dtok_ostore = [0] * NT
    vtok_bank = {k: 0 for k in ("t1", "s", "o1", "o2", "u", "tr")}
    ttok_o = 0
    per_chunk_d = 16 * 4

    for c in range(NCH):
        i = c % NT
        V.wait_dtok(SY, ld0 + per_chunk_d * c)
        if c + 1 < NCH:
            j = (c + 1) % NT
            SY.wait_tok(V, tok_buf_free[j])
            SY.wait_tok(T, ttok_buf_free[j])
            SY.dma_start(out=qf_s[j][:, :], in_=fm_ap(qf_d, c + 1))
            SY.dma_start(out=kf_s[j][:, :], in_=fm_ap(kf_d, c + 1))
            SY.dma_start(out=kt_s[j][:, :], in_=chunk_ap(kt_d, c + 1))
            SY.dma_start(out=vt_s[j][:, :], in_=chunk_ap(vt_d, c + 1))

        V.tensor_tensor_scan(gam[:, :], bet[:, c * CH:(c + 1) * CH],
                             onesb[:, 0:CH], 1.0, ALU.mult, ALU.mult)
        V.tensor_copy(gam_bf[:, :], gam[:, :])
        T.wait(V)
        T.wait_tok(V, vtok_bank["tr"])
        T.transpose(ps_tr[:, 0:8], gam[:, :], ident8[:, :], inc=True)
        T.transpose(ps_tr[:, 8:16], rden[:, c * CH:(c + 1) * CH], ident8[:, :], inc=True)
        V.wait(T)
        V.tensor_copy(gtok[:, 0:8], ps_tr[:, 0:8])
        V.tensor_copy(gtok[:, 16:24], ps_tr[:, 8:16])
        V.reciprocal(gtok[:, 8:16], gtok[:, 0:8])
        vtok_bank["tr"] = V.token()

        V.wait_dtok(S, dtok_ostore[i])

        for u in range(8):
            base = 64 * (u % 2)
            j4 = u // 2
            qsl = qf_s[i][base:base + 64, j4 * 128:(j4 + 1) * 128]
            ksl = kf_s[i][base:base + 64, j4 * 128:(j4 + 1) * 128]
            ktsl = kt_s[i][:, u * 64:(u + 1) * 64]
            vtsl = vt_s[i][:, u * 64:(u + 1) * 64]
            gcol = gtok[:, u:u + 1]
            gicol = gtok[:, 8 + u:9 + u]
            rdcol = gtok[:, 16 + u:17 + u]
            tp = (0, 64) if u % 2 else None
            mrow = slice(base, base + 64)

            T.wait(V)
            T.wait_tok(V, vtok_bank["t1"])
            T.matmul(ps_t1[:, 0:128], m8[:, u * 128:(u + 1) * 128],
                     gam_bf[:, :], start=True, stop=True, inc=True)
            V.wait(T)
            V.scalar_tensor_tensor(dm[:, :], ps_t1[:, 0:128], gicol, tm[:, :],
                                   ALU.mult, ALU.mult)
            vtok_bank["t1"] = V.token()

            T.wait_tok(V, vtok_bank["s"])
            T.matmul(ps_s[:, 0:128], ksl, qsl, start=True, stop=True, inc=True)
            V.wait(T)
            V.tensor_tensor(stl[:, :], ps_s[:, 0:128], dm[:, :], ALU.mult)
            vtok_bank["s"] = V.token()

            T.wait(V)
            T.wait_tok(V, vtok_bank["o1"])
            T.matmul(ps_o1[:, 0:64], stl[:, :], vtsl, start=True, stop=True, inc=True)
            T.wait_tok(V, vtok_bank["o2"])
            T.matmul(ps_o2[:, 0:64], qsl, mbf[mrow, j4 * 64:(j4 + 1) * 64],
                     start=True, stop=True, inc=True)
            V.tensor_scalar(k2[:, :], ktsl, gicol, None, ALU.mult)
            T.wait(V)
            T.wait_tok(V, vtok_bank["u"])
            T.matmul(ps_u[0:64, 0:64] if u % 2 == 0 else ps_u[64:128, 0:64],
                     k2[:, :], vtsl, start=True, stop=True,
                     tile_position=(0, base), inc=True)
            T.matmul(ps_u[0:64, 64:65] if u % 2 == 0 else ps_u[64:128, 64:65],
                     m8[:, u * 128:u * 128 + 64], gam_bf[:, CH - 1:CH],
                     start=True, stop=True, tile_position=(0, base), inc=True)
            V.wait(T)
            V.tensor_scalar(t1s[0:128, 0:64], ps_o2[:, 0:64], gcol, rdcol,
                            ALU.mult, ALU.mult)
            V.scalar_tensor_tensor(o_s[i][:, u * 64:(u + 1) * 64], ps_o1[:, 0:64],
                                   rdcol, t1s[0:128, 0:64], ALU.mult, ALU.add)
            vtok_bank["o1"] = V.token()
            vtok_bank["o2"] = V.token()
            usl = ps_u[mrow, 0:64]
            gtsl = ps_u[mrow, 64:65]
            V.tensor_tensor(mscr[mrow, :],
                            msb[mrow, j4 * 64:(j4 + 1) * 64], usl, ALU.add)
            V.tensor_scalar(msb[mrow, j4 * 64:(j4 + 1) * 64],
                            mscr[mrow, :], gtsl, None, ALU.mult)
            vtok_bank["u"] = V.token()
            V.tensor_copy(mbf[mrow, j4 * 64:(j4 + 1) * 64],
                          msb[mrow, j4 * 64:(j4 + 1) * 64])

        tok_buf_free[i] = V.token()
        ttok_buf_free[i] = T.token()
        S.wait(V)
        S.dma_start(out=chunk_ap(o_d, c), in_=o_s[i][:, :])
        dtok_ostore[i] = S.dtoken()

    S.barrier_all()
    fw.build()
    return nc




def host_regroup_heads(res1):
    q_full = np.empty((4, 2 * TC, D), dtype=BF)
    k_full = np.empty((4, 2 * TC, D), dtype=BF)
    v_full = np.empty((4, 2 * TC, D), dtype=BF)
    be_full = np.empty((4, 2 * TC, 16), dtype=np.float32)
    for c in range(NCORES):
        b, h = c // 2, c % 2
        sl = slice(h * TC, (h + 1) * TC)
        r = res1.results[c]
        q_full[b, sl] = r["q_tok"]
        k_full[b, sl] = r["kt_tok"]
        v_full[b, sl] = r["vt_tok"]
        be_full[b, sl] = r["beta_tok"]
    maps = []
    for m in range(NCORES):
        qf = np.empty((8, 64, 2 * TC), dtype=BF)
        kf = np.empty((8, 64, 2 * TC), dtype=BF)
        kt = np.empty((8, 2 * TC, 64), dtype=BF)
        vt = np.empty((8, 2 * TC, 64), dtype=BF)
        bf_ = np.empty((8, 2 * TC), dtype=np.float32)
        for u in range(8):
            g = m * 8 + u
            b, hh = g // 16, g % 16
            qf[u] = q_full[b, :, hh * 64:(hh + 1) * 64].T
            kf[u] = k_full[b, :, hh * 64:(hh + 1) * 64].T
            kt[u] = k_full[b, :, hh * 64:(hh + 1) * 64]
            vt[u] = v_full[b, :, hh * 64:(hh + 1) * 64]
            bf_[u] = be_full[b, :, hh]
        maps.append(dict(qf=qf, kf=kf, kt=kt, vt=vt, betaf=bf_))
    trimask = np.triu(np.ones((128, 128), np.float32)).astype(BF)
    mask8 = np.zeros((8, 1024), np.float32)
    for u in range(8):
        mask8[u, u * 128:(u + 1) * 128] = 1.0
    mask8 = mask8.astype(BF)
    for mmap in maps:
        mmap["trimask"] = trimask
        mmap["mask8"] = mask8
    return maps


def run_l0_mixer(res1):
    nc2 = build_l0_mixer()
    maps = host_regroup_heads(res1)
    return run_bass_kernel_spmd(nc2, maps, core_ids=list(range(NCORES)))


def host_regroup_o(res2):
    o_full = np.empty((4, 2 * TC, D), dtype=BF)
    for m in range(NCORES):
        o = res2.results[m]["o"]
        for u in range(8):
            g = m * 8 + u
            b, hh = g // 16, g % 16
            o_full[b, :, hh * 64:(hh + 1) * 64] = o[u]
    return [np.ascontiguousarray(o_full[c // 2, (c % 2) * TC:(c % 2 + 1) * TC, :])
            for c in range(NCORES)]




def build_l0_post():
    nc = bass.Bass("TRN2", target_bir_lowering=False)
    fw = Fw(nc)
    V, S, T, G, SY = fw.V, fw.S, fw.T, fw.G, fw.SY

    x_d = nc.dram_tensor("x", [TC, D], F32, kind="ExternalInput")
    of_d = nc.dram_tensor("o_fm", [D, TC], BF16, kind="ExternalInput")
    wo_d = nc.dram_tensor("woT", [D, D], F32, kind="ExternalInput")
    bo_d = nc.dram_tensor("bo", [1, D], F32, kind="ExternalInput")
    lnw_d = nc.dram_tensor("lnw", [1, D], F32, kind="ExternalInput")
    lnb_d = nc.dram_tensor("lnb", [1, D], F32, kind="ExternalInput")
    x1_d = nc.dram_tensor("x1", [TC, D], F32, kind="ExternalOutput")

    wo = fw.sb("wo", [128, 8 * D], BF16)
    ofm = fw.sb("ofm", [128, 8 * TC], BF16)
    brow = fw.sb("brow", [1, 3 * D], F32)
    bt = fw.sb("bt", [128, 3 * D], F32)
    ones_row_f = fw.sb("ones_row_f", [1, 128], F32)
    NT = 2
    xt = [fw.sb(f"xt{i}", [128, D], F32) for i in range(NT)]
    yt = [fw.sb(f"yt{i}", [128, D], F32) for i in range(NT)]
    x1t = [fw.sb(f"x1t{i}", [128, D], F32) for i in range(NT)]
    st_ = [fw.sb(f"st{i}", [128, 16], F32) for i in range(NT)]
    ps_o = fw.psbank("ps_o")
    ps_b = fw.psbank("ps_b")

    fw.init_consts()

    G.dma_start(out=wo[:, :], in_=bass.AP(wo_d, 0, [[D, 128], [128 * D, 8], [1, D]]))
    G.dma_start(out=ofm[:, :], in_=bass.AP(of_d, 0, [[TC, 128], [128 * TC, 8], [1, TC]]))
    SY.dma_start(out=brow[0:1, 0:D], in_=bo_d[:, :])
    SY.dma_start(out=brow[0:1, D:2 * D], in_=lnw_d[:, :])
    SY.dma_start(out=brow[0:1, 2 * D:3 * D], in_=lnb_d[:, :])
    V.wait(SY)
    V.wait(G)
    T.wait(G)
    V.memset(ones_row_f[:, :], 1.0)
    T.wait(V)
    vtb = 0
    for seg in range(6):
        T.wait_tok(V, vtb)
        T.matmul(ps_b[:, 0:512], ones_row_f[:, :], brow[0:1, seg * 512:(seg + 1) * 512],
                 start=True, stop=True, inc=True)
        V.wait(T)
        V.tensor_copy(bt[:, seg * 512:(seg + 1) * 512], ps_b[:, 0:512])
        vtb = V.token()

    ld0 = SY.dtoken()
    SY.dma_start(out=xt[0][:, :], in_=x_d[0:128, :])
    tok_xt_free = [0] * NT
    dtok_store = [0] * NT
    vtok_o = 0

    for c in range(NCHUNK):
        i = c % NT
        st = st_[i]
        V.wait_dtok(SY, ld0 + 16 * (c + 1))
        V.wait_dtok(S, dtok_store[i])
        if c + 1 < NCHUNK:
            j = (c + 1) % NT
            SY.wait_tok(V, tok_xt_free[j])
            SY.dma_start(out=xt[j][:, :], in_=x_d[(c + 1) * 128:(c + 2) * 128, :])

        for half in range(2):
            T.wait(V)
            T.wait_tok(V, vtok_o)
            for ke in range(8):
                T.matmul(ps_o[:, :], ofm[:, ke * TC + c * 128: ke * TC + (c + 1) * 128],
                         wo[:, ke * D + half * 512: ke * D + half * 512 + 512],
                         start=(ke == 0), stop=(ke == 7), inc=(ke == 7))
            V.wait(T)
            V.tensor_copy(yt[i][:, half * 512:half * 512 + 512], ps_o[:, :])
            vtok_o = V.token()
        ln_stats(fw, xt[i][:, :], st, x1t[i][:, :])
        V.tensor_scalar(x1t[i][:, :], xt[i][:, :], st[:, 0:1], st[:, 1:2],
                        ALU.subtract, ALU.mult)
        V.tensor_tensor(yt[i][:, :], yt[i][:, :], x1t[i][:, :], ALU.add)
        V.tensor_tensor(outp[i][:, 0:D], outp[i][:, 0:D], bt[:, 0:D], ALU.add)
        ln_stats(fw, yt[i][:, :], st, x1t[i][:, :])
        V.tensor_scalar(yt[i][:, :], yt[i][:, :], st[:, 0:1], st[:, 1:2],
                        ALU.subtract, ALU.mult)
        V.tensor_tensor(outp[i][:, 0:D], outp[i][:, 0:D], bt[:, D:2 * D], ALU.mult)
        V.tensor_tensor(outp[i][:, 0:D], outp[i][:, 0:D], bt[:, 2 * D:3 * D], ALU.add)
        V.tensor_tensor(x1t[i][:, :], xt[i][:, :], yt[i][:, :], ALU.add)
        tok_xt_free[i] = V.token()
        S.wait(V)
        S.dma_start(out=x1_d[c * 128:(c + 1) * 128, :], in_=x1t[i][:, :])
        dtok_store[i] = S.dtoken()

    S.barrier_all()
    fw.build()
    return nc


def run_l0_post(inp, shards, o_tok_list):
    nc = build_l0_post()
    woT = np.ascontiguousarray(np.asarray(inp["L0_attn_Wo"]).T)
    maps = []
    for c in range(NCORES):
        maps.append(dict(
            x=shards[c],
            o_fm=np.ascontiguousarray(o_tok_list[c].T),
            woT=woT,
            bo=np.ascontiguousarray(np.asarray(inp["L0_attn_bo"]).reshape(1, D)),
            lnw=np.ascontiguousarray(np.asarray(inp["L0_attn_ln_w"]).reshape(1, D)),
            lnb=np.ascontiguousarray(np.asarray(inp["L0_attn_ln_b"]).reshape(1, D)),
        ))
    return run_bass_kernel_spmd(nc, maps, core_ids=list(range(NCORES)))




def build_l0_post_ffn():
    nc = bass.Bass("TRN2", target_bir_lowering=False)
    fw = Fw(nc)
    V, S, T, G, SY = fw.V, fw.S, fw.T, fw.G, fw.SY

    x_d = nc.dram_tensor("x", [TC, D], F32, kind="ExternalInput")
    of_d = nc.dram_tensor("o_fm", [D, TC], BF16, kind="ExternalInput")
    wo_d = nc.dram_tensor("woT", [D, D], F32, kind="ExternalInput")
    bo_d = nc.dram_tensor("bo", [1, D], F32, kind="ExternalInput")
    lnw_d = nc.dram_tensor("lnw", [1, D], F32, kind="ExternalInput")
    lnb_d = nc.dram_tensor("lnb", [1, D], F32, kind="ExternalInput")
    wr_d = nc.dram_tensor("wr", [D, 8], F32, kind="ExternalInput")
    wiT_d = nc.dram_tensor("wiT", [8, D, D], F32, kind="ExternalInput")
    wod_d = nc.dram_tensor("wo", [8, D, D], F32, kind="ExternalInput")
    x2_d = nc.dram_tensor("x2", [TC, D], F32, kind="ExternalOutput")
    we_d = nc.dram_tensor("we_scr", [8, D, D], BF16)

    xnT = fw.sb("xnT", [128, 8 * TC], BF16)
    woz = fw.sb("woz", [128, 8 * D], BF16)
    ofm_c = [fw.sb(f"ofm_c{i}", [128, 8 * 128], BF16) for i in range(2)]
    brow = fw.sb("brow", [1, D], F32)
    bt = fw.sb("bt", [128, 3 * D], F32)
    onesfr = fw.sb("onesfr", [1, 128], F32)
    wes = [fw.sb(f"wes{i}", [128, 8 * D], BF16) for i in range(2)]
    wr = fw.sb("wr_sb", [128, 64], F32)
    gates = fw.sb("gates", [128, 8 * NCHUNK], F32)
    xt = [fw.sb(f"xt{i}", [128, D], F32) for i in range(2)]
    xsc = fw.sb("xsc", [128, D], F32)
    xf32 = fw.sb("xf32", [128, D], F32)
    st0 = fw.sb("st0", [128, 16], F32)
    prow = fw.sb("prow", [128, 24], F32)
    outp = [fw.sb(f"outp{i}", [128, 4 * D], F32) for i in range(2)]

    ps_tr = [fw.psbank(f"ps_tr{i}") for i in range(2)]
    ps_r = fw.psbank("ps_r")
    ps_1 = [fw.psbank(f"ps_1{i}") for i in range(2)]
    ps_2 = [fw.psbank(f"ps_2{i}") for i in range(2)]

    fw.init_consts()

    G.dma_start(out=woz[:, :], in_=bass.AP(wo_d, 0, [[D, 128], [128 * D, 8], [1, D]]))

    def ofm_ap(c):
        return bass.AP(of_d, c * 128, [[TC, 128], [128 * TC, 8], [1, 128]])
    SY.dma_start(out=wr[:, :], in_=bass.AP(wr_d, 0, [[8, 128], [128 * 8, 8], [1, 8]]))
    ld0 = SY.dtoken()
    SY.dma_start(out=xt[0][:, :], in_=x_d[0:128, :])
    SY.dma_start(out=ofm_c[0][:, :], in_=ofm_ap(0))
    T.wait(SY)
    V.wait(SY)
    V.wait(G)
    T.wait(G)
    V.memset(onesfr[:, :], 1.0)
    T.wait(V)
    vtb0 = 0
    for bi, bh in enumerate((bo_d, lnw_d, lnb_d)):
        SY.wait_tok(T, T.token())
        SY.dma_start(out=brow[0:1, 0:D], in_=bh[:, :])
        bdt = SY.dtoken()
        T.wait_dtok(SY, bdt)
        for hf in range(2):
            T.wait_tok(V, vtb0)
            T.matmul(ps_r[:, 0:512], onesfr[:, :], brow[0:1, hf * 512:hf * 512 + 512],
                     start=True, stop=True, inc=True)
            V.wait(T)
            V.tensor_copy(bt[:, bi * D + hf * 512: bi * D + hf * 512 + 512], ps_r[:, 0:512])
            vtb0 = V.token()

    wpat = lambda h, e: bass.AP(h, e * D * D, [[D, 128], [128 * D, 8], [1, D]])
    vtok_we = [0, 0]
    gdt_we = 0
    dtw = [0, 0]
    for e in range(8):
        i = e % 2
        G.wait_tok(T, dtw[0])
        G.wait_tok(T, dtw[1])
        G.dma_start(out=wes[0][:, :], in_=wpat(wiT_d, e))
        G.dma_start(out=wes[1][:, :], in_=wpat(wod_d, e))
        wld = G.dtoken()
        T.wait_dtok(G, wld)
        lw = wes[0]
        rw = wes[1]
        for dc in range(8):
            for half in range(2):
                pb = ps_1[(dc * 2 + half) % 2]
                T.wait_tok(V, vtok_we[(dc * 2 + half) % 2])
                for kf in range(8):
                    T.matmul(pb[:, :], lw[:, kf * D + dc * 128: kf * D + dc * 128 + 128],
                             rw[:, kf * D + half * 512: kf * D + half * 512 + 512],
                             start=(kf == 0), stop=(kf == 7), inc=(kf == 7))
                V.wait(T)
                V.wait_dtok(G, gdt_we)
                V.tensor_copy(outp[0].bitcast(BF16)[:, half * 512:half * 512 + 512], pb[:, :])
                vtok_we[(dc * 2 + half) % 2] = V.token()
            G.wait(V)
            G.dma_start(out=bass.AP(we_d, e * D * D + dc * 128 * D, [[D, 128], [1, D]]),
                        in_=outp[0].bitcast(BF16)[:, 0:D])
            gdt_we = G.dtoken()
        dtw[i] = T.token()

    V.wait_dtok(G, gdt_we)
    vtok_tr = [0, 0]
    vtok_r = vtb0
    vtok_o = vtb0
    ttok_chunk = 0
    ttok_ofm = [0, 0]
    tok_xt_free = [0, 0]
    gdt_x1 = [0, 0]
    for c in range(NCHUNK):
        i = c % 2
        V.wait_dtok(SY, ld0 + 32 * (c + 1))
        if c + 1 < NCHUNK:
            j = (c + 1) % 2
            SY.wait_tok(V, tok_xt_free[j])
            SY.wait_dtok(G, gdt_x1[j])
            SY.wait_tok(T, ttok_ofm[j])
            SY.dma_start(out=xt[j][:, :], in_=x_d[(c + 1) * 128:(c + 2) * 128, :])
            SY.dma_start(out=ofm_c[j][:, :], in_=ofm_ap(c + 1))
        S.wait_tok(T, ttok_chunk)
        V.wait_tok(T, ttok_chunk)
        T.wait_dtok(SY, ld0 + 32 * (c + 1))
        for half in range(2):
            T.wait(V)
            T.wait_tok(V, vtok_o)
            for ke in range(8):
                T.matmul(ps_1[0][:, :], ofm_c[i][:, ke * 128:(ke + 1) * 128],
                         woz[:, ke * D + half * 512: ke * D + half * 512 + 512],
                         start=(ke == 0), stop=(ke == 7), inc=(ke == 7))
            V.wait(T)
            V.tensor_copy(outp[i][:, half * 512:half * 512 + 512], ps_1[0][:, :])
            vtok_o = V.token()
        ln_stats(fw, xt[i][:, :], st0, xsc[:, :])
        V.tensor_scalar(xsc[:, :], xt[i][:, :], st0[:, 0:1], st0[:, 1:2],
                        ALU.subtract, ALU.mult)
        V.tensor_tensor(outp[i][:, 0:D], outp[i][:, 0:D], xsc[:, :], ALU.add)
        V.tensor_tensor(outp[i][:, 0:D], outp[i][:, 0:D], bt[:, 0:D], ALU.add)
        ln_stats(fw, outp[i][:, 0:D], st0, xsc[:, :])
        V.tensor_scalar(outp[i][:, 0:D], outp[i][:, 0:D], st0[:, 0:1], st0[:, 1:2],
                        ALU.subtract, ALU.mult)
        V.tensor_tensor(outp[i][:, 0:D], outp[i][:, 0:D], bt[:, D:2 * D], ALU.mult)
        V.tensor_tensor(outp[i][:, 0:D], outp[i][:, 0:D], bt[:, 2 * D:3 * D], ALU.add)
        V.tensor_tensor(xt[i][:, :], xt[i][:, :], outp[i][:, 0:D], ALU.add)
        V.wait_dtok(G, gdt_x1[i])
        G.wait(V)
        G.dma_start(out=x2_d[c * 128:(c + 1) * 128, :], in_=xt[i][:, :])
        gdt_x1[i] = G.dtoken()
        ln_stats(fw, xt[i][:, :], st0, xsc[:, :])
        V.tensor_scalar(xsc[:, :], xt[i][:, :], st0[:, 0:1], st0[:, 1:2],
                        ALU.subtract, ALU.mult)
        tok_xt_free[i] = V.token()
        T.wait(V)
        for kd in range(8):
            pb = ps_tr[kd % 2]
            T.wait_tok(V, vtok_tr[kd % 2])
            T.transpose(pb[:, 0:128], xsc[:, kd * 128:(kd + 1) * 128], fw.ident, inc=True)
            V.wait(T)
            V.tensor_copy(xnT[:, kd * TC + c * 128: kd * TC + (c + 1) * 128], pb[:, 0:128])
            V.tensor_copy(xf32[:, kd * 128:(kd + 1) * 128], pb[:, 0:128])
            vtok_tr[kd % 2] = V.token()
        ttok_chunk = T.token()
        ttok_ofm[i] = T.token()
        T.wait(V)
        T.wait_tok(V, vtok_r)
        for kd in range(8):
            T.matmul(ps_r[:, 0:8], xf32[:, kd * 128:(kd + 1) * 128],
                     wr[:, kd * 8:(kd + 1) * 8],
                     start=(kd == 0), stop=(kd == 7), inc=(kd == 7))
        V.wait(T)
        V.reduce_max(prow[:, 16:17], ps_r[:, 0:8], axis=AX.X)
        V.tensor_scalar(prow[:, 17:18], prow[:, 16:17], -1.0, None, ALU.mult)
        S.wait(V)
        S.activation(prow[:, 0:8], ps_r[:, 0:8], AF.Exp, bias=prow[:, 17:18])
        V.wait(S)
        V.reduce_sum(prow[:, 18:19], prow[:, 0:8], axis=AX.X)
        vtok_r = V.token()
        V.reciprocal(prow[:, 19:20], prow[:, 18:19])
        V.tensor_scalar(prow[:, 0:8], prow[:, 0:8], prow[:, 19:20], None, ALU.mult)
        V.max(out=prow[:, 8:16], in_=prow[:, 0:8])
        V.tensor_scalar(prow[:, 20:21], prow[:, 9:10], 0.0, None, ALU.bypass)
        V.tensor_scalar(gates[:, c * 8:(c + 1) * 8], prow[:, 0:8], prow[:, 20:21],
                        None, ALU.is_ge)
        V.tensor_tensor(gates[:, c * 8:(c + 1) * 8], gates[:, c * 8:(c + 1) * 8],
                        prow[:, 0:8], ALU.mult)

    T.wait(V)
    wepat = lambda e: bass.AP(we_d, e * D * D, [[D, 128], [128 * D, 8], [1, D]])
    gdtok_out = [0, 0]
    vtok_s2 = [0, 0]
    ttok_we = [0, 0]
    wedt = [0, 0]
    G.wait(V)
    G.dma_start(out=wes[0][:, :], in_=wepat(0))
    wedt[0] = G.dtoken()
    for e in range(8):
        i = e % 2
        if e + 1 < 8:
            G.wait_tok(T, ttok_we[(e + 1) % 2])
            G.dma_start(out=wes[(e + 1) % 2][:, :], in_=wepat(e + 1))
            wedt[(e + 1) % 2] = G.dtoken()
        T.wait_dtok(G, wedt[i])
        for tile in range(8):
            ob = outp[tile % 2]
            V.wait_dtok(G, gdtok_out[tile % 2])
            for tsub in range(4):
                for half in range(2):
                    k2i = (tsub * 2 + half) % 2
                    pb = ps_2[k2i]
                    T.wait_tok(V, vtok_s2[k2i])
                    for kd in range(8):
                        T.matmul(pb[:, :],
                                 xnT[:, kd * TC + tile * 512 + tsub * 128: kd * TC + tile * 512 + tsub * 128 + 128],
                                 wes[i][:, kd * D + half * 512: kd * D + half * 512 + 512],
                                 start=(kd == 0), stop=(kd == 7), inc=(kd == 7))
                    V.wait(T)
                    V.tensor_scalar(ob[:, tsub * D + half * 512: tsub * D + half * 512 + 512],
                                    pb[:, :],
                                    gates[:, (tile * 4 + tsub) * 8 + e:(tile * 4 + tsub) * 8 + e + 1],
                                    None, ALU.mult)
                    vtok_s2[k2i] = V.token()
            ttok_we[i] = T.token()
            G.wait(V)
            G.dma_start(
                out=bass.AP(x2_d, tile * 512 * D, [[D, 128], [128 * D, 4], [512, 2], [1, 512]]),
                in_=bass.AP(ob, 0, [[4 * D, 128], [D, 4], [512, 2], [1, 512]]),
                accum_op=ALU.add)
            gdtok_out[tile % 2] = G.dtoken()

    G.barrier_all()
    S.barrier_all()
    fw.build()
    return nc


def run_l0_post_ffn(inp, shards, o_tok_list):
    nc = build_l0_post_ffn()
    woT = np.ascontiguousarray(np.asarray(inp["L0_attn_Wo"]).T)
    maps = []
    for c in range(NCORES):
        maps.append(dict(
            x=shards[c],
            o_fm=np.ascontiguousarray(o_tok_list[c].T),
            woT=woT,
            bo=np.ascontiguousarray(np.asarray(inp["L0_attn_bo"]).reshape(1, D)),
            lnw=np.ascontiguousarray(np.asarray(inp["L0_attn_ln_w"]).reshape(1, D)),
            lnb=np.ascontiguousarray(np.asarray(inp["L0_attn_ln_b"]).reshape(1, D)),
            wr=np.ascontiguousarray(np.asarray(inp["L0_ffn_Wr"])),
            wiT=np.ascontiguousarray(np.asarray(inp["L0_ffn_Wi"]).transpose(0, 2, 1)),
            wo=np.ascontiguousarray(np.asarray(inp["L0_ffn_Wo"])),
        ))
    return run_bass_kernel_spmd(nc, maps, core_ids=list(range(NCORES)))




def build_l1_abc():
    nc = bass.Bass("TRN2", target_bir_lowering=False)
    fw = Fw(nc)
    V, S, T, G, SY = fw.V, fw.S, fw.T, fw.G, fw.SY

    x2_d = nc.dram_tensor("x2", [TC, D], F32, kind="ExternalInput")
    wa_d = nc.dram_tensor("wa", [D, D], F32, kind="ExternalInput")
    wb_d = nc.dram_tensor("wb", [D, D], F32, kind="ExternalInput")
    wc_d = nc.dram_tensor("wc", [D, D], F32, kind="ExternalInput")
    g1_d = nc.dram_tensor("g1", [1, D], F32, kind="ExternalInput")
    b1_d = nc.dram_tensor("b1", [1, D], F32, kind="ExternalInput")
    ba_d = nc.dram_tensor("ba", [1, D], F32, kind="ExternalInput")
    bb_d = nc.dram_tensor("bb", [1, D], F32, kind="ExternalInput")
    bc_d = nc.dram_tensor("bc", [1, D], F32, kind="ExternalInput")
    xin_d = nc.dram_tensor("xin", [TC, D], F32, kind="ExternalOutput")
    a_d = nc.dram_tensor("a_fm", [D, TC], F32, kind="ExternalOutput")
    u_d = nc.dram_tensor("u_fm", [D, TC], F32, kind="ExternalOutput")
    c_d = nc.dram_tensor("c_fm", [D, TC], F32, kind="ExternalOutput")

    wa = fw.sb("wa_sb", [128, 8 * D], BF16)
    wb = fw.sb("wb_sb", [128, 8 * D], BF16)
    wc = fw.sb("wc_sb", [128, 8 * D], BF16)
    h1T = fw.sb("h1T", [128, 8 * TC], BF16)
    vcols = fw.sb("vcols", [128, 5 * 8], F32)
    NT = 2
    xt = [fw.sb(f"xt{i}", [128, D], F32) for i in range(NT)]
    xinb = [fw.sb(f"xinb{i}", [128, D], F32) for i in range(NT)]
    zsc = fw.sb("zsc", [128, D], F32)
    st_ = [fw.sb(f"st{i}", [128, 16], F32) for i in range(NT)]
    a_sb = [fw.sb(f"a_sb{i}", [128, 512], F32) for i in range(NT)]
    b_sb = [fw.sb(f"b_sb{i}", [128, 512], F32) for i in range(NT)]
    u_sb = [fw.sb(f"u_sb{i}", [128, 512], F32) for i in range(NT)]
    c_sb = [fw.sb(f"c_sb{i}", [128, 512], F32) for i in range(NT)]
    ps_tr = [fw.psbank(f"ps_tr{i}") for i in range(2)]
    ps_a = fw.psbank("ps_a")
    ps_bb = fw.psbank("ps_bb")
    ps_c = fw.psbank("ps_c")

    fw.init_consts()

    G.dma_start(out=wa[:, :], in_=bass.AP(wa_d, 0, [[D, 128], [128 * D, 8], [1, D]]))
    G.dma_start(out=wb[:, :], in_=bass.AP(wb_d, 0, [[D, 128], [128 * D, 8], [1, D]]))
    G.dma_start(out=wc[:, :], in_=bass.AP(wc_d, 0, [[D, 128], [128 * D, 8], [1, D]]))
    for k, h in enumerate((g1_d, b1_d, ba_d, bb_d, bc_d)):
        SY.dma_start(out=vcols[:, k * 8:(k + 1) * 8],
                     in_=bass.AP(h, 0, [[1, 128], [128, 8]]),
                     allow_slow_non_contiguous=True)
    ld0 = SY.dtoken()
    SY.dma_start(out=xt[0][:, :], in_=x2_d[0:128, :])
    V.wait(SY)
    V.wait(G)
    T.wait(SY)
    T.wait(G)
    S.wait(SY)

    vtok_tr = [0, 0]
    ttok_chunk = 0
    tok_xt_free = [0, 0]
    dtok_xin = [0, 0]
    for c in range(NCHUNK):
        i = c % 2
        st = st_[i]
        V.wait_dtok(SY, ld0 + 16 * (c + 1))
        if c + 1 < NCHUNK:
            j = (c + 1) % 2
            SY.wait_tok(V, tok_xt_free[j])
            SY.dma_start(out=xt[j][:, :], in_=x2_d[(c + 1) * 128:(c + 2) * 128, :])
        S.wait_tok(T, ttok_chunk)
        V.wait_tok(T, ttok_chunk)
        V.wait_dtok(S, dtok_xin[i])
        ln_stats(fw, xt[i][:, :], st, zsc[:, :])
        V.tensor_scalar(xinb[i][:, :], xt[i][:, :], st[:, 0:1], st[:, 1:2],
                        ALU.subtract, ALU.mult)
        tok_xt_free[i] = V.token()
        S.wait(V)
        S.dma_start(out=xin_d[c * 128:(c + 1) * 128, :], in_=xinb[i][:, :])
        dtok_xin[i] = S.dtoken()
        ln_stats(fw, xinb[i][:, :], st, zsc[:, :])
        V.tensor_scalar(zsc[:, :], xinb[i][:, :], st[:, 0:1], st[:, 1:2],
                        ALU.subtract, ALU.mult)
        T.wait(V)
        for kd in range(8):
            pb = ps_tr[kd % 2]
            T.wait_tok(V, vtok_tr[kd % 2])
            T.transpose(pb[:, 0:128], zsc[:, kd * 128:(kd + 1) * 128], fw.ident, inc=True)
            V.wait(T)
            V.tensor_scalar(h1T[:, kd * TC + c * 128: kd * TC + (c + 1) * 128],
                            pb[:, 0:128], vcols[:, kd:kd + 1], vcols[:, 8 + kd:9 + kd],
                            ALU.mult, ALU.add)
            vtok_tr[kd % 2] = V.token()
        ttok_chunk = T.token()

    T.wait(V)
    vtok_e = {"a": 0, "b": 0, "c": 0}
    stok_e = {"a": 0, "b": 0, "c": 0}
    dtok_sb = [0, 0]
    for tile in range(8):
        for ec in range(8):
            i = (tile * 8 + ec) % 2
            V.wait_dtok(G, dtok_sb[i])
            S.wait_dtok(G, dtok_sb[i])
            for nm, wsb, psb in (("a", wa, ps_a), ("b", wb, ps_bb), ("c", wc, ps_c)):
                T.wait_tok(S, stok_e[nm])
                T.wait_tok(V, vtok_e[nm])
                for kd in range(8):
                    T.matmul(psb[:, :], wsb[:, kd * D + ec * 128: kd * D + ec * 128 + 128],
                             h1T[:, kd * TC + tile * 512: kd * TC + (tile + 1) * 512],
                             start=(kd == 0), stop=(kd == 7), inc=(kd == 7))
            S.wait(T)
            S.activation(a_sb[i][:, :], ps_a[:, :], AF.Sigmoid, bias=vcols[:, 16 + ec:17 + ec])
            stok_e["a"] = S.token()
            S.activation(b_sb[i][:, :], ps_bb[:, :], AF.Tanh, bias=vcols[:, 24 + ec:25 + ec])
            stok_e["b"] = S.token()
            S.activation(c_sb[i][:, :], ps_c[:, :], AF.Sigmoid, bias=vcols[:, 32 + ec:33 + ec])
            stok_e["c"] = S.token()
            V.wait(S)
            V.tensor_tensor(u_sb[i][:, :], a_sb[i][:, :], b_sb[i][:, :], ALU.mult)
            V.tensor_sub(u_sb[i][:, :], b_sb[i][:, :], u_sb[i][:, :])
            vtok_e["a"] = V.token()
            vtok_e["b"] = V.token()
            vtok_e["c"] = V.token()
            G.wait(V)
            G.wait(S)
            G.dma_start(out=bass.AP(a_d, ec * 128 * TC + tile * 512, [[TC, 128], [1, 512]]),
                        in_=a_sb[i][:, :])
            G.dma_start(out=bass.AP(u_d, ec * 128 * TC + tile * 512, [[TC, 128], [1, 512]]),
                        in_=u_sb[i][:, :])
            G.dma_start(out=bass.AP(c_d, ec * 128 * TC + tile * 512, [[TC, 128], [1, 512]]),
                        in_=c_sb[i][:, :])
            dtok_sb[i] = G.dtoken()

    G.barrier_all()
    S.barrier_all()
    fw.build()
    return nc


def run_l1_abc(inp, x2_list):
    nc = build_l1_abc()
    maps = []
    for c in range(NCORES):
        maps.append(dict(
            x2=x2_list[c],
            wa=np.ascontiguousarray(np.asarray(inp["L1_attn_Wa"])),
            wb=np.ascontiguousarray(np.asarray(inp["L1_attn_Wb"])),
            wc=np.ascontiguousarray(np.asarray(inp["L1_attn_Wc"])),
            g1=np.ascontiguousarray(np.asarray(inp["L1_attn_g1"]).reshape(1, D)),
            b1=np.ascontiguousarray(np.asarray(inp["L1_attn_b1"]).reshape(1, D)),
            ba=np.ascontiguousarray(np.asarray(inp["L1_attn_ba"]).reshape(1, D)),
            bb=np.ascontiguousarray(np.asarray(inp["L1_attn_bb"]).reshape(1, D)),
            bc=np.ascontiguousarray(np.asarray(inp["L1_attn_bc"]).reshape(1, D)),
        ))
    return run_bass_kernel_spmd(nc, maps, core_ids=list(range(NCORES)))


def build_l1_scan():
    nc = bass.Bass("TRN2", target_bir_lowering=False)
    fw = Fw(nc)
    V, S, SY = fw.V, fw.S, fw.SY
    SEQ = 2 * TC

    a_d = nc.dram_tensor("a_sh", [512, SEQ], F32, kind="ExternalInput")
    u_d = nc.dram_tensor("u_sh", [512, SEQ], F32, kind="ExternalInput")
    c_d = nc.dram_tensor("c_sh", [512, SEQ], F32, kind="ExternalInput")
    y_d = nc.dram_tensor("y_sh", [512, SEQ], F32, kind="ExternalOutput")

    at = fw.sb("at", [128, SEQ], F32)
    ut = fw.sb("ut", [128, SEQ], F32)
    ct = fw.sb("ct", [128, SEQ], F32)
    yt = fw.sb("yt", [128, SEQ], F32)

    dtok_y = 0
    vtok_g = 0
    for g in range(4):
        SY.wait_tok(V, vtok_g)
        SY.dma_start(out=at[:, :], in_=a_d[g * 128:(g + 1) * 128, :])
        SY.dma_start(out=ut[:, :], in_=u_d[g * 128:(g + 1) * 128, :])
        SY.dma_start(out=ct[:, :], in_=c_d[g * 128:(g + 1) * 128, :])
        V.wait_dtok(SY, SY.dtoken())
        V.wait_dtok(S, dtok_y)
        V.tensor_tensor_scan(yt[:, :], at[:, :], ut[:, :], 0.0, ALU.mult, ALU.add)
        V.tensor_tensor(yt[:, :], yt[:, :], ct[:, :], ALU.mult)
        vtok_g = V.token()
        S.wait(V)
        S.dma_start(out=y_d[g * 128:(g + 1) * 128, :], in_=yt[:, :])
        dtok_y = S.dtoken()
    S.barrier_all()
    fw.build()
    return nc


def host_regroup_scan(res5):
    full = {}
    for nm in ("a_fm", "u_fm", "c_fm"):
        arr = np.empty((4, D, 2 * TC), np.float32)
        for c in range(NCORES):
            b, h = c // 2, c % 2
            arr[b, :, h * TC:(h + 1) * TC] = res5.results[c][nm]
        full[nm] = arr
    maps = []
    for m in range(NCORES):
        b, half4 = m // 2, m % 2
        sl = slice(half4 * 512, (half4 + 1) * 512)
        maps.append(dict(a_sh=np.ascontiguousarray(full["a_fm"][b, sl]),
                         u_sh=np.ascontiguousarray(full["u_fm"][b, sl]),
                         c_sh=np.ascontiguousarray(full["c_fm"][b, sl])))
    return maps


def run_l1_scan(res5):
    nc = build_l1_scan()
    return run_bass_kernel_spmd(nc, host_regroup_scan(res5), core_ids=list(range(NCORES)))


def host_regroup_y(res6):
    yfull = np.empty((4, D, 2 * TC), np.float32)
    for m in range(NCORES):
        b, half4 = m // 2, m % 2
        yfull[b, half4 * 512:(half4 + 1) * 512, :] = res6.results[m]["y_sh"]
    return [np.ascontiguousarray(yfull[c // 2, :, (c % 2) * TC:(c % 2 + 1) * TC].T)
            for c in range(NCORES)]




def build_l1_post():
    nc = bass.Bass("TRN2", target_bir_lowering=False)
    fw = Fw(nc)
    V, S, T, G, SY = fw.V, fw.S, fw.T, fw.G, fw.SY
    F = 4 * D

    x2_d = nc.dram_tensor("x2", [TC, D], F32, kind="ExternalInput")
    xin_d = nc.dram_tensor("xin", [TC, D], F32, kind="ExternalInput")
    ys_d = nc.dram_tensor("ys", [TC, D], F32, kind="ExternalInput")
    w1_d = nc.dram_tensor("w1", [D, F], F32, kind="ExternalInput")
    w2_d = nc.dram_tensor("w2", [F, D], F32, kind="ExternalInput")
    g2_d = nc.dram_tensor("g2", [1, D], F32, kind="ExternalInput")
    b2_d = nc.dram_tensor("b2", [1, D], F32, kind="ExternalInput")
    bm1_d = nc.dram_tensor("bm1", [1, F], F32, kind="ExternalInput")
    bm2_d = nc.dram_tensor("bm2", [1, D], F32, kind="ExternalInput")
    wg_d = nc.dram_tensor("wg", [D, 8], F32, kind="ExternalInput")
    x3_d = nc.dram_tensor("x3", [TC, D], F32, kind="ExternalOutput")
    xn3_d = nc.dram_tensor("xn3", [TC, D], BF16, kind="ExternalOutput")
    pr_d = nc.dram_tensor("probs", [TC, 8], F32, kind="ExternalOutput")
    h2T_d = nc.dram_tensor("h2T_scr", [128, 8 * TC], BF16)

    w1h = fw.sb("w1h", [128, 8 * 2048], BF16)
    w2h = fw.sb("w2h", [128, 16 * D], BF16)
    wg = fw.sb("wg_sb", [128, 64], F32)
    h2T_t = fw.sb("h2T_t", [128, 8 * 512], BF16)
    mid = fw.sb("mid", [128, 16 * 512], BF16)
    acc = fw.sb("acc", [128, 4 * D], F32)
    vcols = fw.sb("vcols", [128, 16], F32)
    bm1c = fw.sb("bm1c", [128, 32], F32)
    brow = fw.sb("brow", [1, D], F32)
    bm2t = fw.sb("bm2t", [128, D], F32)
    NT = 2
    xt = [fw.sb(f"xt{i}", [128, D], F32) for i in range(NT)]
    x2t = [fw.sb(f"x2t{i}", [128, D], F32) for i in range(NT)]
    yt = [fw.sb(f"yt{i}", [128, D], F32) for i in range(NT)]
    zsc = fw.sb("zsc", [128, D], F32)
    zb = fw.sb("zb", [128, D], BF16)
    st_ = [fw.sb(f"st{i}", [128, 16], F32) for i in range(NT)]
    prow = fw.sb("prow", [128, 24], F32)
    x3t = [fw.sb(f"x3t{i}", [128, D], F32) for i in range(NT)]
    ps_tr = [fw.psbank(f"ps_tr{i}") for i in range(2)]
    ps_1 = [fw.psbank(f"ps_1{i}") for i in range(2)]
    ps_2 = [fw.psbank(f"ps_2{i}") for i in range(2)]
    ps_r = fw.psbank("ps_r")

    fw.init_consts()

    SY.dma_start(out=wg[:, :], in_=bass.AP(wg_d, 0, [[8, 128], [128 * 8, 8], [1, 8]]))
    SY.dma_start(out=vcols[:, 0:8], in_=bass.AP(g2_d, 0, [[1, 128], [128, 8]]),
                 allow_slow_non_contiguous=True)
    SY.dma_start(out=vcols[:, 8:16], in_=bass.AP(b2_d, 0, [[1, 128], [128, 8]]),
                 allow_slow_non_contiguous=True)
    SY.dma_start(out=bm1c[:, :], in_=bass.AP(bm1_d, 0, [[1, 128], [128, 32]]),
                 allow_slow_non_contiguous=True)
    SY.dma_start(out=brow[:, :], in_=bm2_d[:, :])
    V.wait(SY)
    T.wait(SY)
    S.wait(SY)
    onesf = fw.sb("onesf", [1, 128], F32)
    V.memset(onesf[:, :], 1.0)
    T.wait(V)
    vtb = 0
    for seg in range(2):
        T.wait_tok(V, vtb)
        T.matmul(ps_r[:, 0:512], onesf[:, :], brow[0:1, seg * 512:(seg + 1) * 512],
                 start=True, stop=True, inc=True)
        V.wait(T)
        V.tensor_copy(bm2t[:, seg * 512:(seg + 1) * 512], ps_r[:, 0:512])
        vtb = V.token()

    ld0 = SY.dtoken()
    SY.dma_start(out=xt[0][:, :], in_=xin_d[0:128, :])
    SY.dma_start(out=yt[0][:, :], in_=ys_d[0:128, :])
    SY.dma_start(out=x2t[0][:, :], in_=x2_d[0:128, :])
    vtok_tr = [0, 0]
    ttok_chunk = 0
    tok_free = [0, 0]
    dtok_h2 = [0, 0]
    dtok_zb = [0]
    for c in range(NCHUNK):
        i = c % 2
        st = st_[i]
        V.wait_dtok(SY, ld0 + 48 * (c + 1))
        if c + 1 < NCHUNK:
            j = (c + 1) % 2
            SY.wait_tok(V, tok_free[j])
            SY.wait_dtok(G, dtok_h2[j])
            SY.dma_start(out=xt[j][:, :], in_=xin_d[(c + 1) * 128:(c + 2) * 128, :])
            SY.dma_start(out=yt[j][:, :], in_=ys_d[(c + 1) * 128:(c + 2) * 128, :])
            SY.dma_start(out=x2t[j][:, :], in_=x2_d[(c + 1) * 128:(c + 2) * 128, :])
        S.wait_tok(T, ttok_chunk)
        V.wait_tok(T, ttok_chunk)
        V.wait_dtok(G, dtok_h2[i])
        V.wait_dtok(G, dtok_zb[0])
        V.tensor_tensor(xt[i][:, :], xt[i][:, :], yt[i][:, :], ALU.add)
        ln_stats(fw, xt[i][:, :], st, zsc[:, :])
        V.tensor_scalar(zsc[:, :], xt[i][:, :], st[:, 0:1], st[:, 1:2],
                        ALU.subtract, ALU.mult)
        V.tensor_tensor(x2t[i][:, :], x2t[i][:, :], xt[i][:, :], ALU.add)
        V.tensor_tensor(x2t[i][:, :], x2t[i][:, :], bm2t[:, :], ALU.add)
        tok_free[i] = V.token()
        T.wait(V)
        for kd in range(8):
            pb = ps_tr[kd % 2]
            T.wait_tok(V, vtok_tr[kd % 2])
            T.transpose(pb[:, 0:128], zsc[:, kd * 128:(kd + 1) * 128], fw.ident, inc=True)
            V.wait(T)
            V.tensor_scalar(zb[:, kd * 128:(kd + 1) * 128],
                            pb[:, 0:128], vcols[:, kd:kd + 1], vcols[:, 8 + kd:9 + kd],
                            ALU.mult, ALU.add)
            vtok_tr[kd % 2] = V.token()
        ttok_chunk = T.token()
        G.wait(V)
        G.dma_start(out=bass.AP(h2T_d, c * 128, [[8 * TC, 128], [TC, 8], [1, 128]]),
                    in_=bass.AP(zb, 0, [[8 * 128, 128], [128, 8], [1, 128]]))
        dtok_zb[0] = G.dtoken()
        G.dma_start(out=x3_d[c * 128:(c + 1) * 128, :], in_=x2t[i][:, :])
        dtok_h2[i] = G.dtoken()

    T.wait(V)
    gdt_phase1 = G.nd_
    vtok_s1 = [0, 0]
    vtok_s2 = [0, 0]
    ttok_w = 0
    gdtok_acc = [0, 0]
    dtok_h2t = [0, 0]
    ttok_h2t = [0, 0]
    h2T_t2 = [h2T_t, fw.sb("h2T_t2", [128, 8 * 512], BF16)]
    acc2 = [acc, fw.sb("acc2", [128, 4 * D], F32)]
    for fh in range(2):
        G.wait_tok(T, ttok_w)
        G.dma_start(out=w1h[:, :],
                    in_=bass.AP(w1_d, fh * 2048, [[F, 128], [128 * F, 8], [1, 2048]]))
        G.dma_start(out=w2h[:, :],
                    in_=bass.AP(w2_d, fh * 2048 * D, [[D, 128], [128 * D, 16], [1, D]]))
        wdt = G.dtoken()
        T.wait_dtok(G, wdt)
        for tile in range(8):
            ib = tile % 2
            ht = h2T_t2[ib]
            ob = acc2[ib]
            SY.wait_tok(T, ttok_h2t[ib])
            SY.wait_dtok(G, gdt_phase1)
            SY.dma_start(out=ht[:, :],
                         in_=bass.AP(h2T_d, tile * 512, [[8 * TC, 128], [TC, 8], [1, 512]]))
            hdt = SY.dtoken()
            T.wait_dtok(SY, hdt)
            for fc in range(16):
                pb = ps_1[fc % 2]
                T.wait_tok(S, vtok_s1[fc % 2])
                for kd in range(8):
                    T.matmul(pb[:, :], w1h[:, kd * 2048 + fc * 128: kd * 2048 + fc * 128 + 128],
                             ht[:, kd * 512:(kd + 1) * 512],
                             start=(kd == 0), stop=(kd == 7), inc=(kd == 7))
                S.wait(T)
                S.activation(mid[:, fc * 512:(fc + 1) * 512], pb[:, :], AF.Gelu,
                             bias=bm1c[:, fh * 16 + fc:fh * 16 + fc + 1])
                vtok_s1[fc % 2] = S.token()
            ttok_h2t[ib] = T.token()
            T.wait(S)
            V.wait_dtok(G, gdtok_acc[ib])
            for tsub in range(4):
                for half in range(2):
                    k2i = (tsub * 2 + half) % 2
                    pb = ps_2[k2i]
                    T.wait_tok(V, vtok_s2[k2i])
                    for fc in range(16):
                        T.matmul(pb[:, :], mid[:, fc * 512 + tsub * 128: fc * 512 + tsub * 128 + 128],
                                 w2h[:, fc * D + half * 512: fc * D + half * 512 + 512],
                                 start=(fc == 0), stop=(fc == 15), inc=(fc == 15))
                    V.wait(T)
                    V.tensor_copy(ob[:, tsub * D + half * 512: tsub * D + half * 512 + 512],
                                  pb[:, :])
                    vtok_s2[k2i] = V.token()
            ttok_w = T.token()
            G.wait(V)
            G.dma_start(
                out=bass.AP(x3_d, tile * 512 * D, [[D, 128], [128 * D, 4], [512, 2], [1, 512]]),
                in_=bass.AP(ob, 0, [[4 * D, 128], [D, 4], [512, 2], [1, 512]]),
                accum_op=ALU.add)
            gdtok_acc[ib] = G.dtoken()

    G.barrier_all()
    ld3 = SY.dtoken()
    SY.wait_dtok(G, G.nd_)
    SY.dma_start(out=xt[0][:, :], in_=x3_d[0:128, :])
    vtok_r = vtb
    vtok_tr = [0, 0]
    ttok_chunk2 = 0
    tok_free3 = [0, 0]
    dtok_st = [0, 0]
    for c in range(NCHUNK):
        i = c % 2
        st = st_[i]
        V.wait_dtok(SY, ld3 + 16 * (c + 1))
        if c + 1 < NCHUNK:
            j = (c + 1) % 2
            SY.wait_tok(V, tok_free3[j])
            SY.dma_start(out=xt[j][:, :], in_=x3_d[(c + 1) * 128:(c + 2) * 128, :])
        S.wait_tok(T, ttok_chunk2)
        V.wait_tok(T, ttok_chunk2)
        V.wait_dtok(S, dtok_st[i])
        ln_stats(fw, xt[i][:, :], st, zsc[:, :])
        V.tensor_scalar(zsc[:, :], xt[i][:, :], st[:, 0:1], st[:, 1:2],
                        ALU.subtract, ALU.mult)
        V.tensor_copy(zb[:, 0:D], zsc[:, :])
        tok_free3[i] = V.token()
        T.wait(V)
        for kd in range(8):
            pb = ps_tr[kd % 2]
            T.wait_tok(V, vtok_tr[kd % 2])
            T.transpose(pb[:, 0:128], zsc[:, kd * 128:(kd + 1) * 128], fw.ident, inc=True)
            V.wait(T)
            V.tensor_copy(x3t[i][:, kd * 128:(kd + 1) * 128], pb[:, 0:128])
            vtok_tr[kd % 2] = V.token()
        T.wait(V)
        T.wait_tok(V, vtok_r)
        for kd in range(8):
            T.matmul(ps_r[:, 0:8], x3t[i][:, kd * 128:(kd + 1) * 128],
                     wg[:, kd * 8:(kd + 1) * 8],
                     start=(kd == 0), stop=(kd == 7), inc=(kd == 7))
        ttok_chunk2 = T.token()
        V.wait(T)
        V.reduce_max(prow[:, 16:17], ps_r[:, 0:8], axis=AX.X)
        V.tensor_scalar(prow[:, 17:18], prow[:, 16:17], -1.0, None, ALU.mult)
        S.wait(V)
        S.activation(prow[:, 0:8], ps_r[:, 0:8], AF.Exp, bias=prow[:, 17:18])
        V.wait(S)
        V.reduce_sum(prow[:, 18:19], prow[:, 0:8], axis=AX.X)
        vtok_r = V.token()
        V.reciprocal(prow[:, 19:20], prow[:, 18:19])
        V.tensor_scalar(prow[:, 0:8], prow[:, 0:8], prow[:, 19:20], None, ALU.mult)
        S.wait(V)
        S.dma_start(out=xn3_d[c * 128:(c + 1) * 128, :], in_=zb[:, 0:D])
        S.dma_start(out=pr_d[c * 128:(c + 1) * 128, :], in_=prow[:, 0:8])
        dtok_st[i] = S.dtoken()

    S.barrier_all()
    fw.build()
    return nc


def run_l1_post(inp, x2_list, res5, ys_list):
    nc = build_l1_post()
    maps = []
    for c in range(NCORES):
        maps.append(dict(
            x2=x2_list[c],
            xin=res5.results[c]["xin"],
            ys=ys_list[c],
            w1=np.ascontiguousarray(np.asarray(inp["L1_attn_W1"])),
            w2=np.ascontiguousarray(np.asarray(inp["L1_attn_W2"])),
            g2=np.ascontiguousarray(np.asarray(inp["L1_attn_g2"]).reshape(1, D)),
            b2=np.ascontiguousarray(np.asarray(inp["L1_attn_b2"]).reshape(1, D)),
            bm1=np.ascontiguousarray(np.asarray(inp["L1_attn_bm1"]).reshape(1, 4 * D)),
            bm2=np.ascontiguousarray(np.asarray(inp["L1_attn_bm2"]).reshape(1, D)),
            wg=np.ascontiguousarray(np.asarray(inp["L1_ffn_Wg"])),
        ))
    return run_bass_kernel_spmd(nc, maps, core_ids=list(range(NCORES)))




def build_l1_route():
    nc = bass.Bass("TRN2", target_bir_lowering=False)
    fw = Fw(nc)
    V, S, T, G, SY = fw.V, fw.S, fw.T, fw.G, fw.SY
    CAP = 4096.0

    p_d = nc.dram_tensor("pT", [128, 256], F32, kind="ExternalInput")
    o_d = nc.dram_tensor("route", [1, 4], F32, kind="ExternalOutput")

    pt = fw.sb("pt", [128, 256], F32)
    gidx = fw.sb("gidx", [128, 256], F32)
    sc = fw.sb("sc", [128, 256], F32)
    sc2 = fw.sb("sc2", [128, 256], F32)
    rows = fw.sb("rows", [128, 4], F32)
    onesc = fw.sb("onesc", [128, 1], F32)
    onesr = fw.sb("onesr", [1, 128], F32)
    sm = fw.sb("sm", [1, 16], F32)
    ps_a = fw.psbank("ps_a")
    ps_b = fw.psbank("ps_b")

    fw.init_consts()
    G.iota(gidx[:, :], [[1, 256]], channel_multiplier=256,
           allow_small_or_imprecise_dtypes=True)
    SY.dma_start(out=pt[:, :], in_=p_d[:, :])
    V.wait(SY)
    V.wait(G)
    V.memset(onesc[:, :], 1.0)
    V.memset(onesr[:, :], 1.0)
    V.memset(sm[0:1, 0:1], 0.0)
    V.memset(sm[0:1, 1:2], 1.000001)
    T.wait(V)

    vtok_ps = [0, 0]

    def bcast_col(src11):
        nonlocal_ = 0
        T.wait(V)
        T.wait_tok(V, vtok_ps[0])
        T.matmul(ps_a[:, 0:1], onesr[:, :], src11, start=True, stop=True, inc=True)
        V.wait(T)

    def total_count(src_tile, out11):
        V.reduce_sum(rows[:, 0:1], src_tile, axis=AX.X)
        T.wait(V)
        T.wait_tok(V, vtok_ps[1])
        T.matmul(ps_b[0:1, 0:1], rows[:, 0:1], onesc[:, :], start=True, stop=True, inc=True)
        V.wait(T)
        V.tensor_copy(out11, ps_b[0:1, 0:1])
        vtok_ps[1] = V.token()

    def upd(lo_sl, hi_sl, th_sl, m_sl):
        V.tensor_tensor(sm[0:1, 5:6], th_sl, lo_sl, ALU.subtract)
        V.tensor_tensor(sm[0:1, 5:6], sm[0:1, 5:6], m_sl, ALU.mult)
        V.tensor_tensor(lo_sl, lo_sl, sm[0:1, 5:6], ALU.add)
        V.tensor_tensor(sm[0:1, 5:6], th_sl, hi_sl, ALU.subtract)
        V.tensor_tensor(sm[0:1, 11:12], sm[0:1, 5:6], m_sl, ALU.mult)
        V.tensor_tensor(sm[0:1, 5:6], sm[0:1, 5:6], sm[0:1, 11:12], ALU.subtract)
        V.tensor_tensor(hi_sl, hi_sl, sm[0:1, 5:6], ALU.add)

    for it in range(40):
        V.tensor_tensor(sm[0:1, 2:3], sm[0:1, 0:1], sm[0:1, 1:2], ALU.add)
        V.tensor_scalar(sm[0:1, 2:3], sm[0:1, 2:3], 0.5, None, ALU.mult)
        bcast_col(sm[0:1, 2:3])
        V.tensor_scalar(sc[:, :], pt[:, :], ps_a[:, 0:1], None, ALU.is_ge)
        vtok_ps[0] = V.token()
        total_count(sc[:, :], sm[0:1, 3:4])
        V.tensor_scalar(sm[0:1, 4:5], sm[0:1, 3:4], CAP, None, ALU.is_ge)
        upd(sm[0:1, 0:1], sm[0:1, 1:2], sm[0:1, 2:3], sm[0:1, 4:5])

    bcast_col(sm[0:1, 0:1])
    V.tensor_scalar(sc[:, :], pt[:, :], ps_a[:, 0:1], None, ALU.is_ge)
    vtok_ps[0] = V.token()
    V.tensor_tensor(sc2[:, :], pt[:, :], sc[:, :], ALU.mult)
    V.tensor_scalar(sc[:, :], sc[:, :], -1.0, 1.0, ALU.mult, ALU.add)
    V.tensor_scalar(sc[:, :], sc[:, :], 2.0, None, ALU.mult)
    V.tensor_tensor(sc2[:, :], sc2[:, :], sc[:, :], ALU.add)
    V.tensor_reduce(rows[:, 1:2], sc2[:, :], axis=AX.X, op=ALU.min)
    T.wait(V)
    T.transpose(ps_b[0:1, 0:128], rows[:, 1:2], fw.ident, inc=True)
    V.wait(T)
    V.tensor_reduce(sm[0:1, 6:7], ps_b[0:1, 0:128], axis=AX.X, op=ALU.min)
    vtok_ps[1] = V.token()
    bcast_col(sm[0:1, 6:7])
    V.tensor_scalar(sc[:, :], pt[:, :], ps_a[:, 0:1], None, ALU.is_gt)
    vtok_ps[0] = V.token()
    total_count(sc[:, :], sm[0:1, 10:11])
    V.tensor_scalar(sm[0:1, 7:8], sm[0:1, 10:11], -1.0, CAP, ALU.mult, ALU.add)
    bcast_col(sm[0:1, 6:7])
    V.tensor_scalar(sc[:, :], pt[:, :], ps_a[:, 0:1], None, ALU.is_equal)
    vtok_ps[0] = V.token()
    V.memset(sm[0:1, 8:9], -1.0)
    V.memset(sm[0:1, 9:10], 32768.0)
    for it in range(18):
        V.tensor_tensor(sm[0:1, 2:3], sm[0:1, 8:9], sm[0:1, 9:10], ALU.add)
        V.tensor_scalar(sm[0:1, 2:3], sm[0:1, 2:3], 0.5, None, ALU.mult)
        bcast_col(sm[0:1, 2:3])
        V.tensor_scalar(sc2[:, :], gidx[:, :], ps_a[:, 0:1], None, ALU.is_le)
        vtok_ps[0] = V.token()
        V.tensor_tensor(sc2[:, :], sc2[:, :], sc[:, :], ALU.mult)
        total_count(sc2[:, :], sm[0:1, 3:4])
        V.tensor_scalar(sm[0:1, 4:5], sm[0:1, 3:4], sm[0:1, 7:8], None, ALU.is_ge)
        V.tensor_tensor(sm[0:1, 5:6], sm[0:1, 2:3], sm[0:1, 9:10], ALU.subtract)
        V.tensor_tensor(sm[0:1, 5:6], sm[0:1, 5:6], sm[0:1, 4:5], ALU.mult)
        V.tensor_tensor(sm[0:1, 9:10], sm[0:1, 9:10], sm[0:1, 5:6], ALU.add)
        V.tensor_tensor(sm[0:1, 5:6], sm[0:1, 2:3], sm[0:1, 8:9], ALU.subtract)
        V.tensor_tensor(sm[0:1, 11:12], sm[0:1, 5:6], sm[0:1, 4:5], ALU.mult)
        V.tensor_tensor(sm[0:1, 5:6], sm[0:1, 5:6], sm[0:1, 11:12], ALU.subtract)
        V.tensor_tensor(sm[0:1, 8:9], sm[0:1, 8:9], sm[0:1, 5:6], ALU.add)

    V.tensor_copy(rows[0:1, 0:1], sm[0:1, 6:7])
    V.tensor_copy(rows[0:1, 1:2], sm[0:1, 9:10])
    V.tensor_copy(rows[0:1, 2:3], sm[0:1, 7:8])
    V.tensor_copy(rows[0:1, 3:4], sm[0:1, 10:11])
    S.wait(V)
    S.dma_start(out=o_d[:, :], in_=rows[0:1, 0:4])
    S.barrier_all()
    fw.build()
    return nc


def run_l1_route(res7):
    probs_full = np.concatenate([res7.results[c]["probs"] for c in range(NCORES)], 0)
    nc = build_l1_route()
    maps = [dict(pT=np.ascontiguousarray(probs_full[:, m].reshape(128, 256)))
            for m in range(NCORES)]
    return run_bass_kernel_spmd(nc, maps, core_ids=list(range(NCORES)))




def build_l1_ffn():
    nc = bass.Bass("TRN2", target_bir_lowering=False)
    fw = Fw(nc)
    V, S, T, G, SY = fw.V, fw.S, fw.T, fw.G, fw.SY
    F = 2048
    CAP = 1024
    NLIST = 8 * CAP + 256

    x3_d = nc.dram_tensor("x3", [TC, D], F32, kind="ExternalInput")
    xn3_d = nc.dram_tensor("xn3", [TC, D], BF16, kind="ExternalInput")
    pr_d = nc.dram_tensor("probs", [TC, 8], F32, kind="ExternalInput")
    vi_d = nc.dram_tensor("vitab", [2, 8], F32, kind="ExternalInput")
    base_d = nc.dram_tensor("idxbase", [1, 1], F32, kind="ExternalInput")
    w1_d = nc.dram_tensor("w1", [8, D, F], F32, kind="ExternalInput")
    w2_d = nc.dram_tensor("w2", [8, F, D], F32, kind="ExternalInput")
    x4_d = nc.dram_tensor("x4", [TC, D], F32, kind="ExternalOutput")
    il_d = nc.dram_tensor("idxlist", [NLIST, 1], I32)
    gl_d = nc.dram_tensor("gatelist", [NLIST, 1], F32)
    xnp_d = nc.dram_tensor("xn3pad", [TC + 128, D], BF16)
    x4a_d = nc.dram_tensor("x4acc", [TC + 128, D], F32)

    w1 = fw.sb("w1_sb", [128, 8 * F], BF16)
    w2 = fw.sb("w2_sb", [128, 16 * D], BF16)
    mid = fw.sb("mid", [128, 16 * 512], BF16)
    xg = [fw.sb(f"xg{i}", [128, 4 * D], BF16) for i in range(2)]
    xgT = [fw.sb(f"xgT{i}", [128, 8 * 512], BF16) for i in range(2)]
    idxc = [fw.sb(f"idxc{i}", [128, 4], I32) for i in range(2)]
    gc = [fw.sb(f"gc{i}", [128, 4], F32) for i in range(2)]
    outg = [fw.sb(f"outg{i}", [128, 4 * D], F32) for i in range(2)]
    pb_ = [fw.sb(f"pb{i}", [128, 8], F32) for i in range(2)]
    prow = fw.sb("prow", [128, 48], F32)
    vit = fw.sb("vit", [128, 16], F32)
    wpow = fw.sb("wpow", [128, 8], F32)
    ecap = fw.sb("ecap", [128, 8], F32)
    pcol = fw.sb("pcol", [128, 1], F32)
    bcol = fw.sb("bcol", [128, 1], F32)
    virow = fw.sb("virow", [1, 16], F32)
    brow = fw.sb("brow", [1, 1], F32)
    onesf = fw.sb("onesf", [1, 128], F32)
    onescf = fw.sb("onescf", [128, 1], BF16)
    ltri = fw.sb("ltri", [128, 128], BF16)
    jt = fw.sb("jt", [128, 128], F32)
    totrow = fw.sb("totrow", [1, 8], F32)
    pickb = fw.sb("pickb", [128, 8], BF16)
    zloc = fw.sb("zloc", [128, 8], F32)
    znov = fw.sb("znov", [128, 8], F32)
    slotc = fw.sb("slotc", [128, 1], F32)
    sloti = fw.sb("sloti", [128, 1], I32)
    tokid = fw.sb("tokid", [128, 1], I32)
    gsel = fw.sb("gsel", [128, 1], F32)
    fillc = fw.sb("fillc", [128, 66], I32)
    identb = fw.sb("identb", [128, 128], BF16)
    ps_trb = [fw.st.enter_context(nc.psum_tensor(f"ps_trb{i}", [128, 128], BF16))
              for i in range(2)]
    ps_1 = [fw.psbank(f"ps_1{i}") for i in range(2)]
    ps_2 = [fw.psbank(f"ps_2{i}") for i in range(2)]
    ps_m = fw.psbank("ps_m")
    ps_c = fw.psbank("ps_c")

    fw.init_consts()
    STAGE = 9

    SY.dma_start(out=virow[:, :], in_=bass.AP(vi_d, 0, [[16, 1], [1, 16]]))
    SY.dma_start(out=brow[:, :], in_=base_d[:, :])
    G.iota(pcol[:, :], [[0, 1]], channel_multiplier=1,
           allow_small_or_imprecise_dtypes=True)
    G.iota(wpow[:, :], [[1, 8]], channel_multiplier=0,
           allow_small_or_imprecise_dtypes=True)
    G.iota(jt[:, :], [[1, 128]], channel_multiplier=0,
           allow_small_or_imprecise_dtypes=True)
    G.iota(fillc[:, :], [[0, 66]], base=TC, channel_multiplier=1)
    V.wait(SY)
    V.wait(G)
    V.memset(onesf[:, :], 1.0)
    V.memset(onescf[:, :], 1.0)
    V.memset(totrow[:, :], 0.0)

    V.tensor_copy(identb[:, :], fw.ident[:, :])
    V.tensor_scalar(jt[:, :], jt[:, :], pcol[:, :], None, ALU.is_ge)
    V.tensor_copy(ltri[:, :], jt[:, :])
    V.tensor_scalar(ecap[:, :], wpow[:, :], float(CAP), None, ALU.mult)
    V.tensor_scalar(wpow[:, :], wpow[:, :], -0.6931471805599453, None, ALU.mult)
    S.wait(V)
    S.activation(wpow[:, :], wpow[:, :], AF.Exp)
    T.wait(V)
    T.wait(S)
    T.matmul(ps_m[:, 0:8], onesf[:, :], virow[0:1, 0:8], start=True, stop=True, inc=True)
    T.matmul(ps_m[:, 8:16], onesf[:, :], virow[0:1, 8:16], start=True, stop=True, inc=True)
    T.matmul(ps_m[:, 16:17], onesf[:, :], brow[0:1, :], start=True, stop=True, inc=True)
    V.wait(T)
    V.tensor_copy(vit[:, 0:8], ps_m[:, 0:8])
    V.tensor_copy(vit[:, 8:16], ps_m[:, 8:16])
    V.tensor_copy(bcol[:, :], ps_m[:, 16:17])

    zpad = fw.sb("zpad", [128, D], BF16)
    V.memset(zpad[:, :], 0.0)
    G.wait(V)
    G.dma_start(out=x4a_d[0:TC, :], in_=x3_d[:, :])
    G.dma_start(out=bass.AP(il_d, 0, [[1, NLIST]]),
                in_=bass.AP(fillc, 0, [[66, 128], [1, 66]]))
    G.dma_start(out=xnp_d[0:TC, :], in_=xn3_d[:, :])
    G.dma_start(out=xnp_d[TC:TC + 128, :], in_=zpad[:, :])
    gdt_init = G.dtoken()

    ld0 = SY.dtoken()
    SY.dma_start(out=pb_[0][:, :], in_=pr_d[0:128, :])
    tok_free = [0, 0]
    gdt_sc = 0
    for c in range(NCHUNK):
        i = c % 2
        V.wait_dtok(SY, ld0 + 16 * (c + 1))
        V.wait_dtok(G, gdt_sc)
        if c + 1 < NCHUNK:
            j = (c + 1) % 2
            SY.wait_tok(V, tok_free[j])
            SY.dma_start(out=pb_[j][:, :], in_=pr_d[(c + 1) * 128:(c + 2) * 128, :])
        p = pb_[i]
        V.tensor_scalar(prow[:, 40:41], pcol[:, :], bcol[:, :], float(c * 128),
                        ALU.add, ALU.add)
        V.tensor_tensor(prow[:, 0:8], p[:, :], vit[:, 0:8], ALU.is_gt)
        V.tensor_tensor(prow[:, 8:16], p[:, :], vit[:, 0:8], ALU.is_equal)
        V.tensor_scalar(prow[:, 16:24], vit[:, 8:16], prow[:, 40:41], None, ALU.is_ge)
        V.tensor_tensor(prow[:, 8:16], prow[:, 8:16], prow[:, 16:24], ALU.mult)
        V.tensor_tensor(prow[:, 0:8], prow[:, 0:8], prow[:, 8:16], ALU.add)
        V.reduce_max(prow[:, 41:42], prow[:, 0:8], axis=AX.X)
        V.tensor_tensor(prow[:, 8:16], p[:, :], prow[:, 0:8], ALU.mult)
        V.tensor_tensor(prow[:, 8:16], prow[:, 8:16], prow[:, 0:8], ALU.add)
        V.tensor_scalar(prow[:, 8:16], prow[:, 8:16], 1.0, None, ALU.subtract)
        V.tensor_scalar(prow[:, 8:16], prow[:, 8:16], prow[:, 41:42], None, ALU.mult)
        V.tensor_scalar(prow[:, 42:43], prow[:, 41:42], -1.0, 1.0, ALU.mult, ALU.add)
        V.tensor_scalar(prow[:, 16:24], p[:, :], prow[:, 42:43], None, ALU.mult)
        V.tensor_tensor(prow[:, 8:16], prow[:, 8:16], prow[:, 16:24], ALU.add)
        V.reduce_max(prow[:, 43:44], prow[:, 8:16], axis=AX.X)
        V.tensor_scalar(prow[:, 16:24], prow[:, 8:16], prow[:, 43:44], None, ALU.is_ge)
        V.tensor_tensor(prow[:, 16:24], prow[:, 16:24], wpow[:, :], ALU.mult)
        V.reduce_max(prow[:, 44:45], prow[:, 16:24], axis=AX.X)
        V.tensor_scalar(prow[:, 16:24], prow[:, 16:24], prow[:, 44:45], None, ALU.is_ge)
        V.tensor_tensor(prow[:, 24:32], p[:, :], prow[:, 16:24], ALU.mult)
        V.reduce_sum(gsel[:, :], prow[:, 24:32], axis=AX.X)
        V.tensor_copy(pickb[:, :], prow[:, 16:24])
        tok_free[i] = V.token()
        T.wait(V)
        T.wait_tok(V, 0)
        T.matmul(ps_c[:, 0:8], ltri[:, :], pickb[:, :], start=True, stop=True, inc=True)
        T.matmul(ps_c[0:1, 8:16], onescf[:, :], pickb[:, :], start=True, stop=True, inc=True)
        V.wait(T)
        V.tensor_scalar(zloc[:, :], ps_c[:, 0:8], -1.0, None, ALU.add)
        T.wait(V)
        T.matmul(ps_m[:, 24:32], onesf[:, :], totrow[0:1, :], start=True, stop=True, inc=True)
        V.wait(T)
        V.tensor_tensor(zloc[:, :], zloc[:, :], ps_m[:, 24:32], ALU.add)
        V.tensor_tensor(totrow[0:1, :], totrow[0:1, :], ps_c[0:1, 8:16], ALU.add)
        V.tensor_scalar(znov[:, :], zloc[:, :], float(CAP), None, ALU.is_lt)
        V.tensor_tensor(zloc[:, :], zloc[:, :], ecap[:, :], ALU.add)
        V.tensor_tensor(zloc[:, :], zloc[:, :], znov[:, :], ALU.mult)
        V.tensor_scalar(znov[:, :], znov[:, :], -float(8 * CAP + 32), float(8 * CAP + 32),
                        ALU.mult, ALU.add)
        V.tensor_tensor(zloc[:, :], zloc[:, :], znov[:, :], ALU.add)
        V.tensor_tensor(zloc[:, :], zloc[:, :], prow[:, 16:24], ALU.mult)
        V.reduce_sum(slotc[:, :], zloc[:, :], axis=AX.X)
        V.tensor_copy(sloti[:, :], slotc[:, :])
        V.tensor_scalar(tokid[:, :], pcol[:, :], float(c * 128), None, ALU.add)
        if STAGE >= 2:
            G.wait(V)
            G.wait_dtok(G, gdt_init)
            G.indirect_dma_start(il_d[:, :], bass.IndirectOffsetOnAxis(sloti[:, :], 0),
                                 tokid[:, :], None)
            G.indirect_dma_start(gl_d[:, :], bass.IndirectOffsetOnAxis(sloti[:, :], 0),
                                 gsel[:, :], None)
            gdt_sc = G.dtoken()

    if STAGE < 3:
        G.barrier_all()
        S.barrier_all()
        V.barrier_all()
        fw.build()
        return nc
    SY.wait_dtok(G, gdt_sc)
    vtok_tr = [0, 0]
    stok_s1 = [0, 0]
    vtok_s2 = [0, 0]
    ttok_w = 0
    ttok_xg = [0, 0]
    gdt_out = [0, 0]
    dma_in = [0, 0]
    wpat1 = lambda e: bass.AP(w1_d, e * D * F, [[F, 128], [128 * F, 8], [1, F]])
    wpat2 = lambda e: bass.AP(w2_d, e * F * D, [[D, 128], [128 * D, 16], [1, D]])
    for e in range(8):
        G.wait_tok(T, ttok_w)
        G.dma_start(out=w1[:, :], in_=wpat1(e))
        G.dma_start(out=w2[:, :], in_=wpat2(e))
        wld = G.dtoken()
        T.wait_dtok(G, wld)
        for stile in range(2):
            ib = stile % 2 if e % 2 == 0 else (stile + 1) % 2
            base = e * CAP + stile * 512
            SY.wait_tok(T, ttok_xg[ib])
            SY.wait_dtok(G, gdt_out[ib])
            for g in range(4):
                SY.dma_start(out=idxc[ib][:, g:g + 1],
                             in_=il_d[base + g * 128: base + (g + 1) * 128, :])
                SY.dma_start(out=gc[ib][:, g:g + 1],
                             in_=gl_d[base + g * 128: base + (g + 1) * 128, :])
            idt = SY.dtoken()
            G.wait_dtok(SY, idt)
            G.wait_tok(T, ttok_xg[ib])
            for g in range(4):
                G.indirect_dma_start(xg[ib][:, g * D:(g + 1) * D], None,
                                     xnp_d[:, :],
                                     bass.IndirectOffsetOnAxis(idxc[ib][:, g:g + 1], 0))
            gdt_g = G.dtoken()
            if STAGE < 4:
                continue
            T.wait_dtok(G, gdt_g)
            for g in range(4):
                for kd in range(8):
                    pbk = ps_trb[kd % 2]
                    T.wait_tok(V, vtok_tr[kd % 2])
                    T.transpose(pbk[:, :], xg[ib][:, g * D + kd * 128: g * D + (kd + 1) * 128],
                                identb[:, :], inc=True)
                    V.wait(T)
                    V.tensor_copy(xgT[ib][:, kd * 512 + g * 128: kd * 512 + (g + 1) * 128],
                                  pbk[:, :])
                    vtok_tr[kd % 2] = V.token()
            ttok_xg[ib] = T.token()
            T.wait(V)
            for fc in range(16):
                pbk = ps_1[fc % 2]
                T.wait_tok(S, stok_s1[fc % 2])
                for kd in range(8):
                    T.matmul(pbk[:, :], w1[:, kd * F + fc * 128: kd * F + fc * 128 + 128],
                             xgT[ib][:, kd * 512:(kd + 1) * 512],
                             start=(kd == 0), stop=(kd == 7), inc=(kd == 7))
                S.wait(T)
                S.activation(mid[:, fc * 512:(fc + 1) * 512], pbk[:, :], AF.Gelu)
                stok_s1[fc % 2] = S.token()
            T.wait(S)
            V.wait_dtok(G, gdt_out[ib])
            for g in range(4):
                for half in range(2):
                    k2i = (g * 2 + half) % 2
                    pbk = ps_2[k2i]
                    T.wait_tok(V, vtok_s2[k2i])
                    for fc in range(16):
                        T.matmul(pbk[:, :], mid[:, fc * 512 + g * 128: fc * 512 + g * 128 + 128],
                                 w2[:, fc * D + half * 512: fc * D + half * 512 + 512],
                                 start=(fc == 0), stop=(fc == 15), inc=(fc == 15))
                    V.wait(T)
                    V.tensor_scalar(outg[ib][:, g * D + half * 512: g * D + half * 512 + 512],
                                    pbk[:, :], gc[ib][:, g:g + 1], None, ALU.mult)
                    vtok_s2[k2i] = V.token()
            ttok_w = T.token()
            if STAGE >= 5:
                G.wait(V)
                for g in range(4):
                    G.indirect_dma_start(x4a_d[:, :],
                                         bass.IndirectOffsetOnAxis(idxc[ib][:, g:g + 1], 0),
                                         outg[ib][:, g * D:(g + 1) * D], None,
                                         oob_is_err=True,
                                         compute_op=ALU.add)
                gdt_out[ib] = G.dtoken()

    G.dma_start(out=x4_d[:, :], in_=x4a_d[0:TC, :])
    G.barrier_all()
    S.barrier_all()
    fw.build()
    return nc


def run_l1_ffn(inp, res7, res8):
    vitab = np.empty((2, 8), np.float32)
    for m in range(NCORES):
        vitab[0, m] = res8.results[m]["route"][0, 0]
        vitab[1, m] = res8.results[m]["route"][0, 1]
    nc = build_l1_ffn()
    maps = []
    for c in range(NCORES):
        maps.append(dict(
            x3=res7.results[c]["x3"],
            xn3=res7.results[c]["xn3"],
            probs=res7.results[c]["probs"],
            vitab=vitab,
            idxbase=np.full((1, 1), float(c * TC), np.float32),
            w1=np.ascontiguousarray(np.asarray(inp["L1_ffn_W1"])),
            w2=np.ascontiguousarray(np.asarray(inp["L1_ffn_W2"])),
        ))
    return run_bass_kernel_spmd(nc, maps, core_ids=list(range(NCORES)))


def _shard_tokens(x):
    out = []
    for b in range(4):
        for h in range(2):
            out.append(np.ascontiguousarray(x[b, h * 4096:(h + 1) * 4096, :]))
    return out


def run_l0_qkv(inp, shards):
    nc1 = build_l0_qkv()
    wqT = np.ascontiguousarray(np.asarray(inp["L0_attn_Wq"]).T)
    wkT = np.ascontiguousarray(np.asarray(inp["L0_attn_Wk"]).T)
    wvT = np.ascontiguousarray(np.asarray(inp["L0_attn_Wv"]).T)
    wbT = np.ascontiguousarray(np.asarray(inp["L0_attn_beta_w"]).T)
    in_maps = [dict(
        x=shards[c], wqT=wqT, wkT=wkT, wvT=wvT, wbT=wbT,
        rin=np.ascontiguousarray(np.asarray(inp["L0_attn_rin_w"]).reshape(1, D)),
        rk=np.ascontiguousarray(np.asarray(inp["L0_attn_rk_w"]).reshape(1, D)),
        rv=np.ascontiguousarray(np.asarray(inp["L0_attn_rv_w"]).reshape(1, D)),
        beta_b=np.ascontiguousarray(np.asarray(inp["L0_attn_beta_b"]).reshape(1, 16)),
        bbl=np.ascontiguousarray(np.asarray(inp["L0_attn_base_beta_logit"]).reshape(1, 16)),
    ) for c in range(NCORES)]
    return run_bass_kernel_spmd(nc1, in_maps, core_ids=list(range(NCORES)))


def kernel(**inp):
    x = np.asarray(inp["L0_attn_x"])
    shards = _shard_tokens(x)
    res1 = run_l0_qkv(inp, shards)
    res2 = run_l0_mixer(res1)
    o_tok = host_regroup_o(res2)
    res4 = run_l0_post_ffn(inp, shards, o_tok)
    x2_list = [res4.results[c]["x2"] for c in range(NCORES)]
    res5 = run_l1_abc(inp, x2_list)
    res6 = run_l1_scan(res5)
    ys_list = host_regroup_y(res6)
    res7 = run_l1_post(inp, x2_list, res5, ys_list)
    res8 = run_l1_route(res7)
    res9 = run_l1_ffn(inp, res7, res8)
    out = np.empty((4, 2 * TC, D), np.float32)
    for c in range(NCORES):
        out[c // 2, (c % 2) * TC:(c % 2 + 1) * TC, :] = res9.results[c]["x4"]
    return out
